# version 09
# baseline (speedup 1.0000x reference)
import numpy as np
from contextlib import ExitStack

import concourse.bass as bass
import concourse.mybir as mybir
from concourse.bass_utils import run_bass_kernel_spmd

F32 = mybir.dt.float32
BF16 = mybir.dt.bfloat16
AF = mybir.ActivationFunctionType
ALU = mybir.AluOpType
AX = mybir.AxisListType

KB = 1024
D = 1024
DC = 8
TALL = 4096
TOWN = 2048
NQB = 4
NKC = 32
TH = 1024

O_ONES = 0
O_EYE = 256
O_ROUT = 512
O_EPS5 = 640
O_EPS35 = 648
O_WQ = 1 * KB
O_XLN = 49 * KB
O_SPILL = [177 * KB + 512, 181 * KB + 512]
O_KSQ = 113 * KB
O_XS = [121 * KB, 137 * KB]
O_U = 153 * KB
O_ASTATS = 161 * KB
O_KMAX = 177 * KB
O_QSQ = 185 * KB + 512
O_LNQ = 189 * KB + 512
O_MREP = 177 * KB + 512
O_PT = 1 * KB
O_KST = [33 * KB, 41 * KB]
O_QST = [49 * KB, 57 * KB]
O_XOWN = 65 * KB
O_WB = 81 * KB
O_UW = 89 * KB
O_STATS = 97 * KB
O_NEGM = 107 * KB
O_VBF = 113 * KB
O_X2SL = 1 * KB
O_X2LN = 33 * KB
O_ACC = 49 * KB
O_WG = [186 * KB, 81 * KB]
O_WU = [89 * KB, 97 * KB]
O_FM = 105 * KB
O_FMS = 107 * KB
O_FR = 109 * KB
O_FT2 = 111 * KB
O_FT = 113 * KB
O_COEF = 114 * KB
O_SEL = 204 * KB
O_CREP = 118 * KB
O_H = 122 * KB
O_WD = 154 * KB
O_UWT = 194 * KB
ARENA = 206 * KB


class C:

    def __init__(self, stack, nc, name):
        self.h = stack.enter_context(nc.semaphore(name))
        self.n = 0
        self.name = name


def build(stage="full"):
    nc = bass.Bass("TRN2", target_bir_lowering=False)

    xb = nc.dram_tensor("xb", [D, TALL], F32, kind="ExternalInput").ap()
    wqkvT = nc.dram_tensor("wqkvT", [D, 3 * D], F32, kind="ExternalInput").ap()
    router = nc.dram_tensor("router", [D, 8], F32, kind="ExternalInput").ap()
    eg = nc.dram_tensor("eg", [8, D, 512], F32, kind="ExternalInput").ap()
    eu = nc.dram_tensor("eu", [8, D, 512], F32, kind="ExternalInput").ap()
    ed = nc.dram_tensor("ed", [8, 512, D], F32, kind="ExternalInput").ap()
    sg = nc.dram_tensor("sg", [D, D], F32, kind="ExternalInput").ap()
    su = nc.dram_tensor("su", [D, D], F32, kind="ExternalInput").ap()
    sd = nc.dram_tensor("sd", [D, D], F32, kind="ExternalInput").ap()
    eye_in = nc.dram_tensor("eye", [128, 128], F32, kind="ExternalInput").ap()
    sel_in = nc.dram_tensor("sel8", [32, 1024], F32, kind="ExternalInput").ap()

    k_hbm = nc.dram_tensor("k_hbm", [DC, 128, TALL], BF16, kind="Internal").ap()
    q_hbm = nc.dram_tensor("q_hbm", [DC, 128, TOWN], BF16, kind="Internal").ap()
    x2_hbm = nc.dram_tensor("x2_hbm", [DC, 128, TOWN], F32, kind="Internal").ap()
    mr_hbm = nc.dram_tensor("mr_hbm", [128, 4096], F32, kind="Internal").ap()

    y = nc.dram_tensor("y", [D, TOWN], F32, kind="ExternalOutput").ap()
    y_r = y.rearrange("(c p) t -> p c t", p=128)

    dbg = {}
    if stage == "ln1":
        dbg["xln"] = nc.dram_tensor("dbg_xln", [D, TALL], F32, kind="ExternalOutput").ap()
    if stage == "qkv":
        dbg["k"] = nc.dram_tensor("dbg_k", [DC, 128, TALL], F32, kind="ExternalOutput").ap()
        dbg["q"] = nc.dram_tensor("dbg_q", [DC, 128, TOWN], F32, kind="ExternalOutput").ap()
        dbg["v"] = nc.dram_tensor("dbg_v", [TALL, D], F32, kind="ExternalOutput").ap()
        dbg["negm"] = nc.dram_tensor("dbg_negm", [128, TOWN], F32, kind="ExternalOutput").ap()
    if stage == "attn":
        dbg["x2"] = nc.dram_tensor("dbg_x2", [DC, 128, TOWN], F32, kind="ExternalOutput").ap()
    if stage == "ffn0":
        dbg["x2ln"] = nc.dram_tensor("dbg_x2ln", [128, DC * TH], F32, kind="ExternalOutput").ap()
        dbg["coef"] = nc.dram_tensor("dbg_coef", [128, 1024], F32, kind="ExternalOutput").ap()
        dbg["acc"] = nc.dram_tensor("dbg_acc", [128, DC * TH], F32, kind="ExternalOutput").ap()
        dbg["h0"] = nc.dram_tensor("dbg_h0", [128, 4 * TH], F32, kind="ExternalOutput").ap()
        dbg["crep0"] = nc.dram_tensor("dbg_crep0", [128, TH], F32, kind="ExternalOutput").ap()
    if stage.startswith("qb"):
        for nm, shp in [("pt", [128, NKC * 512]), ("wb", [128, DC * 512]),
                        ("eps", [128, 512]), ("sm", [128, 512]), ("sms", [128, 512]),
                        ("sr", [128, 512]), ("xo", [128, DC * 512])]:
            dbg[nm] = nc.dram_tensor("dbg_" + nm, shp, F32, kind="ExternalOutput").ap()

    with ExitStack() as stack:
        arena = stack.enter_context(nc.sbuf_tensor("arena", [128, ARENA], mybir.dt.uint8))

        def carve(off, dtype, shape):
            esz = 2 if dtype == BF16 else 4
            nfree = 1
            for s in shape[1:]:
                nfree *= s
            v = arena[0:shape[0], off:off + nfree * esz].bitcast(dtype)
            if len(shape) == 2:
                return v
            return v.rearrange("p (a b) -> p a b", a=shape[1], b=shape[2])

        ones_bf = carve(O_ONES, BF16, [128, 128])
        eye_bf = carve(O_EYE, BF16, [128, 128])
        rout_b = carve(O_ROUT, BF16, [128, 8, 8])
        eps5 = carve(O_EPS5, F32, [128, 1])
        eps35 = carve(O_EPS35, F32, [128, 1])
        wq = carve(O_WQ, BF16, [128, DC, 3 * D])
        xln = carve(O_XLN, BF16, [128, DC, TALL])
        spill = [carve(O_SPILL[i], BF16, [128, 4, 512]) for i in range(2)]
        spill_f = [carve(O_SPILL[i], BF16, [128, 2048]) for i in range(2)]
        ksq = carve(O_KSQ, BF16, [128, 2048])
        xs = [carve(O_XS[i], F32, [128, DC, 512]) for i in range(2)]
        ubuf = carve(O_U, BF16, [128, DC, 512])
        a_m = carve(O_ASTATS, F32, [128, 512])
        a_ms = carve(O_ASTATS + 2 * KB, F32, [128, 512])
        a_t = carve(O_ASTATS + 4 * KB, F32, [128, 512])
        a_r = carve(O_ASTATS + 6 * KB, F32, [128, 512])
        a_rb = carve(O_ASTATS + 8 * KB, BF16, [128, 512])
        a_bb = carve(O_ASTATS + 9 * KB, BF16, [128, 512])
        kmax = carve(O_KMAX, F32, [128, 1])
        klog = carve(O_KMAX + 128, F32, [128, 1])
        qsq = carve(O_QSQ, BF16, [128, 2048])
        lnq = carve(O_LNQ, F32, [128, 2048])
        mrep = carve(O_MREP, F32, [128, 2048])
        negm = carve(O_NEGM, BF16, [128, 2048])
        mr3 = [carve(177 * KB + 512 + 4 * KB * i, F32, [128, 2, 512]) for i in range(4)]
        mr3all = carve(177 * KB + 512, F32, [128, 4096])
        mrh = [carve(200 * KB + 2 * KB * i, BF16, [128, 2, 512]) for i in range(2)]
        pt = carve(O_PT, BF16, [128, NKC, 512])
        kst = [carve(O_KST[i], BF16, [128, DC, 512]) for i in range(2)]
        qst = [carve(O_QST[i], BF16, [128, DC, 512]) for i in range(2)]
        xown = carve(O_XOWN, F32, [128, DC, 512])
        w_b = carve(O_WB, BF16, [128, DC, 512])
        uw = carve(O_UW, BF16, [128, DC, 512])
        s_m = carve(O_STATS, F32, [128, 512])
        s_ms = carve(O_STATS + 2 * KB, F32, [128, 512])
        s_eps = carve(O_STATS + 4 * KB, F32, [128, 512])
        s_r = carve(O_STATS + 6 * KB, F32, [128, 512])
        s_mb = carve(O_STATS + 8 * KB, BF16, [128, 512])
        s_rb = carve(O_STATS + 9 * KB, BF16, [128, 512])
        vbf = carve(O_VBF, BF16, [128, NKC, 1024])
        x2sl = carve(O_X2SL, F32, [128, DC, TH])
        x2ln = carve(O_X2LN, BF16, [128, DC, TH])
        acc = carve(O_ACC, F32, [128, DC, TH])
        wgb = [carve(O_WG[i], BF16, [128, DC, 512]) for i in range(2)]
        wub = [carve(O_WU[i], BF16, [128, DC, 512]) for i in range(2)]
        f_m = carve(O_FM, F32, [128, 512])
        f_ms = carve(O_FMS, F32, [128, 512])
        f_r = carve(O_FR, F32, [128, 512])
        f_t2 = carve(O_FT2, F32, [128, 512])
        f_t = carve(O_FT, F32, [128, 16])
        coef_tma = carve(O_FT + 64, BF16, [128, 64])
        lmax = carve(O_FT + 192, F32, [128, 1])
        lsum = carve(O_FT + 224, F32, [128, 1])
        lrec = carve(O_FT + 256, F32, [128, 1])
        coef_fm = carve(O_COEF, BF16, [128, 1024])
        sel_b = carve(O_SEL, BF16, [32, 8, 128])
        crep = [carve(O_CREP + 2 * KB * i, BF16, [128, 1024]) for i in range(2)]
        crep += [carve(196 * KB + 2 * KB * i, BF16, [128, 1024]) for i in range(2)]
        hb = [carve(O_H + 8 * KB * i, BF16, [128, 4, 1024]) for i in range(4)]
        wdb = [carve(O_WD + 8 * KB * i, BF16, [128, 4, 1024]) for i in range(4)]
        sqb = carve(O_H, BF16, [128, DC, 512])
        castb = carve(O_H + 8 * KB, BF16, [128, DC, 512])
        uwtmp = carve(O_UWT, BF16, [128, 2, 512])

        PS = [stack.enter_context(nc.psum_tensor("ps%d" % i, [128, 512], F32)) for i in range(8)]

        sPE = C(stack, nc, "sPE")
        sDV = C(stack, nc, "sDV")
        sAC = C(stack, nc, "sAC")
        sPL = C(stack, nc, "sPL")
        s_wdma = C(stack, nc, "s_wdma")
        s_xs = [C(stack, nc, "s_xs0"), C(stack, nc, "s_xs1")]
        s_ksp = C(stack, nc, "s_ksp")
        s_qsp = C(stack, nc, "s_qsp")
        s_kst = [C(stack, nc, "s_kst0"), C(stack, nc, "s_kst1")]
        s_qst = [C(stack, nc, "s_qst0"), C(stack, nc, "s_qst1")]
        s_xown = C(stack, nc, "s_xown")
        s_x2w = C(stack, nc, "s_x2w")
        s_x2r = C(stack, nc, "s_x2r")
        s_wg = [C(stack, nc, "s_wg0"), C(stack, nc, "s_wg1")]
        s_wu = [C(stack, nc, "s_wu0"), C(stack, nc, "s_wu1")]
        s_wd = [C(stack, nc, "s_wd%d" % i) for i in range(4)]
        s_y = C(stack, nc, "s_y")
        s_mr = C(stack, nc, "s_mr")
        s_dbg = C(stack, nc, "s_dbg")

        block = stack.enter_context(nc.Block())

        def on(name, f):
            getattr(block, name)(f)

        def _(g):
            g.memset(ones_bf, 1.0)
            g.memset(eps5, 1e-5)
            g.memset(eps35, 1e-35)
            g.engine_nop().then_inc(sPL.h, 1)
            g.dma_start(out=eye_bf, in_=eye_in).then_inc(s_wdma.h, 16)
            g.dma_start(out=rout_b, in_=router.rearrange("(c p) e -> p c e", p=128)).then_inc(s_wdma.h, 16)
            g.dma_start(out=sel_b, in_=sel_in.rearrange("p (a b) -> p a b", a=8, b=128)).then_inc(s_wdma.h, 16)
            for c in range(DC):
                g.dma_start(out=wq[:, c, :], in_=wqkvT[c * 128:(c + 1) * 128, :]).then_inc(s_wdma.h, 16)
        on("gpsimd", _)
        sPL.n += 1
        s_wdma.n += 16 * (3 + DC)
        WQ_READY = s_wdma.n
        ONES_READY = sPL.n

        _sid = nc.enter_named_scope("phA", False)[0]
        xb_r = xb.rearrange("(c p) t -> p c t", p=128)
        apply_done = [0] * 8
        prev_stats = [0]

        for s in range(8):
            p = s % 2

            def _(sy, s=s, p=p):
                if s >= 2:
                    sy.wait_ge(sDV.h, apply_done[s - 2])
                sy.dma_start(out=xs[p], in_=xb_r[:, :, s * 512:(s + 1) * 512]).then_inc(s_xs[p].h, 16)
            on("sync", _)
            s_xs[p].n += 16

            def _(v, s=s, p=p, w=s_xs[p].n):
                v.wait_ge(s_xs[p].h, w)
                if s == 0:
                    v.wait_ge(sPL.h, ONES_READY)
                xls = xln[:, :, s * 512:(s + 1) * 512]
                v.tensor_copy(xls, xs[p]).then_inc(sDV.h, 1)
            on("vector", _)
            sDV.n += 1
            cast_done = sDV.n

            def _(sc, s=s, cast_done=cast_done, pstats=prev_stats[0]):
                sc.wait_ge(sDV.h, cast_done)
                if pstats:
                    sc.wait_ge(sPE.h, pstats)
                xls = xln[:, :, s * 512:(s + 1) * 512]
                sc.square(ubuf[:], xls).then_inc(sAC.h, 1)
            on("scalar", _)
            sAC.n += 1
            sq_done = sAC.n

            def _(t, s=s, cast_done=cast_done, sq_done=sq_done):
                t.wait_ge(sDV.h, cast_done)
                xls = xln[:, :, s * 512:(s + 1) * 512]
                for dc in range(DC):
                    t.matmul(PS[0][:], ones_bf, xls[:, dc, :], start=(dc == 0), stop=(dc == DC - 1))
                t.wait_ge(sAC.h, sq_done)
                for dc in range(DC):
                    mm = t.matmul(PS[1][:], ones_bf, ubuf[:, dc, :], start=(dc == 0), stop=(dc == DC - 1))
                mm.then_inc(sPE.h, 1)
            on("tensor", _)
            sPE.n += 1
            stats_done = sPE.n
            prev_stats[0] = stats_done

            def _(v, stats_done=stats_done):
                v.wait_ge(sPE.h, stats_done)
                v.tensor_scalar(a_m[:], PS[0][:], 1.0 / D, None, ALU.mult)
                v.tensor_scalar(a_ms[:], PS[1][:], 1.0 / D, None, ALU.mult)
                v.tensor_tensor(a_t[:], a_m[:], a_m[:], ALU.mult)
                v.tensor_tensor(a_ms[:], a_ms[:], a_t[:], ALU.subtract).then_inc(sDV.h, 1)
            on("vector", _)
            sDV.n += 1
            var_done = sDV.n

            def _(sc, var_done=var_done):
                sc.wait_ge(sDV.h, var_done)
                if var_done <= 3:
                    sc.wait_ge(sPL.h, 1)
                sc.activation(a_t[:], a_ms[:], AF.Ln, bias=eps5[:])
                sc.activation(a_r[:], a_t[:], AF.Exp, scale=-0.5).then_inc(sAC.h, 1)
            on("scalar", _)
            sAC.n += 1
            rstd_done = sAC.n

            def _(v, s=s, rstd_done=rstd_done):
                v.wait_ge(sAC.h, rstd_done)
                v.tensor_copy(a_rb[:], a_r[:])
                v.tensor_tensor(a_t[:], a_m[:], a_r[:], ALU.mult)
                v.tensor_scalar(a_bb[:], a_t[:], -1.0, None, ALU.mult)
                xls = xln[:, :, s * 512:(s + 1) * 512]
                for dc in range(DC):
                    v.tensor_tensor(xls[:, dc, :], xls[:, dc, :], a_rb[:], ALU.mult)
                    mm = v.tensor_tensor(xls[:, dc, :], xls[:, dc, :], a_bb[:], ALU.add)
                mm.then_inc(sDV.h, 1)
            on("vector", _)
            sDV.n += 1
            apply_done[s] = sDV.n

        nc.leave_named_scope("phA", _sid, False)
        LN1_DONE = apply_done[7]

        if stage == "ln1":
            def _(g):
                g.wait_ge(sDV.h, LN1_DONE)
                g.dma_start(out=dbg["xln"].rearrange("(c p) t -> p c t", p=128),
                            in_=xln[:]).then_inc(s_dbg.h, 16)
            on("gpsimd", _)
            s_dbg.n += 16

            def _(sy):
                sy.wait_ge(s_dbg.h, s_dbg.n)
            on("sync", _)
            return nc

        _sid = nc.enter_named_scope("phK", False)[0]
        kss_free_dv = 0
        kq_evac_pair = [0, 0]
        pending_kss = [None]
        for kg in range(2):
            for fc in range(DC):
                b = fc % 2
                for half in range(2):
                    def _(t, kg=kg, fc=fc, half=half, ge=kq_evac_pair[half]):
                        if kg == 0 and fc == 0 and half == 0:
                            t.wait_ge(s_wdma.h, WQ_READY)
                            t.wait_ge(sDV.h, LN1_DONE)
                        if ge:
                            t.wait_ge(sAC.h, ge)
                        for dc in range(DC):
                            for k2 in range(2):
                                ks = half * 2 + k2
                                mm = t.matmul(PS[ks][:], wq[:, dc, D + fc * 128:D + (fc + 1) * 128],
                                              xln[:, dc, kg * 2048 + ks * 512:kg * 2048 + (ks + 1) * 512],
                                              start=(dc == 0), stop=(dc == DC - 1))
                        mm.then_inc(sPE.h, 1)
                    on("tensor", _)
                    sPE.n += 1
                    prod_done = sPE.n
                    if half == 0 and pending_kss[0] is not None:
                        pending_kss[0]()
                        pending_kss[0] = None

                    def _(sc, b=b, half=half, prod_done=prod_done,
                          ws=(s_ksp.n if b == 0 else s_qsp.n)):
                        sc.wait_ge(sPE.h, prod_done)
                        if half == 0:
                            sc.wait_ge((s_ksp if b == 0 else s_qsp).h, ws)
                        for k2 in range(2):
                            ks = half * 2 + k2
                            cp = sc.copy(spill[b][:, ks, :], PS[ks][:])
                        cp.then_inc(sAC.h, 1)
                    on("scalar", _)
                    sAC.n += 1
                    kq_evac_pair[half] = sAC.n
                evac_done = sAC.n

                def _(sy, kg=kg, fc=fc, b=b, evac_done=evac_done):
                    sy.wait_ge(sAC.h, evac_done)
                    sy.wait_ge(sDV.h, sDV.n)
                    sy.dma_start(out=k_hbm[fc, :, kg * 2048:(kg + 1) * 2048],
                                 in_=spill_f[b]).then_inc((s_ksp if b == 0 else s_qsp).h, 16)
                on("sync", _)
                if b == 0:
                    s_ksp.n += 16
                else:
                    s_qsp.n += 16

                def _(v, b=b, evac_done=evac_done):
                    v.wait_ge(sAC.h, evac_done)
                    v.wait_ge(sPE.h, sPE.n)
                    v.tensor_tensor(ksq[:], spill_f[b], spill_f[b], ALU.mult).then_inc(sDV.h, 1)
                on("vector", _)
                sDV.n += 1
                ksq_done = sDV.n

                def emit_kss(fc=fc, ksq_done=ksq_done, kss_free_dv=kss_free_dv):
                    def _(t):
                        t.wait_ge(sDV.h, ksq_done)
                        if fc == 0:
                            t.wait_ge(sDV.h, kss_free_dv)
                        for ks in range(4):
                            mm = t.matmul(PS[4 + ks][:], ones_bf, ksq[:, ks * 512:(ks + 1) * 512],
                                          start=(fc == 0), stop=(fc == DC - 1))
                        mm.then_inc(sPE.h, 1)
                    on("tensor", _)
                    sPE.n += 1
                pending_kss[0] = emit_kss

            pending_kss[0]()
            pending_kss[0] = None
            kss_mm_done = sPE.n

            def _(v, kg=kg, kss_mm_done=kss_mm_done, base=sDV.n):
                v.wait_ge(sPE.h, kss_mm_done)
                n = base
                for ks in range(4):
                    v.reduce_max(a_m[:, 0:1], PS[4 + ks][:], axis=AX.X).then_inc(sDV.h, 1)
                    n += 1
                    v.wait_ge(sDV.h, n)
                    if kg == 0 and ks == 0:
                        op = v.tensor_copy(kmax[:], a_m[:, 0:1])
                    else:
                        op = v.tensor_tensor(kmax[:], kmax[:], a_m[:, 0:1], ALU.max)
                    op.then_inc(sDV.h, 1)
                    n += 1
                    v.wait_ge(sDV.h, n)
            on("vector", _)
            sDV.n += 8
            kss_free_dv = sDV.n

        nc.leave_named_scope("phK", _sid, False)
        _sid = nc.enter_named_scope("phV", False)[0]
        v_evac = [0, 0]
        for tt in range(NKC):
            bk = (tt % 2) * 2
            def _(t, tt=tt, bk=bk, ge=v_evac[tt % 2]):
                if ge:
                    t.wait_ge(sAC.h, ge)
                if tt < 2:
                    t.wait_ge(sAC.h, sAC.n)
                    t.wait_ge(sDV.h, sDV.n)
                for dc in range(DC):
                    for fs in range(2):
                        mm = t.matmul(PS[bk + fs][:], xln[:, dc, tt * 128:(tt + 1) * 128],
                                      wq[:, dc, 2 * D + fs * 512:2 * D + (fs + 1) * 512],
                                      start=(dc == 0), stop=(dc == DC - 1))
                mm.then_inc(sPE.h, 1)
            on("tensor", _)
            sPE.n += 1
            vprod = sPE.n

            def _(sc, tt=tt, bk=bk, vprod=vprod, wk=s_ksp.n):
                sc.wait_ge(sPE.h, vprod)
                if tt < 2:
                    sc.wait_ge(s_ksp.h, wk)
                for fs in range(2):
                    cp = sc.copy(vbf[:, tt, fs * 512:(fs + 1) * 512], PS[bk + fs][:])
                cp.then_inc(sAC.h, 1)
            on("scalar", _)
            sAC.n += 1
            v_evac[tt % 2] = sAC.n

        nc.leave_named_scope("phV", _sid, False)
        V_DONE_AC = sAC.n

        _sid = nc.enter_named_scope("phQ", False)[0]
        for fc in range(DC):
            b = fc % 2
            for half in range(2):
                def _(t, fc=fc, half=half, ge=kq_evac_pair[half]):
                    if fc == 0 and half == 0:
                        t.wait_ge(sAC.h, sAC.n)
                    if ge:
                        t.wait_ge(sAC.h, ge)
                    if half == 0 and fc > 0:
                        t.wait_ge(sDV.h, sDV.n)
                    for dc in range(DC):
                        for k2 in range(2):
                            qs = half * 2 + k2
                            mm = t.matmul(PS[qs][:], wq[:, dc, fc * 128:(fc + 1) * 128],
                                          xln[:, dc, qs * 512:(qs + 1) * 512],
                                          start=(dc == 0), stop=(dc == DC - 1))
                    mm.then_inc(sPE.h, 1)
                on("tensor", _)
                sPE.n += 1
                prod_done = sPE.n
                if half == 0 and pending_kss[0] is not None:
                    pending_kss[0]()
                    pending_kss[0] = None

                def _(sc, b=b, half=half, prod_done=prod_done,
                      ws=(s_ksp.n if b == 0 else s_qsp.n)):
                    sc.wait_ge(sPE.h, prod_done)
                    if half == 0:
                        sc.wait_ge((s_ksp if b == 0 else s_qsp).h, ws)
                    for k2 in range(2):
                        qs = half * 2 + k2
                        cp = sc.activation(spill[b][:, qs, :], PS[qs][:], AF.Copy, scale=0.03125)
                    cp.then_inc(sAC.h, 1)
                on("scalar", _)
                sAC.n += 1
                kq_evac_pair[half] = sAC.n
            evac_done = sAC.n

            def _(sy, fc=fc, b=b, evac_done=evac_done):
                sy.wait_ge(sAC.h, evac_done)
                sy.wait_ge(sDV.h, sDV.n)
                sy.dma_start(out=q_hbm[fc, :, :], in_=spill_f[b]).then_inc((s_ksp if b == 0 else s_qsp).h, 16)
            on("sync", _)
            if b == 0:
                s_ksp.n += 16
            else:
                s_qsp.n += 16

            def _(v, b=b, evac_done=evac_done):
                v.wait_ge(sAC.h, evac_done)
                v.wait_ge(sPE.h, sPE.n)
                v.tensor_tensor(qsq[:], spill_f[b], spill_f[b], ALU.mult).then_inc(sDV.h, 1)
            on("vector", _)
            sDV.n += 1
            qsq_done = sDV.n

            def emit_qss(fc=fc, qsq_done=qsq_done):
                def _(t):
                    t.wait_ge(sDV.h, qsq_done)
                    for qs in range(4):
                        mm = t.matmul(PS[4 + qs][:], ones_bf, qsq[:, qs * 512:(qs + 1) * 512],
                                      start=(fc == 0), stop=(fc == DC - 1))
                    mm.then_inc(sPE.h, 1)
                on("tensor", _)
                sPE.n += 1
            pending_kss[0] = emit_qss

        pending_kss[0]()
        pending_kss[0] = None
        Q_DONE_PE = sPE.n

        nc.leave_named_scope("phQ", _sid, False)
        _sid = nc.enter_named_scope("phNegm", False)[0]
        def _(sc, w0=s_ksp.n, w1=s_qsp.n):
            sc.wait_ge(sPE.h, Q_DONE_PE)
            sc.wait_ge(s_ksp.h, w0)
            sc.wait_ge(s_qsp.h, w1)
            for qs in range(4):
                sc.activation(lnq[:, qs * 512:(qs + 1) * 512], PS[4 + qs][:], AF.Ln)
            sc.activation(klog[:], kmax[:], AF.Ln).then_inc(sAC.h, 1)
        on("scalar", _)
        sAC.n += 1
        lnq_done = sAC.n

        def _(v, lnq_done=lnq_done):
            v.wait_ge(sAC.h, lnq_done)
            v.tensor_scalar(klog[:], klog[:], 0.5, None, ALU.mult)
            v.engine_nop().then_inc(sDV.h, 1)
        on("vector", _)
        sDV.n += 1
        klog_done = sDV.n

        def _(sc, klog_done=klog_done):
            sc.wait_ge(sDV.h, klog_done)
            sc.activation(mrep[:], lnq[:], AF.Exp, scale=0.5, bias=klog[:]).then_inc(sAC.h, 1)
        on("scalar", _)
        sAC.n += 1
        m_done = sAC.n

        def _(v, m_done=m_done):
            v.wait_ge(sAC.h, m_done)
            v.tensor_scalar(negm[:], mrep[:], -1.0, 19.0, ALU.mult, ALU.add)
            v.engine_nop().then_inc(sDV.h, 1)
        on("vector", _)
        sDV.n += 1
        NEGM_DONE = sDV.n

        nc.leave_named_scope("phNegm", _sid, False)
        if stage == "qkv":
            def _(g, wk=s_ksp.n, wqq=s_qsp.n):
                g.wait_ge(sDV.h, NEGM_DONE)
                g.wait_ge(s_ksp.h, wk)
                g.wait_ge(s_qsp.h, wqq)
                g.dma_start(out=dbg["k"], in_=k_hbm).then_inc(s_dbg.h, 16)
                g.dma_start(out=dbg["q"], in_=q_hbm).then_inc(s_dbg.h, 16)
                g.dma_start(out=dbg["v"].rearrange("(a p) f -> p a f", p=128), in_=vbf[:]).then_inc(s_dbg.h, 16)
                g.dma_start(out=dbg["negm"], in_=negm[:]).then_inc(s_dbg.h, 16)
            on("gpsimd", _)
            s_dbg.n += 64

            def _(sy):
                sy.wait_ge(s_dbg.h, s_dbg.n)
            on("sync", _)
            return nc

        K_SPILLED = s_ksp.n
        Q_SPILLED = s_qsp.n
        kst_pe_free = [0, 0]
        qst_pe_free = [0, 0]
        pt_pe_free = 0
        wb_dv_free = 0
        x2b_cast_prev = [0]
        bank_evac = [0, 0, 0]
        ST_BANKS = [0, 1, 7]
        pv_bank_evac = [0, 0]

        for qb in range(NQB):
            _sqb = nc.enter_named_scope("qb%d" % qb, False)[0]
            def _(sy, qb=qb, w=qst_pe_free[qb % 2]):
                if qb == 0:
                    sy.wait_ge(sDV.h, NEGM_DONE)
                    sy.wait_ge(s_ksp.h, K_SPILLED)
                    sy.wait_ge(s_qsp.h, Q_SPILLED)
                sy.wait_ge(sPE.h, w)
                sy.dma_start(out=qst[qb % 2],
                             in_=q_hbm[:, :, qb * 512:(qb + 1) * 512].rearrange("c p t -> p c t")
                             ).then_inc(s_qst[qb % 2].h, 16)
            on("sync", _)
            s_qst[qb % 2].n += 16
            qst_ready = s_qst[qb % 2].n

            for kh in range(8):
                def _(sy, kh=kh, w=kst_pe_free[kh % 2]):
                    sy.wait_ge(sPE.h, w)
                    sy.dma_start(out=kst[kh % 2],
                                 in_=k_hbm[:, :, kh * 512:(kh + 1) * 512].rearrange("c p t -> p c t")
                                 ).then_inc(s_kst[kh % 2].h, 16)
                on("sync", _)
                s_kst[kh % 2].n += 16
                kr = s_kst[kh % 2].n

                for kk in range(4):
                    kc = kh * 4 + kk
                    bank_guard = bank_evac[kc % 3]

                    def _(t, qb=qb, kh=kh, kk=kk, kc=kc, qst_ready=qst_ready, kr=kr,
                          bank_guard=bank_guard, pt_guard=pt_pe_free):
                        if kk == 0:
                            t.wait_ge(s_kst[kh % 2].h, kr)
                            if kh == 0:
                                t.wait_ge(s_qst[qb % 2].h, qst_ready)
                        if bank_guard:
                            t.wait_ge(sAC.h, bank_guard)
                        bk = PS[ST_BANKS[kc % 3]]
                        for dc in range(DC):
                            t.matmul(bk[:], kst[kh % 2][:, dc, kk * 128:(kk + 1) * 128],
                                     qst[qb % 2][:, dc, :], start=(dc == 0), stop=False)
                        mm = t.matmul(bk[:], ones_bf[0:1, :],
                                      negm[0:1, qb * 512:(qb + 1) * 512], start=False, stop=True)
                        mm.then_inc(sPE.h, 1)
                    on("tensor", _)
                    sPE.n += 1
                    st_done = sPE.n

                    def _(sc, kc=kc, st_done=st_done, pt_guard=pt_pe_free):
                        sc.wait_ge(sPE.h, st_done)
                        if kc == 0 and pt_guard:
                            sc.wait_ge(sPE.h, pt_guard)
                        sc.activation(pt[:, kc, :], PS[ST_BANKS[kc % 3]][:], AF.Exp).then_inc(sAC.h, 1)
                    on("scalar", _)
                    sAC.n += 1
                    bank_evac[kc % 3] = sAC.n
                kst_pe_free[kh % 2] = sPE.n
            qst_pe_free[qb % 2] = sPE.n
            EXP_DONE = sAC.n

            def _(t, EXP_DONE=EXP_DONE):
                t.wait_ge(sAC.h, EXP_DONE)
                t.wait_ge(sDV.h, sDV.n)
                for kc in range(NKC):
                    mm = t.matmul(PS[2][:], ones_bf, pt[:, kc, :], start=(kc == 0), stop=(kc == NKC - 1))
                mm.then_inc(sPE.h, 1)
            on("tensor", _)
            sPE.n += 1
            sum_done = sPE.n

            def _(v, sum_done=sum_done, wb_dv_free=wb_dv_free):
                v.wait_ge(sPE.h, sum_done)
                v.wait_ge(sDV.h, wb_dv_free)
                v.tensor_scalar(s_r[:], PS[2][:], 1.0, None, ALU.mult).then_inc(sDV.h, 1)
            on("vector", _)
            sDV.n += 1
            sum_ev = sDV.n

            def _(v, sum_ev=sum_ev):
                v.wait_ge(sDV.h, sum_ev)
                v.reciprocal(s_eps[:], s_r[:])
                v.engine_nop().then_inc(sDV.h, 1)
            on("vector", _)
            sDV.n += 1
            eps_done = sDV.n

            def _(sy, qb=qb, w=s_x2w.n, wc=x2b_cast_prev[0]):
                sy.wait_ge(s_x2w.h, w)
                if wc:
                    sy.wait_ge(sDV.h, wc)
                sy.dma_start(out=xown, in_=xb_r[:, :, qb * 512:(qb + 1) * 512]).then_inc(s_xown.h, 16)
            on("sync", _)
            s_xown.n += 16
            xown_ready = s_xown.n

            sq_done_fs = [0] * DC
            for fs in range(DC):
                def _(t, fs=fs, eps_done=eps_done, bg=pv_bank_evac[fs % 2]):
                    if fs == 0:
                        t.wait_ge(sDV.h, eps_done)
                        t.wait_ge(sAC.h, V_DONE_AC)
                    if bg:
                        t.wait_ge(sDV.h, bg)
                    for kc in range(NKC):
                        mm = t.matmul(PS[3 + fs % 2][:], vbf[:, kc, fs * 128:(fs + 1) * 128],
                                      pt[:, kc, :], start=(kc == 0), stop=(kc == NKC - 1))
                    mm.then_inc(sPE.h, 1)
                on("tensor", _)
                sPE.n += 1
                pv_done = sPE.n

                def _(v, fs=fs, pv_done=pv_done):
                    v.wait_ge(sPE.h, pv_done)
                    v.tensor_tensor(w_b[:, fs, :], PS[3 + fs % 2][:], s_eps[:], ALU.mult)
                    v.tensor_tensor(uw[:, fs, :], w_b[:, fs, :], w_b[:, fs, :], ALU.mult).then_inc(sDV.h, 1)
                on("vector", _)
                sDV.n += 1
                pv_bank_evac[fs % 2] = sDV.n
                sq_done = sDV.n

                sq_done_fs[fs] = sq_done
                if fs > 0:
                    def _(t, fs=fs, sq_done=sq_done_fs[fs - 1]):
                        t.wait_ge(sDV.h, sq_done)
                        t.matmul(PS[5][:], ones_bf, w_b[:, fs - 1, :], start=(fs - 1 == 0), stop=False)
                        mm = t.matmul(PS[6][:], ones_bf, uw[:, fs - 1, :], start=(fs - 1 == 0), stop=False)
                        mm.then_inc(sPE.h, 1)
                    on("tensor", _)
                    sPE.n += 1

            def _(t, sq_done=sq_done_fs[DC - 1]):
                t.wait_ge(sDV.h, sq_done)
                t.matmul(PS[5][:], ones_bf, w_b[:, DC - 1, :], start=False, stop=True)
                mm = t.matmul(PS[6][:], ones_bf, uw[:, DC - 1, :], start=False, stop=True)
                mm.then_inc(sPE.h, 1)
            on("tensor", _)
            sPE.n += 1
            pt_pe_free = sPE.n
            ln2_mm_done = sPE.n

            def _(v, ln2_mm_done=ln2_mm_done):
                v.wait_ge(sPE.h, ln2_mm_done)
                v.tensor_scalar(s_m[:], PS[5][:], 1.0 / D, None, ALU.mult)
                v.tensor_scalar(s_ms[:], PS[6][:], 1.0 / D, None, ALU.mult)
                v.tensor_tensor(s_r[:], s_m[:], s_m[:], ALU.mult)
                v.tensor_tensor(s_ms[:], s_ms[:], s_r[:], ALU.subtract)
                v.engine_nop().then_inc(sDV.h, 1)
            on("vector", _)
            sDV.n += 1
            var2_done = sDV.n

            def _(sc, var2_done=var2_done):
                sc.wait_ge(sDV.h, var2_done)
                sc.activation(s_ms[:], s_ms[:], AF.Ln, bias=eps5[:])
                sc.activation(s_r[:], s_ms[:], AF.Exp, scale=-0.5).then_inc(sAC.h, 1)
            on("scalar", _)
            sAC.n += 1
            rstd2_done = sAC.n

            def _(v, rstd2_done=rstd2_done, xown_ready=xown_ready):
                v.wait_ge(sAC.h, rstd2_done)
                v.wait_ge(s_xown.h, xown_ready)
                v.tensor_copy(s_mb[:], s_m[:])
                v.tensor_copy(s_rb[:], s_r[:])
                for fs in range(DC):
                    v.tensor_tensor(uw[:, fs, :], w_b[:, fs, :], s_mb[:], ALU.subtract)
                    v.tensor_tensor(uw[:, fs, :], uw[:, fs, :], s_rb[:], ALU.mult)
                    mm = v.tensor_tensor(xown[:, fs, :], xown[:, fs, :], uw[:, fs, :], ALU.add)
                mm.then_inc(sDV.h, 1)
            on("vector", _)
            sDV.n += 1
            wb_dv_free = sDV.n
            x2_done = sDV.n

            def _(v, x2_done=x2_done):
                v.wait_ge(sDV.h, x2_done)
                v.tensor_copy(uw[:], xown[:]).then_inc(sDV.h, 1)
            on("vector", _)
            sDV.n += 1
            x2b_done = sDV.n
            x2b_cast_prev[0] = x2b_done

            def _(sc, x2b_done=x2b_done):
                sc.wait_ge(sDV.h, x2b_done)
                sc.square(w_b[:], uw[:]).then_inc(sAC.h, 1)
            on("scalar", _)
            sAC.n += 1
            sq3q_done = sAC.n

            def _(t, x2b_done=x2b_done, sq3q_done=sq3q_done):
                t.wait_ge(sDV.h, x2b_done)
                for dc in range(DC):
                    t.matmul(PS[5][:], ones_bf, uw[:, dc, :], start=(dc == 0), stop=(dc == DC - 1))
                t.wait_ge(sAC.h, sq3q_done)
                for dc in range(DC):
                    mm = t.matmul(PS[6][:], ones_bf, w_b[:, dc, :], start=(dc == 0), stop=(dc == DC - 1))
                mm.then_inc(sPE.h, 1)
            on("tensor", _)
            sPE.n += 1
            st3q_done = sPE.n

            def _(v, qb=qb, st3q_done=st3q_done):
                v.wait_ge(sPE.h, st3q_done)
                v.tensor_scalar(mr3[qb][:, 0, :], PS[5][:], 1.0 / D, None, ALU.mult)
                v.tensor_scalar(s_ms[:], PS[6][:], 1.0 / D, None, ALU.mult)
                v.tensor_tensor(s_r[:], mr3[qb][:, 0, :], mr3[qb][:, 0, :], ALU.mult)
                v.tensor_tensor(s_ms[:], s_ms[:], s_r[:], ALU.subtract).then_inc(sDV.h, 1)
            on("vector", _)
            sDV.n += 1
            var3q = sDV.n

            def _(sc, qb=qb, var3q=var3q):
                sc.wait_ge(sDV.h, var3q)
                sc.activation(s_ms[:], s_ms[:], AF.Ln, bias=eps5[:])
                sc.activation(mr3[qb][:, 1, :], s_ms[:], AF.Exp, scale=-0.5).then_inc(sAC.h, 1)
            on("scalar", _)
            sAC.n += 1
            wb_dv_free = sDV.n
            MR3_AC = sAC.n

            if qb in (1, 3):
                def _(g, qb=qb, w=MR3_AC):
                    g.wait_ge(sAC.h, w)
                    g.dma_start(out=mr_hbm[:, (qb - 1) * 1024:(qb + 1) * 1024],
                                in_=mr3all[:, (qb - 1) * 1024:(qb + 1) * 1024]).then_inc(s_mr.h, 16)
                on("gpsimd", _)
                s_mr.n += 16

            def _(g, qb=qb, x2_done=x2_done):
                g.wait_ge(sDV.h, x2_done)
                g.dma_start(out=x2_hbm[:, :, qb * 512:(qb + 1) * 512].rearrange("c p t -> p c t"),
                             in_=xown[:]).then_inc(s_x2w.h, 16)
            on("gpsimd", _)
            s_x2w.n += 16
            nc.leave_named_scope("qb%d" % qb, _sqb, False)

            if stage.startswith("qb") and qb == int(stage[2:]):
                def _(g, w=s_x2w.n):
                    g.wait_ge(s_x2w.h, w)
                    g.dma_start(out=dbg["pt"], in_=pt.rearrange("p a b -> p (a b)")).then_inc(s_dbg.h, 16)
                    g.dma_start(out=dbg["wb"], in_=w_b.rearrange("p a b -> p (a b)")).then_inc(s_dbg.h, 16)
                    g.dma_start(out=dbg["eps"], in_=s_eps).then_inc(s_dbg.h, 16)
                    g.dma_start(out=dbg["sm"], in_=s_m).then_inc(s_dbg.h, 16)
                    g.dma_start(out=dbg["sms"], in_=s_ms).then_inc(s_dbg.h, 16)
                    g.dma_start(out=dbg["sr"], in_=s_r).then_inc(s_dbg.h, 16)
                    g.dma_start(out=dbg["xo"], in_=xown.rearrange("p a b -> p (a b)")).then_inc(s_dbg.h, 16)
                on("gpsimd", _)
                s_dbg.n += 16 * 7

                def _(sy, w=s_dbg.n):
                    sy.wait_ge(s_dbg.h, w)
                on("sync", _)
                return nc


        PT_FREE_QB3 = pt_pe_free
        ATTN_PE = sPE.n
        ATTN_DV = sDV.n
        ATTN_AC3 = sAC.n
        X2_STORED = s_x2w.n

        if stage == "attn":
            def _(g):
                g.wait_ge(s_x2w.h, X2_STORED)
                g.dma_start(out=dbg["x2"], in_=x2_hbm).then_inc(s_dbg.h, 16)
            on("gpsimd", _)
            s_dbg.n += 16

            def _(sy):
                sy.wait_ge(s_dbg.h, s_dbg.n)
            on("sync", _)
            return nc

        def pe_srcs(j):
            if j < 8:
                return (eg[j].rearrange("(c p) d -> p c d", p=128),
                        eu[j].rearrange("(c p) d -> p c d", p=128),
                        ed[j].rearrange("(c p) hh -> p c hh", p=128), j)
            jj = j - 8
            return (sg[:, jj * 512:(jj + 1) * 512].rearrange("(c p) d -> p c d", p=128),
                    su[:, jj * 512:(jj + 1) * 512].rearrange("(c p) d -> p c d", p=128),
                    sd[jj * 512:(jj + 1) * 512, :].rearrange("(c p) hh -> p c hh", p=128), None)

        groups = [[8, 9, 0, 1], [2, 3, 4, 5], [6, 7]]

        wg_free = [0, 0]
        wu_free = [0, 0]
        wd_free = [0] * 4
        crep_free = [0, 0]
        gu_evac = [(0, 0), (0, 0)]
        down_evac = [(0, 0), (0, 0)]

        first_ffn_load = [True]
        for hf in range(2):
            _shf = nc.enter_named_scope("ffn_pre%d" % hf, False)[0]
            x2_ready_ts = [0, 0]
            for ts in range(2):
                def _(sy, hf=hf, ts=ts, wy=s_y.n):
                    if hf == 0:
                        if ts == 0:
                            sy.wait_ge(s_x2w.h, 32)
                            sy.wait_ge(sPE.h, PT_FREE_QB3)
                    else:
                        if ts == 0:
                            sy.wait_ge(s_x2w.h, X2_STORED)
                            sy.wait_ge(s_y.h, wy)
                            sy.wait_ge(sDV.h, sDV.n)
                    sy.dma_start(out=x2sl[:, :, ts * 512:(ts + 1) * 512],
                                 in_=x2_hbm[:, :, hf * TH + ts * 512:hf * TH + (ts + 1) * 512]
                                 .rearrange("c p t -> p c t")).then_inc(s_x2r.h, 16)
                on("sync", _)
                s_x2r.n += 16
                x2_ready_ts[ts] = s_x2r.n
            x2_ready = s_x2r.n

            cast3_ts = [0, 0]
            for ts in range(2):
                def _(v, ts=ts, w=x2_ready_ts[ts], hf=hf):
                    v.wait_ge(s_x2r.h, w)
                    if hf == 0 and ts == 0:
                        v.memset(coef_fm[0:32, :], 0.0)
                    v.tensor_copy(x2ln[:, :, ts * 512:(ts + 1) * 512],
                                  x2sl[:, :, ts * 512:(ts + 1) * 512]).then_inc(sDV.h, 1)
                on("vector", _)
                sDV.n += 1
                cast3_ts[ts] = sDV.n
            cast3_done = sDV.n

            def _(g, hf=hf, w=(16 if hf == 0 else s_mr.n), wu=sDV.n):
                g.wait_ge(s_mr.h, w)
                g.wait_ge(sDV.h, wu)
                for ts in range(2):
                    g.dma_start(out=mrh[ts],
                                in_=mr_hbm[:, (hf * 2 + ts) * 1024:(hf * 2 + ts + 1) * 1024]
                                .rearrange("p (a b) -> p a b", a=2, b=512)).then_inc(s_mr.h, 16)
            on("gpsimd", _)
            s_mr.n += 32
            mr_ready = s_mr.n

            apply_ts = [0, 0]
            for ts in range(2):
                def _(v, ts=ts, cast3_done=cast3_ts[ts], mr_ready=mr_ready):
                    v.wait_ge(sDV.h, cast3_done)
                    v.wait_ge(s_mr.h, mr_ready)
                    for dc in range(DC):
                        sl = x2ln[:, dc, ts * 512:(ts + 1) * 512]
                        v.tensor_tensor(sl, sl, mrh[ts][:, 0, :], ALU.subtract)
                        mm = v.tensor_tensor(sl, sl, mrh[ts][:, 1, :], ALU.mult)
                    mm.then_inc(sDV.h, 1)
                on("vector", _)
                sDV.n += 1
                apply_ts[ts] = sDV.n
            x2ln_apply = sDV.n

            rst = {"coef_done_dv": 0, "router_ac": 0, "logits_pe": 0}
            top2_done = [0] * 8
            pexp_done_tt = [0] * 8

            def emit_logits(apply_ts=apply_ts):
                for half4 in range(2):
                    def _(t, half4=half4, w=apply_ts[half4]):
                        t.wait_ge(sDV.h, w)
                        n = 0
                        for tt in range(half4 * 4, half4 * 4 + 4):
                            for dc in range(DC):
                                n += 1
                                mm = t.matmul(PS[6][:, tt * 8:(tt + 1) * 8],
                                              x2ln[:, dc, tt * 128:(tt + 1) * 128],
                                              rout_b[:, dc, :], start=(half4 == 0 and n == 1),
                                              stop=(half4 == 1 and n == 32),
                                              skip_group_check=True)
                        mm.then_inc(sPE.h, 1)
                    on("tensor", _)
                    sPE.n += 1
                rst["logits_pe"] = sPE.n

            def make_chain(tt):
                def chain():
                    def _(v, tt=tt, lp=rst["logits_pe"], base=sDV.n,
                          pe_prev=(pexp_done_tt[tt - 1] if tt else 0)):
                        v.wait_ge(sPE.h, lp)
                        if pe_prev:
                            v.wait_ge(sAC.h, pe_prev)
                        v.reduce_max(lmax[:], PS[6][:, tt * 8:(tt + 1) * 8], axis=AX.X).then_inc(sDV.h, 1)
                        v.wait_ge(sDV.h, base + 1)
                        v.tensor_scalar(lmax[:], lmax[:], -1.0, None, ALU.mult).then_inc(sDV.h, 1)
                    on("vector", _)
                    sDV.n += 2
                    nmax_done = sDV.n

                    def _(sc, tt=tt, nmax_done=nmax_done,
                          t2p=(top2_done[tt - 1] if tt else 0)):
                        sc.wait_ge(sDV.h, nmax_done)
                        if t2p:
                            sc.wait_ge(sDV.h, t2p)
                        sc.activation(f_t[:, 0:8], PS[6][:, tt * 8:(tt + 1) * 8],
                                      AF.Exp, bias=lmax[:]).then_inc(sAC.h, 1)
                    on("scalar", _)
                    sAC.n += 1
                    pexp_done_tt[tt] = sAC.n

                    def _(v, tt=tt, pexp=pexp_done_tt[tt], base=sDV.n):
                        v.wait_ge(sAC.h, pexp)
                        n = base
                        ops = [
                            lambda: v.reduce_sum(lsum[:], f_t[:, 0:8], axis=AX.X),
                            lambda: v.reciprocal(lrec[:], lsum[:]),
                            lambda: v.tensor_scalar(f_t[:, 0:8], f_t[:, 0:8], lrec[:], None, ALU.mult),
                            lambda: v.reduce_max(lmax[:], f_t[:, 0:8], axis=AX.X),
                            lambda: v.tensor_scalar(f_t[:, 8:16], f_t[:, 0:8], lmax[:], None, ALU.is_ge),
                            lambda: v.tensor_scalar(f_t[:, 8:16], f_t[:, 8:16], 2.0, None, ALU.mult),
                            lambda: v.tensor_tensor(f_t[:, 8:16], f_t[:, 0:8], f_t[:, 8:16], ALU.subtract),
                            lambda: v.reduce_max(lmax[:], f_t[:, 8:16], axis=AX.X),
                            lambda: v.tensor_scalar(f_t[:, 8:16], f_t[:, 0:8], lmax[:], None, ALU.is_ge),
                            lambda: v.tensor_tensor(coef_tma[:, tt * 8:(tt + 1) * 8],
                                                    f_t[:, 0:8], f_t[:, 8:16], ALU.mult),
                        ]
                        for op in ops:
                            op().then_inc(sDV.h, 1)
                            n += 1
                            v.wait_ge(sDV.h, n)
                    on("vector", _)
                    sDV.n += 10
                    top2_done[tt] = sDV.n
                return chain

            chain_queue = [make_chain(tt) for tt in range(8)]

            def emit_router_tail():
                while chain_queue:
                    chain_queue.pop(0)()

                def _(t, w=top2_done, pe7=pexp_done_tt):
                    t.wait_ge(sDV.h, max(w))
                    t.wait_ge(sAC.h, max(pe7))
                    for tt in range(8):
                        bank = PS[6] if tt < 4 else PS[7]
                        t.matmul(bank[0:8, (tt % 4) * 128:(tt % 4 + 1) * 128],
                                 coef_tma[:, tt * 8:(tt + 1) * 8], eye_bf,
                                 start=(tt % 4 == 0), stop=(tt % 4 == 3),
                                 skip_group_check=True).then_inc(sPE.h, 1)
                on("tensor", _)
                sPE.n += 8
                ctr_done = sPE.n

                def _(v, ctr_done=ctr_done):
                    v.wait_ge(sPE.h, ctr_done)
                    v.tensor_copy(coef_fm[0:8, 0:512], PS[6][0:8, 0:512])
                    mm = v.tensor_copy(coef_fm[0:8, 512:1024], PS[7][0:8, 0:512])
                    mm.then_inc(sDV.h, 1)
                on("vector", _)
                sDV.n += 1
                rst["coef_done_dv"] = sDV.n
            coef_done_dv = None

            for gi, grp in enumerate(groups):
                _sgg = nc.enter_named_scope("ffn_h%dg%d_gu" % (hf, gi), False)[0]
                grp_crep = [0]

                def emit_group_creps(routed):
                    if not routed:
                        return
                    for wave in range(0, len(routed), 2):
                        pair = routed[wave:wave + 2]

                        def _(t, pair=pair, cd=rst["coef_done_dv"]):
                            t.wait_ge(sDV.h, max(cd, sDV.n))
                            for i, ce in enumerate(pair):
                                for ts2 in range(2):
                                    mm = t.matmul(PS[4 + 2 * i + ts2][:], sel_b[:, ce, :],
                                                  coef_fm[0:32, ts2 * 512:(ts2 + 1) * 512],
                                                  start=True, stop=True)
                            mm.then_inc(sPE.h, 1)
                        on("tensor", _)
                        sPE.n += 1
                        bc_done = sPE.n

                        def _(v, pair=pair, bc_done=bc_done):
                            v.wait_ge(sPE.h, bc_done)
                            for i, ce in enumerate(pair):
                                for ts2 in range(2):
                                    mm = v.tensor_copy(crep[ce % 4][:, ts2 * 512:(ts2 + 1) * 512],
                                                       PS[4 + 2 * i + ts2][:])
                            mm.then_inc(sDV.h, 1)
                        on("vector", _)
                        sDV.n += 1
                    grp_crep[0] = sDV.n

                if gi > 0:
                    emit_group_creps([p for p in grp if p < 8])
                for j, pj in enumerate(grp):
                    if gi == 0 and j == 0:
                        emit_logits()
                    if gi == 0 and j == 2:
                        emit_router_tail()
                        emit_group_creps([p for p in grp if p < 8])
                    pre_router = (gi == 0 and j < 2)
                    wsrc, usrc, dsrc, ce = pe_srcs(pj)
                    slot = j % 2

                    def _(g, slot=slot, j=j, wsrc=wsrc, usrc=usrc, dsrc=dsrc,
                          wgf=wg_free[slot], wuf=wu_free[slot], wdf=wd_free[j],
                          first=first_ffn_load[0]):
                        if first:
                            g.wait_ge(sPE.h, ATTN_PE)
                            g.wait_ge(sDV.h, ATTN_DV)
                        g.wait_ge(sPE.h, wgf)
                        g.dma_start(out=wgb[slot], in_=wsrc).then_inc(s_wg[slot].h, 16)
                        g.wait_ge(sPE.h, wuf)
                        g.dma_start(out=wub[slot], in_=usrc).then_inc(s_wu[slot].h, 16)
                        g.wait_ge(sPE.h, wdf)
                        g.dma_start(out=wdb[j], in_=dsrc).then_inc(s_wd[j].h, 16)
                    on("gpsimd", _)
                    first_ffn_load[0] = False
                    s_wg[slot].n += 16
                    s_wu[slot].n += 16
                    s_wd[j].n += 16
                    wg_ready = s_wg[slot].n
                    wu_ready = s_wu[slot].n

                    cslot = (ce % 4) if ce is not None else None
                    crep_ready = grp_crep[0] if ce is not None else 0

                    for dffc in range(4):
                        par = (dffc % 2) if not pre_router else 0
                        bg = 0 if par == 0 else 4
                        eac, edv = gu_evac[par]

                        def _(t, slot=slot, dffc=dffc, bg=bg, wg_ready=wg_ready,
                              wu_ready=wu_ready, eac=eac, edv=edv, x2ln_apply=x2ln_apply,
                              crep_ready=crep_ready):
                            if dffc == 0:
                                t.wait_ge(s_wg[slot].h, wg_ready)
                                t.wait_ge(s_wu[slot].h, wu_ready)
                                t.wait_ge(sDV.h, x2ln_apply)
                            if dffc == 1 and crep_ready:
                                t.wait_ge(sDV.h, crep_ready)
                            if eac:
                                t.wait_ge(sAC.h, eac)
                            if edv:
                                t.wait_ge(sDV.h, edv)
                            for dc in range(DC):
                                for ts2 in range(2):
                                    t.matmul(PS[bg + ts2][:], wgb[slot][:, dc, dffc * 128:(dffc + 1) * 128],
                                             x2ln[:, dc, ts2 * 512:(ts2 + 1) * 512],
                                             start=(dc == 0), stop=(dc == DC - 1))
                            for dc in range(DC):
                                for ts2 in range(2):
                                    mm = t.matmul(PS[bg + 2 + ts2][:], wub[slot][:, dc, dffc * 128:(dffc + 1) * 128],
                                                  x2ln[:, dc, ts2 * 512:(ts2 + 1) * 512],
                                                  start=(dc == 0), stop=(dc == DC - 1))
                            mm.then_inc(sPE.h, 1)
                        on("tensor", _)
                        sPE.n += 1
                        gu_done = sPE.n

                        def _(sc, j=j, dffc=dffc, bg=bg, gu_done=gu_done):
                            sc.wait_ge(sPE.h, gu_done)
                            for ts2 in range(2):
                                cp = sc.activation(hb[j][:, dffc, ts2 * 512:(ts2 + 1) * 512],
                                                   PS[bg + ts2][:], AF.Silu)
                            cp.then_inc(sAC.h, 1)
                        on("scalar", _)
                        sAC.n += 1
                        silu_done = sAC.n

                        def _(v, j=j, dffc=dffc, bg=bg, silu_done=silu_done,
                              cslot=cslot, crep_ready=crep_ready):
                            v.wait_ge(sAC.h, silu_done)
                            if crep_ready:
                                v.wait_ge(sDV.h, crep_ready)
                            for ts2 in range(2):
                                hsl = hb[j][:, dffc, ts2 * 512:(ts2 + 1) * 512]
                                if cslot is not None:
                                    v.tensor_tensor(uwtmp[:, ts2, :], PS[bg + 2 + ts2][:],
                                                    crep[cslot][:, ts2 * 512:(ts2 + 1) * 512], ALU.mult)
                                else:
                                    v.tensor_copy(uwtmp[:, ts2, :], PS[bg + 2 + ts2][:])
                                mm = v.tensor_tensor(hsl, hsl, uwtmp[:, ts2, :], ALU.mult)
                            mm.then_inc(sDV.h, 1)
                        on("vector", _)
                        sDV.n += 1
                        gu_evac[par] = (silu_done, sDV.n)
                        if pre_router and chain_queue:
                            chain_queue.pop(0)()
                    wg_free[slot] = sPE.n
                    wu_free[slot] = sPE.n

                nc.leave_named_scope("ffn_h%dg%d_gu" % (hf, gi), _sgg, False)
                _sgd = nc.enter_named_scope("ffn_h%dg%d_dn" % (hf, gi), False)[0]
                H_READY_DV = sDV.n
                WD_READY = [s_wd[j].n for j in range(len(grp))]

                for hc in range(DC):
                    par = hc % 2
                    bk = 0 if par == 0 else 2
                    eac, edv = down_evac[par]

                    def _(t, gi=gi, grp=grp, hc=hc, bk=bk, H_READY_DV=H_READY_DV,
                          WD_READY=WD_READY, eac=eac, edv=edv):
                        t.wait_ge(sDV.h, H_READY_DV)
                        if hc == 0:
                            for j in range(len(grp)):
                                t.wait_ge(s_wd[j].h, WD_READY[j])
                        if eac:
                            t.wait_ge(sAC.h, eac)
                        if edv:
                            t.wait_ge(sDV.h, edv)
                        nmm = 4 * len(grp)
                        for ts2 in range(2):
                            n = 0
                            for j in range(len(grp)):
                                for dffc in range(4):
                                    n += 1
                                    mm = t.matmul(PS[bk + ts2][:], wdb[j][:, dffc, hc * 128:(hc + 1) * 128],
                                                  hb[j][:, dffc, ts2 * 512:(ts2 + 1) * 512],
                                                  start=(n == 1), stop=(n == nmm))
                        mm.then_inc(sPE.h, 1)
                    on("tensor", _)
                    sPE.n += 1
                    down_done = sPE.n

                    if gi == 0:
                        def _(sc, hc=hc, bk=bk, down_done=down_done):
                            sc.wait_ge(sPE.h, down_done)
                            for ts2 in range(2):
                                cp = sc.copy(acc[:, hc, ts2 * 512:(ts2 + 1) * 512], PS[bk + ts2][:])
                            cp.then_inc(sAC.h, 1)
                        on("scalar", _)
                        sAC.n += 1
                        down_evac[par] = (sAC.n, down_evac[par][1])
                    else:
                        def _(v, hc=hc, bk=bk, down_done=down_done):
                            v.wait_ge(sPE.h, down_done)
                            for ts2 in range(2):
                                asl = acc[:, hc, ts2 * 512:(ts2 + 1) * 512]
                                mm = v.tensor_tensor(asl, asl, PS[bk + ts2][:], ALU.add)
                            mm.then_inc(sDV.h, 1)
                        on("vector", _)
                        sDV.n += 1
                        down_evac[par] = (down_evac[par][0], sDV.n)
                for j in range(len(grp)):
                    wd_free[j] = sPE.n
                gu_evac[0] = (max(gu_evac[0][0], down_evac[0][0], down_evac[1][0]),
                              max(gu_evac[0][1], down_evac[0][1], down_evac[1][1]))
                nc.leave_named_scope("ffn_h%dg%d_dn" % (hf, gi), _sgd, False)
            DOWN_PE = sPE.n
            DOWN_DV = sDV.n
            DOWN_AC = sAC.n

            if stage == "ffn0":
                def _(g, a=DOWN_PE, b=DOWN_DV, c=DOWN_AC):
                    g.wait_ge(sPE.h, a)
                    g.wait_ge(sDV.h, b)
                    g.wait_ge(sAC.h, c)
                    g.dma_start(out=dbg["x2ln"], in_=x2ln.rearrange("p a b -> p (a b)")).then_inc(s_dbg.h, 16)
                    g.dma_start(out=dbg["coef"], in_=coef_fm).then_inc(s_dbg.h, 16)
                    g.dma_start(out=dbg["acc"], in_=acc.rearrange("p a b -> p (a b)")).then_inc(s_dbg.h, 16)
                    g.dma_start(out=dbg["h0"], in_=hb[0].rearrange("p a b -> p (a b)")).then_inc(s_dbg.h, 16)
                    g.dma_start(out=dbg["crep0"], in_=crep[0]).then_inc(s_dbg.h, 16)
                on("gpsimd", _)
                s_dbg.n += 16 * 5

                def _(sy, w=s_dbg.n):
                    sy.wait_ge(s_dbg.h, w)
                on("sync", _)
                return nc

            _sl4 = nc.enter_named_scope("ffn_h%d_ln4" % hf, False)[0]
            ln4_stats_pe = 0
            for ts in range(2):
                def _(v, ts=ts, DOWN_DV=DOWN_DV, DOWN_PE=DOWN_PE, DOWN_AC=DOWN_AC,
                      ln4_stats_pe=ln4_stats_pe):
                    v.wait_ge(sDV.h, DOWN_DV)
                    v.wait_ge(sAC.h, DOWN_AC)
                    v.wait_ge(sPE.h, DOWN_PE if ts == 0 else ln4_stats_pe)
                    for dc in range(DC):
                        v.tensor_copy(castb[:, dc, :], acc[:, dc, ts * 512:(ts + 1) * 512])
                        mm = v.tensor_tensor(sqb[:, dc, :], castb[:, dc, :], castb[:, dc, :], ALU.mult)
                    mm.then_inc(sDV.h, 1)
                on("vector", _)
                sDV.n += 1
                sq4_done = sDV.n

                def _(t, sq4_done=sq4_done):
                    t.wait_ge(sDV.h, sq4_done)
                    for dc in range(DC):
                        t.matmul(PS[6][:], ones_bf, castb[:, dc, :], start=(dc == 0), stop=(dc == DC - 1))
                    for dc in range(DC):
                        mm = t.matmul(PS[7][:], ones_bf, sqb[:, dc, :], start=(dc == 0), stop=(dc == DC - 1))
                    mm.then_inc(sPE.h, 1)
                on("tensor", _)
                sPE.n += 1
                ln4_stats_pe = sPE.n

                def _(v, done=ln4_stats_pe):
                    v.wait_ge(sPE.h, done)
                    v.tensor_scalar(f_m[:], PS[6][:], 1.0 / D, None, ALU.mult)
                    v.tensor_scalar(f_ms[:], PS[7][:], 1.0 / D, None, ALU.mult)
                    v.tensor_tensor(f_r[:], f_m[:], f_m[:], ALU.mult)
                    v.tensor_tensor(f_ms[:], f_ms[:], f_r[:], ALU.subtract).then_inc(sDV.h, 1)
                on("vector", _)
                sDV.n += 1
                var4 = sDV.n

                def _(sc, var4=var4):
                    sc.wait_ge(sDV.h, var4)
                    sc.activation(f_ms[:], f_ms[:], AF.Ln, bias=eps5[:])
                    sc.activation(f_r[:], f_ms[:], AF.Exp, scale=-0.5).then_inc(sAC.h, 1)
                on("scalar", _)
                sAC.n += 1
                rstd4 = sAC.n

                def _(v, ts=ts, rstd4=rstd4):
                    v.wait_ge(sAC.h, rstd4)
                    for dc in range(DC):
                        asl = acc[:, dc, ts * 512:(ts + 1) * 512]
                        xsl = x2sl[:, dc, ts * 512:(ts + 1) * 512]
                        v.tensor_tensor(f_t2[:], asl, f_m[:], ALU.subtract)
                        v.tensor_tensor(f_t2[:], f_t2[:], f_r[:], ALU.mult)
                        mm = v.tensor_tensor(xsl, xsl, f_t2[:], ALU.add)
                    mm.then_inc(sDV.h, 1)
                on("vector", _)
                sDV.n += 1
            OUT_DONE = sDV.n

            def _(sy, hf=hf, OUT_DONE=OUT_DONE):
                sy.wait_ge(sDV.h, OUT_DONE)
                sy.dma_start(out=y_r[:, :, hf * TH:(hf + 1) * TH], in_=x2sl).then_inc(s_y.h, 16)
            on("sync", _)
            s_y.n += 16
            nc.leave_named_scope("ffn_h%d_ln4" % hf, _sl4, False)

        def _(sy, w=s_y.n):
            sy.wait_ge(s_y.h, w)
        on("sync", _)

    return nc


_NC_CACHE = {}


def _get_nc(stage):
    if stage not in _NC_CACHE:
        _NC_CACHE[stage] = build(stage)
    return _NC_CACHE[stage]


def make_in_maps(L0_attn_x, L0_attn_W_qkv, L0_ffn_router, L0_ffn_experts_gate,
                 L0_ffn_experts_up, L0_ffn_experts_down, L0_ffn_shared_gate,
                 L0_ffn_shared_up, L0_ffn_shared_down):
    x = np.asarray(L0_attn_x, np.float32)
    sel8 = np.zeros((32, 1024), dtype=np.float32)
    for e in range(8):
        sel8[e, e * 128:(e + 1) * 128] = 1.0
    shared = dict(
        wqkvT=np.ascontiguousarray(np.asarray(L0_attn_W_qkv, np.float32).T),
        router=np.ascontiguousarray(np.asarray(L0_ffn_router, np.float32)),
        eg=np.ascontiguousarray(np.asarray(L0_ffn_experts_gate, np.float32)),
        eu=np.ascontiguousarray(np.asarray(L0_ffn_experts_up, np.float32)),
        ed=np.ascontiguousarray(np.asarray(L0_ffn_experts_down, np.float32)),
        sg=np.ascontiguousarray(np.asarray(L0_ffn_shared_gate, np.float32)),
        su=np.ascontiguousarray(np.asarray(L0_ffn_shared_up, np.float32)),
        sd=np.ascontiguousarray(np.asarray(L0_ffn_shared_down, np.float32)),
        eye=np.eye(128, dtype=np.float32),
        sel8=sel8,
    )
    in_maps = []
    for c in range(8):
        b, h = c // 2, c % 2
        xb = x[b]
        if h:
            xb = np.concatenate([xb[2048:], xb[:2048]], axis=0)
        m = dict(shared)
        m["xb"] = np.ascontiguousarray(xb.T)
        in_maps.append(m)
    return in_maps


def kernel(**inputs):
    in_maps = make_in_maps(**inputs)
    nc = _get_nc("full")
    res = run_bass_kernel_spmd(nc, in_maps, core_ids=list(range(8)))
    out = np.empty((4, 4096, 1024), np.float32)
    for c in range(8):
        b, h = c // 2, c % 2
        out[b, h * 2048:(h + 1) * 2048, :] = np.asarray(res.results[c]["y"]).T
    return out


# version 10
# speedup vs baseline: 1.0443x; 1.0443x over previous
import numpy as np
from contextlib import ExitStack

import concourse.bass as bass
import concourse.mybir as mybir
from concourse.bass_utils import run_bass_kernel_spmd

F32 = mybir.dt.float32
BF16 = mybir.dt.bfloat16
AF = mybir.ActivationFunctionType
ALU = mybir.AluOpType
AX = mybir.AxisListType

KB = 1024
D = 1024
DC = 8
TALL = 4096
TOWN = 2048
NQB = 4
NKC = 32
TH = 1024

O_ONES = 0
O_EYE = 256
O_ROUT = 512
O_EPS5 = 640
O_EPS35 = 648
O_WQ = 1 * KB
O_XLN = 49 * KB
O_SPILL = [177 * KB + 512, 181 * KB + 512]
O_KSQ = 113 * KB
O_XS = [121 * KB, 137 * KB]
O_U = 153 * KB
O_ASTATS = 161 * KB
O_KMAX = 177 * KB
O_QSQ = 185 * KB + 512
O_LNQ = 189 * KB + 512
O_MREP = 177 * KB + 512
O_PT = 1 * KB
O_KST = [33 * KB, 41 * KB]
O_QST = [49 * KB, 57 * KB]
O_XOWN = 65 * KB
O_WB = 81 * KB
O_UW = 89 * KB
O_STATS = 97 * KB
O_NEGM = 107 * KB
O_VBF = 113 * KB
O_X2SL = 1 * KB
O_X2LN = 33 * KB
O_ACC = 49 * KB
O_WG = [186 * KB, 81 * KB]
O_WU = [89 * KB, 97 * KB]
O_FM = 105 * KB
O_FMS = 107 * KB
O_FR = 109 * KB
O_FT2 = 111 * KB
O_FT = 113 * KB
O_COEF = 114 * KB
O_SEL = 204 * KB
O_CREP = 118 * KB
O_H = 122 * KB
O_WD = 154 * KB
O_UWT = 194 * KB
ARENA = 206 * KB


class C:

    def __init__(self, stack, nc, name):
        self.h = stack.enter_context(nc.semaphore(name))
        self.n = 0
        self.name = name


def build(stage="full"):
    nc = bass.Bass("TRN2", target_bir_lowering=False)

    xb = nc.dram_tensor("xb", [D, TALL], F32, kind="ExternalInput").ap()
    wqkvT = nc.dram_tensor("wqkvT", [D, 3 * D], F32, kind="ExternalInput").ap()
    router = nc.dram_tensor("router", [D, 8], F32, kind="ExternalInput").ap()
    eg = nc.dram_tensor("eg", [8, D, 512], F32, kind="ExternalInput").ap()
    eu = nc.dram_tensor("eu", [8, D, 512], F32, kind="ExternalInput").ap()
    ed = nc.dram_tensor("ed", [8, 512, D], F32, kind="ExternalInput").ap()
    sg = nc.dram_tensor("sg", [D, D], F32, kind="ExternalInput").ap()
    su = nc.dram_tensor("su", [D, D], F32, kind="ExternalInput").ap()
    sd = nc.dram_tensor("sd", [D, D], F32, kind="ExternalInput").ap()
    eye_in = nc.dram_tensor("eye", [128, 128], F32, kind="ExternalInput").ap()
    sel_in = nc.dram_tensor("sel8", [32, 1024], F32, kind="ExternalInput").ap()

    k_hbm = nc.dram_tensor("k_hbm", [DC, 128, TALL], BF16, kind="Internal").ap()
    q_hbm = nc.dram_tensor("q_hbm", [DC, 128, TOWN], BF16, kind="Internal").ap()
    x2_hbm = nc.dram_tensor("x2_hbm", [DC, 128, TOWN], F32, kind="Internal").ap()
    mr_hbm = nc.dram_tensor("mr_hbm", [128, 4096], F32, kind="Internal").ap()

    y = nc.dram_tensor("y", [D, TOWN], F32, kind="ExternalOutput").ap()
    y_r = y.rearrange("(c p) t -> p c t", p=128)

    dbg = {}
    if stage == "ln1":
        dbg["xln"] = nc.dram_tensor("dbg_xln", [D, TALL], F32, kind="ExternalOutput").ap()
    if stage == "qkv":
        dbg["k"] = nc.dram_tensor("dbg_k", [DC, 128, TALL], F32, kind="ExternalOutput").ap()
        dbg["q"] = nc.dram_tensor("dbg_q", [DC, 128, TOWN], F32, kind="ExternalOutput").ap()
        dbg["v"] = nc.dram_tensor("dbg_v", [TALL, D], F32, kind="ExternalOutput").ap()
        dbg["negm"] = nc.dram_tensor("dbg_negm", [128, TOWN], F32, kind="ExternalOutput").ap()
    if stage == "attn":
        dbg["x2"] = nc.dram_tensor("dbg_x2", [DC, 128, TOWN], F32, kind="ExternalOutput").ap()
    if stage == "ffn0":
        dbg["x2ln"] = nc.dram_tensor("dbg_x2ln", [128, DC * TH], F32, kind="ExternalOutput").ap()
        dbg["coef"] = nc.dram_tensor("dbg_coef", [128, 1024], F32, kind="ExternalOutput").ap()
        dbg["acc"] = nc.dram_tensor("dbg_acc", [128, DC * TH], F32, kind="ExternalOutput").ap()
        dbg["h0"] = nc.dram_tensor("dbg_h0", [128, 4 * TH], F32, kind="ExternalOutput").ap()
        dbg["crep0"] = nc.dram_tensor("dbg_crep0", [128, TH], F32, kind="ExternalOutput").ap()
    if stage.startswith("qb"):
        for nm, shp in [("pt", [128, NKC * 512]), ("wb", [128, DC * 512]),
                        ("eps", [128, 512]), ("sm", [128, 512]), ("sms", [128, 512]),
                        ("sr", [128, 512]), ("xo", [128, DC * 512])]:
            dbg[nm] = nc.dram_tensor("dbg_" + nm, shp, F32, kind="ExternalOutput").ap()

    with ExitStack() as stack:
        arena = stack.enter_context(nc.sbuf_tensor("arena", [128, ARENA], mybir.dt.uint8))

        def carve(off, dtype, shape):
            esz = 2 if dtype == BF16 else 4
            nfree = 1
            for s in shape[1:]:
                nfree *= s
            v = arena[0:shape[0], off:off + nfree * esz].bitcast(dtype)
            if len(shape) == 2:
                return v
            return v.rearrange("p (a b) -> p a b", a=shape[1], b=shape[2])

        ones_bf = carve(O_ONES, BF16, [128, 128])
        eye_bf = carve(O_EYE, BF16, [128, 128])
        rout_b = carve(O_ROUT, BF16, [128, 8, 8])
        eps5 = carve(O_EPS5, F32, [128, 1])
        eps35 = carve(O_EPS35, F32, [128, 1])
        wq = carve(O_WQ, BF16, [128, DC, 3 * D])
        xln = carve(O_XLN, BF16, [128, DC, TALL])
        spill = [carve(O_SPILL[i], BF16, [128, 4, 512]) for i in range(2)]
        spill_f = [carve(O_SPILL[i], BF16, [128, 2048]) for i in range(2)]
        ksq = carve(O_KSQ, BF16, [128, 2048])
        xs = [carve(O_XS[i], F32, [128, DC, 512]) for i in range(2)]
        ubuf = carve(O_U, BF16, [128, DC, 512])
        a_m = carve(O_ASTATS, F32, [128, 512])
        a_ms = carve(O_ASTATS + 2 * KB, F32, [128, 512])
        a_t = carve(O_ASTATS + 4 * KB, F32, [128, 512])
        a_r = carve(O_ASTATS + 6 * KB, F32, [128, 512])
        a_rb = carve(O_ASTATS + 8 * KB, BF16, [128, 512])
        a_bb = carve(O_ASTATS + 9 * KB, BF16, [128, 512])
        kmax = carve(O_KMAX, F32, [128, 1])
        klog = carve(O_KMAX + 128, F32, [128, 1])
        qsq = carve(O_QSQ, BF16, [128, 2048])
        lnq = carve(O_LNQ, F32, [128, 2048])
        mrep = carve(O_MREP, F32, [128, 2048])
        negm = carve(O_NEGM, BF16, [128, 2048])
        mr3 = [carve(177 * KB + 512 + 4 * KB * i, F32, [128, 2, 512]) for i in range(4)]
        mr3all = carve(177 * KB + 512, F32, [128, 4096])
        mrh = [carve(200 * KB + 2 * KB * i, BF16, [128, 2, 512]) for i in range(2)]
        pt = carve(O_PT, BF16, [128, NKC, 512])
        kst = [carve(O_KST[i], BF16, [128, DC, 512]) for i in range(2)]
        qst = [carve(O_QST[i], BF16, [128, DC, 512]) for i in range(2)]
        xown = carve(O_XOWN, F32, [128, DC, 512])
        w_b = carve(O_WB, BF16, [128, DC, 512])
        uw = carve(O_UW, BF16, [128, DC, 512])
        s_m = carve(O_STATS, F32, [128, 512])
        s_ms = carve(O_STATS + 2 * KB, F32, [128, 512])
        s_eps = carve(O_STATS + 4 * KB, F32, [128, 512])
        s_r = carve(O_STATS + 6 * KB, F32, [128, 512])
        s_mb = carve(O_STATS + 8 * KB, BF16, [128, 512])
        s_rb = carve(O_STATS + 9 * KB, BF16, [128, 512])
        vbf = carve(O_VBF, BF16, [128, NKC, 1024])
        x2sl = carve(O_X2SL, F32, [128, DC, TH])
        x2ln = carve(O_X2LN, BF16, [128, DC, TH])
        acc = carve(O_ACC, F32, [128, DC, TH])
        wgb = [carve(O_WG[i], BF16, [128, DC, 512]) for i in range(2)]
        wub = [carve(O_WU[i], BF16, [128, DC, 512]) for i in range(2)]
        f_m = carve(O_FM, F32, [128, 512])
        f_ms = carve(O_FMS, F32, [128, 512])
        f_r = carve(O_FR, F32, [128, 512])
        f_t2 = carve(O_FT2, F32, [128, 512])
        f_t = carve(O_FT, F32, [128, 16])
        coef_tma = carve(O_FT + 64, BF16, [128, 64])
        lmax = carve(O_FT + 192, F32, [128, 1])
        lsum = carve(O_FT + 224, F32, [128, 1])
        lrec = carve(O_FT + 256, F32, [128, 1])
        coef_fm = carve(O_COEF, BF16, [128, 1024])
        sel_b = carve(O_SEL, BF16, [32, 8, 128])
        crep = [carve(O_CREP + 2 * KB * i, BF16, [128, 1024]) for i in range(2)]
        crep += [carve(196 * KB + 2 * KB * i, BF16, [128, 1024]) for i in range(2)]
        hb = [carve(O_H + 8 * KB * i, BF16, [128, 4, 1024]) for i in range(4)]
        wdb = [carve(O_WD + 8 * KB * i, BF16, [128, 4, 1024]) for i in range(4)]
        sqb = carve(O_H, BF16, [128, DC, 512])
        castb = carve(O_H + 8 * KB, BF16, [128, DC, 512])
        uwtmp = carve(O_UWT, BF16, [128, 2, 512])

        PS = [stack.enter_context(nc.psum_tensor("ps%d" % i, [128, 512], F32)) for i in range(8)]

        sPE = C(stack, nc, "sPE")
        sDV = C(stack, nc, "sDV")
        sAC = C(stack, nc, "sAC")
        sPL = C(stack, nc, "sPL")
        s_wdma = C(stack, nc, "s_wdma")
        s_xs = [C(stack, nc, "s_xs0"), C(stack, nc, "s_xs1")]
        s_ksp = C(stack, nc, "s_ksp")
        s_qsp = C(stack, nc, "s_qsp")
        s_kst = [C(stack, nc, "s_kst0"), C(stack, nc, "s_kst1")]
        s_qst = [C(stack, nc, "s_qst0"), C(stack, nc, "s_qst1")]
        s_xown = C(stack, nc, "s_xown")
        s_x2w = C(stack, nc, "s_x2w")
        s_x2r = C(stack, nc, "s_x2r")
        s_wg = [C(stack, nc, "s_wg0"), C(stack, nc, "s_wg1")]
        s_wu = [C(stack, nc, "s_wu0"), C(stack, nc, "s_wu1")]
        s_wd = [C(stack, nc, "s_wd%d" % i) for i in range(4)]
        s_y = C(stack, nc, "s_y")
        s_mr = C(stack, nc, "s_mr")
        s_dbg = C(stack, nc, "s_dbg")

        block = stack.enter_context(nc.Block())

        def on(name, f):
            getattr(block, name)(f)

        def _(g):
            g.memset(ones_bf, 1.0)
            g.memset(eps5, 1e-5)
            g.memset(eps35, 1e-35)
            g.engine_nop().then_inc(sPL.h, 1)
            g.dma_start(out=eye_bf, in_=eye_in).then_inc(s_wdma.h, 16)
            g.dma_start(out=rout_b, in_=router.rearrange("(c p) e -> p c e", p=128)).then_inc(s_wdma.h, 16)
            g.dma_start(out=sel_b, in_=sel_in.rearrange("p (a b) -> p a b", a=8, b=128)).then_inc(s_wdma.h, 16)
            for c in range(DC):
                g.dma_start(out=wq[:, c, :], in_=wqkvT[c * 128:(c + 1) * 128, :]).then_inc(s_wdma.h, 16)
        on("gpsimd", _)
        sPL.n += 1
        s_wdma.n += 16 * (3 + DC)
        WQ_READY = s_wdma.n
        ONES_READY = sPL.n

        _sid = nc.enter_named_scope("phA", False)[0]
        xb_r = xb.rearrange("(c p) t -> p c t", p=128)
        apply_done = [0] * 8
        prev_stats = [0]

        for s in range(8):
            p = s % 2

            def _(sy, s=s, p=p):
                if s >= 2:
                    sy.wait_ge(sDV.h, apply_done[s - 2])
                sy.dma_start(out=xs[p], in_=xb_r[:, :, s * 512:(s + 1) * 512]).then_inc(s_xs[p].h, 16)
            on("sync", _)
            s_xs[p].n += 16

            def _(v, s=s, p=p, w=s_xs[p].n):
                v.wait_ge(s_xs[p].h, w)
                if s == 0:
                    v.wait_ge(sPL.h, ONES_READY)
                xls = xln[:, :, s * 512:(s + 1) * 512]
                v.tensor_copy(xls, xs[p]).then_inc(sDV.h, 1)
            on("vector", _)
            sDV.n += 1
            cast_done = sDV.n

            def _(sc, s=s, cast_done=cast_done, pstats=prev_stats[0]):
                sc.wait_ge(sDV.h, cast_done)
                if pstats:
                    sc.wait_ge(sPE.h, pstats)
                xls = xln[:, :, s * 512:(s + 1) * 512]
                sc.square(ubuf[:], xls).then_inc(sAC.h, 1)
            on("scalar", _)
            sAC.n += 1
            sq_done = sAC.n

            def _(t, s=s, cast_done=cast_done, sq_done=sq_done):
                t.wait_ge(sDV.h, cast_done)
                xls = xln[:, :, s * 512:(s + 1) * 512]
                for dc in range(DC):
                    t.matmul(PS[0][:], ones_bf, xls[:, dc, :], start=(dc == 0), stop=(dc == DC - 1))
                t.wait_ge(sAC.h, sq_done)
                for dc in range(DC):
                    mm = t.matmul(PS[1][:], ones_bf, ubuf[:, dc, :], start=(dc == 0), stop=(dc == DC - 1))
                mm.then_inc(sPE.h, 1)
            on("tensor", _)
            sPE.n += 1
            stats_done = sPE.n
            prev_stats[0] = stats_done

            def _(v, stats_done=stats_done):
                v.wait_ge(sPE.h, stats_done)
                v.tensor_scalar(a_m[:], PS[0][:], 1.0 / D, None, ALU.mult)
                v.tensor_scalar(a_ms[:], PS[1][:], 1.0 / D, None, ALU.mult)
                v.tensor_tensor(a_t[:], a_m[:], a_m[:], ALU.mult)
                v.tensor_tensor(a_ms[:], a_ms[:], a_t[:], ALU.subtract).then_inc(sDV.h, 1)
            on("vector", _)
            sDV.n += 1
            var_done = sDV.n

            def _(sc, var_done=var_done):
                sc.wait_ge(sDV.h, var_done)
                if var_done <= 3:
                    sc.wait_ge(sPL.h, 1)
                sc.activation(a_t[:], a_ms[:], AF.Ln, bias=eps5[:])
                sc.activation(a_r[:], a_t[:], AF.Exp, scale=-0.5).then_inc(sAC.h, 1)
            on("scalar", _)
            sAC.n += 1
            rstd_done = sAC.n

            def _(v, s=s, rstd_done=rstd_done):
                v.wait_ge(sAC.h, rstd_done)
                v.tensor_copy(a_rb[:], a_r[:])
                v.tensor_tensor(a_t[:], a_m[:], a_r[:], ALU.mult)
                v.tensor_scalar(a_bb[:], a_t[:], -1.0, None, ALU.mult)
                xls = xln[:, :, s * 512:(s + 1) * 512]
                for dc in range(DC):
                    v.tensor_tensor(xls[:, dc, :], xls[:, dc, :], a_rb[:], ALU.mult)
                    mm = v.tensor_tensor(xls[:, dc, :], xls[:, dc, :], a_bb[:], ALU.add)
                mm.then_inc(sDV.h, 1)
            on("vector", _)
            sDV.n += 1
            apply_done[s] = sDV.n

        nc.leave_named_scope("phA", _sid, False)
        LN1_DONE = apply_done[7]

        if stage == "ln1":
            def _(g):
                g.wait_ge(sDV.h, LN1_DONE)
                g.dma_start(out=dbg["xln"].rearrange("(c p) t -> p c t", p=128),
                            in_=xln[:]).then_inc(s_dbg.h, 16)
            on("gpsimd", _)
            s_dbg.n += 16

            def _(sy):
                sy.wait_ge(s_dbg.h, s_dbg.n)
            on("sync", _)
            return nc

        _sid = nc.enter_named_scope("phK", False)[0]
        kss_free_dv = 0
        kq_evac_pair = [0, 0]
        pending_kss = [None]
        for kg in range(2):
            for fc in range(DC):
                b = fc % 2
                for half in range(2):
                    def _(t, kg=kg, fc=fc, half=half, ge=kq_evac_pair[half]):
                        if kg == 0 and fc == 0 and half == 0:
                            t.wait_ge(s_wdma.h, WQ_READY)
                            t.wait_ge(sDV.h, LN1_DONE)
                        if ge:
                            t.wait_ge(sAC.h, ge)
                        for dc in range(DC):
                            for k2 in range(2):
                                ks = half * 2 + k2
                                mm = t.matmul(PS[ks][:], wq[:, dc, D + fc * 128:D + (fc + 1) * 128],
                                              xln[:, dc, kg * 2048 + ks * 512:kg * 2048 + (ks + 1) * 512],
                                              start=(dc == 0), stop=(dc == DC - 1))
                        mm.then_inc(sPE.h, 1)
                    on("tensor", _)
                    sPE.n += 1
                    prod_done = sPE.n
                    if half == 0 and pending_kss[0] is not None:
                        pending_kss[0]()
                        pending_kss[0] = None

                    def _(sc, b=b, half=half, prod_done=prod_done,
                          ws=(s_ksp.n if b == 0 else s_qsp.n)):
                        sc.wait_ge(sPE.h, prod_done)
                        if half == 0:
                            sc.wait_ge((s_ksp if b == 0 else s_qsp).h, ws)
                        for k2 in range(2):
                            ks = half * 2 + k2
                            cp = sc.copy(spill[b][:, ks, :], PS[ks][:])
                        cp.then_inc(sAC.h, 1)
                    on("scalar", _)
                    sAC.n += 1
                    kq_evac_pair[half] = sAC.n
                evac_done = sAC.n

                def _(sy, kg=kg, fc=fc, b=b, evac_done=evac_done):
                    sy.wait_ge(sAC.h, evac_done)
                    sy.wait_ge(sDV.h, sDV.n)
                    sy.dma_start(out=k_hbm[fc, :, kg * 2048:(kg + 1) * 2048],
                                 in_=spill_f[b]).then_inc((s_ksp if b == 0 else s_qsp).h, 16)
                on("sync", _)
                if b == 0:
                    s_ksp.n += 16
                else:
                    s_qsp.n += 16

                def _(v, b=b, evac_done=evac_done):
                    v.wait_ge(sAC.h, evac_done)
                    v.wait_ge(sPE.h, sPE.n)
                    v.tensor_tensor(ksq[:], spill_f[b], spill_f[b], ALU.mult).then_inc(sDV.h, 1)
                on("vector", _)
                sDV.n += 1
                ksq_done = sDV.n

                def emit_kss(fc=fc, ksq_done=ksq_done, kss_free_dv=kss_free_dv):
                    def _(t):
                        t.wait_ge(sDV.h, ksq_done)
                        if fc == 0:
                            t.wait_ge(sDV.h, kss_free_dv)
                        for ks in range(4):
                            mm = t.matmul(PS[4 + ks][:], ones_bf, ksq[:, ks * 512:(ks + 1) * 512],
                                          start=(fc == 0), stop=(fc == DC - 1))
                        mm.then_inc(sPE.h, 1)
                    on("tensor", _)
                    sPE.n += 1
                pending_kss[0] = emit_kss

            pending_kss[0]()
            pending_kss[0] = None
            kss_mm_done = sPE.n

            def _(v, kg=kg, kss_mm_done=kss_mm_done, base=sDV.n):
                v.wait_ge(sPE.h, kss_mm_done)
                n = base
                for ks in range(4):
                    v.reduce_max(a_m[:, 0:1], PS[4 + ks][:], axis=AX.X).then_inc(sDV.h, 1)
                    n += 1
                    v.wait_ge(sDV.h, n)
                    if kg == 0 and ks == 0:
                        op = v.tensor_copy(kmax[:], a_m[:, 0:1])
                    else:
                        op = v.tensor_tensor(kmax[:], kmax[:], a_m[:, 0:1], ALU.max)
                    op.then_inc(sDV.h, 1)
                    n += 1
                    v.wait_ge(sDV.h, n)
            on("vector", _)
            sDV.n += 8
            kss_free_dv = sDV.n

        nc.leave_named_scope("phK", _sid, False)
        _sid = nc.enter_named_scope("phV", False)[0]
        v_evac = [0, 0]
        for tt in range(NKC):
            bk = (tt % 2) * 2
            def _(t, tt=tt, bk=bk, ge=v_evac[tt % 2]):
                if ge:
                    t.wait_ge(sAC.h, ge)
                if tt < 2:
                    t.wait_ge(sAC.h, sAC.n)
                    t.wait_ge(sDV.h, sDV.n)
                for dc in range(DC):
                    for fs in range(2):
                        mm = t.matmul(PS[bk + fs][:], xln[:, dc, tt * 128:(tt + 1) * 128],
                                      wq[:, dc, 2 * D + fs * 512:2 * D + (fs + 1) * 512],
                                      start=(dc == 0), stop=(dc == DC - 1))
                mm.then_inc(sPE.h, 1)
            on("tensor", _)
            sPE.n += 1
            vprod = sPE.n

            def _(sc, tt=tt, bk=bk, vprod=vprod, wk=s_ksp.n):
                sc.wait_ge(sPE.h, vprod)
                if tt < 2:
                    sc.wait_ge(s_ksp.h, wk)
                for fs in range(2):
                    cp = sc.copy(vbf[:, tt, fs * 512:(fs + 1) * 512], PS[bk + fs][:])
                cp.then_inc(sAC.h, 1)
            on("scalar", _)
            sAC.n += 1
            v_evac[tt % 2] = sAC.n

        nc.leave_named_scope("phV", _sid, False)
        V_DONE_AC = sAC.n

        _sid = nc.enter_named_scope("phQ", False)[0]
        for fc in range(DC):
            b = fc % 2
            for half in range(2):
                def _(t, fc=fc, half=half, ge=kq_evac_pair[half]):
                    if fc == 0 and half == 0:
                        t.wait_ge(sAC.h, sAC.n)
                    if ge:
                        t.wait_ge(sAC.h, ge)
                    if half == 0 and fc > 0:
                        t.wait_ge(sDV.h, sDV.n)
                    for dc in range(DC):
                        for k2 in range(2):
                            qs = half * 2 + k2
                            mm = t.matmul(PS[qs][:], wq[:, dc, fc * 128:(fc + 1) * 128],
                                          xln[:, dc, qs * 512:(qs + 1) * 512],
                                          start=(dc == 0), stop=(dc == DC - 1))
                    mm.then_inc(sPE.h, 1)
                on("tensor", _)
                sPE.n += 1
                prod_done = sPE.n
                if half == 0 and pending_kss[0] is not None:
                    pending_kss[0]()
                    pending_kss[0] = None

                def _(sc, b=b, half=half, prod_done=prod_done,
                      ws=(s_ksp.n if b == 0 else s_qsp.n)):
                    sc.wait_ge(sPE.h, prod_done)
                    if half == 0:
                        sc.wait_ge((s_ksp if b == 0 else s_qsp).h, ws)
                    for k2 in range(2):
                        qs = half * 2 + k2
                        cp = sc.activation(spill[b][:, qs, :], PS[qs][:], AF.Copy, scale=0.03125)
                    cp.then_inc(sAC.h, 1)
                on("scalar", _)
                sAC.n += 1
                kq_evac_pair[half] = sAC.n
            evac_done = sAC.n

            def _(sy, fc=fc, b=b, evac_done=evac_done):
                sy.wait_ge(sAC.h, evac_done)
                sy.wait_ge(sDV.h, sDV.n)
                sy.dma_start(out=q_hbm[fc, :, :], in_=spill_f[b]).then_inc((s_ksp if b == 0 else s_qsp).h, 16)
            on("sync", _)
            if b == 0:
                s_ksp.n += 16
            else:
                s_qsp.n += 16

            def _(v, b=b, evac_done=evac_done):
                v.wait_ge(sAC.h, evac_done)
                v.wait_ge(sPE.h, sPE.n)
                v.tensor_tensor(qsq[:], spill_f[b], spill_f[b], ALU.mult).then_inc(sDV.h, 1)
            on("vector", _)
            sDV.n += 1
            qsq_done = sDV.n

            def emit_qss(fc=fc, qsq_done=qsq_done):
                def _(t):
                    t.wait_ge(sDV.h, qsq_done)
                    for qs in range(4):
                        mm = t.matmul(PS[4 + qs][:], ones_bf, qsq[:, qs * 512:(qs + 1) * 512],
                                      start=(fc == 0), stop=(fc == DC - 1))
                    mm.then_inc(sPE.h, 1)
                on("tensor", _)
                sPE.n += 1
            pending_kss[0] = emit_qss

        pending_kss[0]()
        pending_kss[0] = None
        Q_DONE_PE = sPE.n

        nc.leave_named_scope("phQ", _sid, False)
        _sid = nc.enter_named_scope("phNegm", False)[0]
        def _(sc, w0=s_ksp.n, w1=s_qsp.n):
            sc.wait_ge(sPE.h, Q_DONE_PE)
            sc.wait_ge(s_ksp.h, w0)
            sc.wait_ge(s_qsp.h, w1)
            for qs in range(4):
                sc.activation(lnq[:, qs * 512:(qs + 1) * 512], PS[4 + qs][:], AF.Ln)
            sc.activation(klog[:], kmax[:], AF.Ln).then_inc(sAC.h, 1)
        on("scalar", _)
        sAC.n += 1
        lnq_done = sAC.n

        def _(v, lnq_done=lnq_done):
            v.wait_ge(sAC.h, lnq_done)
            v.tensor_scalar(klog[:], klog[:], 0.5, None, ALU.mult)
            v.engine_nop().then_inc(sDV.h, 1)
        on("vector", _)
        sDV.n += 1
        klog_done = sDV.n

        def _(sc, klog_done=klog_done):
            sc.wait_ge(sDV.h, klog_done)
            sc.activation(mrep[:], lnq[:], AF.Exp, scale=0.5, bias=klog[:]).then_inc(sAC.h, 1)
        on("scalar", _)
        sAC.n += 1
        m_done = sAC.n

        def _(v, m_done=m_done):
            v.wait_ge(sAC.h, m_done)
            v.tensor_scalar(negm[:], mrep[:], -1.0, 19.0, ALU.mult, ALU.add)
            v.engine_nop().then_inc(sDV.h, 1)
        on("vector", _)
        sDV.n += 1
        NEGM_DONE = sDV.n

        nc.leave_named_scope("phNegm", _sid, False)
        if stage == "qkv":
            def _(g, wk=s_ksp.n, wqq=s_qsp.n):
                g.wait_ge(sDV.h, NEGM_DONE)
                g.wait_ge(s_ksp.h, wk)
                g.wait_ge(s_qsp.h, wqq)
                g.dma_start(out=dbg["k"], in_=k_hbm).then_inc(s_dbg.h, 16)
                g.dma_start(out=dbg["q"], in_=q_hbm).then_inc(s_dbg.h, 16)
                g.dma_start(out=dbg["v"].rearrange("(a p) f -> p a f", p=128), in_=vbf[:]).then_inc(s_dbg.h, 16)
                g.dma_start(out=dbg["negm"], in_=negm[:]).then_inc(s_dbg.h, 16)
            on("gpsimd", _)
            s_dbg.n += 64

            def _(sy):
                sy.wait_ge(s_dbg.h, s_dbg.n)
            on("sync", _)
            return nc

        K_SPILLED = s_ksp.n
        Q_SPILLED = s_qsp.n
        kst_pe_free = [0, 0]
        qst_pe_free = [0, 0]
        pt_pe_free = 0
        wb_dv_free = 0
        x2b_cast_prev = [0]
        bank_evac = [0, 0, 0]
        ST_BANKS = [0, 1, 7]
        pv_bank_evac = [0, 0, 0]
        PV_BANKS = [3, 4, 2]

        for qb in range(NQB):
            _sqb = nc.enter_named_scope("qb%d" % qb, False)[0]
            def _(sy, qb=qb, w=qst_pe_free[qb % 2]):
                if qb == 0:
                    sy.wait_ge(sDV.h, NEGM_DONE)
                    sy.wait_ge(s_ksp.h, K_SPILLED)
                    sy.wait_ge(s_qsp.h, Q_SPILLED)
                sy.wait_ge(sPE.h, w)
                sy.dma_start(out=qst[qb % 2],
                             in_=q_hbm[:, :, qb * 512:(qb + 1) * 512].rearrange("c p t -> p c t")
                             ).then_inc(s_qst[qb % 2].h, 16)
            on("sync", _)
            s_qst[qb % 2].n += 16
            qst_ready = s_qst[qb % 2].n

            for kh in range(8):
                def _(sy, kh=kh, w=kst_pe_free[kh % 2]):
                    sy.wait_ge(sPE.h, w)
                    sy.dma_start(out=kst[kh % 2],
                                 in_=k_hbm[:, :, kh * 512:(kh + 1) * 512].rearrange("c p t -> p c t")
                                 ).then_inc(s_kst[kh % 2].h, 16)
                on("sync", _)
                s_kst[kh % 2].n += 16
                kr = s_kst[kh % 2].n

                for kk in range(4):
                    kc = kh * 4 + kk
                    bank_guard = bank_evac[kc % 3]

                    def _(t, qb=qb, kh=kh, kk=kk, kc=kc, qst_ready=qst_ready, kr=kr,
                          bank_guard=bank_guard, pt_guard=pt_pe_free):
                        if kk == 0:
                            t.wait_ge(s_kst[kh % 2].h, kr)
                            if kh == 0:
                                t.wait_ge(s_qst[qb % 2].h, qst_ready)
                        if bank_guard:
                            t.wait_ge(sAC.h, bank_guard)
                        bk = PS[ST_BANKS[kc % 3]]
                        for dc in range(DC):
                            t.matmul(bk[:], kst[kh % 2][:, dc, kk * 128:(kk + 1) * 128],
                                     qst[qb % 2][:, dc, :], start=(dc == 0), stop=False)
                        mm = t.matmul(bk[:], ones_bf[0:1, :],
                                      negm[0:1, qb * 512:(qb + 1) * 512], start=False, stop=True)
                        mm.then_inc(sPE.h, 1)
                    on("tensor", _)
                    sPE.n += 1
                    st_done = sPE.n

                    def _(sc, kc=kc, st_done=st_done, pt_guard=pt_pe_free):
                        sc.wait_ge(sPE.h, st_done)
                        if kc == 0 and pt_guard:
                            sc.wait_ge(sPE.h, pt_guard)
                        sc.activation(pt[:, kc, :], PS[ST_BANKS[kc % 3]][:], AF.Exp).then_inc(sAC.h, 1)
                    on("scalar", _)
                    sAC.n += 1
                    bank_evac[kc % 3] = sAC.n
                kst_pe_free[kh % 2] = sPE.n
            qst_pe_free[qb % 2] = sPE.n
            EXP_DONE = sAC.n

            def _(t, EXP_DONE=EXP_DONE):
                t.wait_ge(sAC.h, EXP_DONE)
                t.wait_ge(sDV.h, sDV.n)
                for kc in range(NKC):
                    mm = t.matmul(PS[2][:], ones_bf, pt[:, kc, :], start=(kc == 0), stop=(kc == NKC - 1))
                mm.then_inc(sPE.h, 1)
            on("tensor", _)
            sPE.n += 1
            sum_done = sPE.n

            def _(v, sum_done=sum_done, wb_dv_free=wb_dv_free):
                v.wait_ge(sPE.h, sum_done)
                v.wait_ge(sDV.h, wb_dv_free)
                v.tensor_scalar(s_r[:], PS[2][:], 1.0, None, ALU.mult).then_inc(sDV.h, 1)
            on("vector", _)
            sDV.n += 1
            sum_ev = sDV.n

            def _(v, sum_ev=sum_ev):
                v.wait_ge(sDV.h, sum_ev)
                v.reciprocal(s_eps[:], s_r[:])
                v.engine_nop().then_inc(sDV.h, 1)
            on("vector", _)
            sDV.n += 1
            eps_done = sDV.n

            def _(sy, qb=qb, w=s_x2w.n, wc=x2b_cast_prev[0]):
                sy.wait_ge(s_x2w.h, w)
                if wc:
                    sy.wait_ge(sDV.h, wc)
                sy.dma_start(out=xown, in_=xb_r[:, :, qb * 512:(qb + 1) * 512]).then_inc(s_xown.h, 16)
            on("sync", _)
            s_xown.n += 16
            xown_ready = s_xown.n

            sq_done_fs = [0] * DC
            for fs in range(DC):
                def _(t, fs=fs, eps_done=eps_done, bg=pv_bank_evac[fs % 3]):
                    if fs == 0:
                        t.wait_ge(sDV.h, eps_done)
                        t.wait_ge(sAC.h, V_DONE_AC)
                    if bg:
                        t.wait_ge(sDV.h, bg)
                    for kc in range(NKC):
                        mm = t.matmul(PS[PV_BANKS[fs % 3]][:], vbf[:, kc, fs * 128:(fs + 1) * 128],
                                      pt[:, kc, :], start=(kc == 0), stop=(kc == NKC - 1))
                    mm.then_inc(sPE.h, 1)
                on("tensor", _)
                sPE.n += 1
                pv_done = sPE.n

                def _(v, fs=fs, pv_done=pv_done):
                    v.wait_ge(sPE.h, pv_done)
                    v.tensor_tensor(w_b[:, fs, :], PS[PV_BANKS[fs % 3]][:], s_eps[:], ALU.mult)
                    v.tensor_tensor(uw[:, fs, :], w_b[:, fs, :], w_b[:, fs, :], ALU.mult).then_inc(sDV.h, 1)
                on("vector", _)
                sDV.n += 1
                pv_bank_evac[fs % 3] = sDV.n
                sq_done = sDV.n

                sq_done_fs[fs] = sq_done
                if fs > 0:
                    def _(t, fs=fs, sq_done=sq_done_fs[fs - 1]):
                        t.wait_ge(sDV.h, sq_done)
                        t.matmul(PS[5][:], ones_bf, w_b[:, fs - 1, :], start=(fs - 1 == 0), stop=False)
                        mm = t.matmul(PS[6][:], ones_bf, uw[:, fs - 1, :], start=(fs - 1 == 0), stop=False)
                        mm.then_inc(sPE.h, 1)
                    on("tensor", _)
                    sPE.n += 1

            def _(t, sq_done=sq_done_fs[DC - 1]):
                t.wait_ge(sDV.h, sq_done)
                t.matmul(PS[5][:], ones_bf, w_b[:, DC - 1, :], start=False, stop=True)
                mm = t.matmul(PS[6][:], ones_bf, uw[:, DC - 1, :], start=False, stop=True)
                mm.then_inc(sPE.h, 1)
            on("tensor", _)
            sPE.n += 1
            pt_pe_free = sPE.n
            ln2_mm_done = sPE.n

            def _(v, ln2_mm_done=ln2_mm_done):
                v.wait_ge(sPE.h, ln2_mm_done)
                v.tensor_scalar(s_m[:], PS[5][:], 1.0 / D, None, ALU.mult)
                v.tensor_scalar(s_ms[:], PS[6][:], 1.0 / D, None, ALU.mult)
                v.tensor_tensor(s_r[:], s_m[:], s_m[:], ALU.mult)
                v.tensor_tensor(s_ms[:], s_ms[:], s_r[:], ALU.subtract)
                v.engine_nop().then_inc(sDV.h, 1)
            on("vector", _)
            sDV.n += 1
            var2_done = sDV.n

            def _(sc, var2_done=var2_done):
                sc.wait_ge(sDV.h, var2_done)
                sc.activation(s_ms[:], s_ms[:], AF.Ln, bias=eps5[:])
                sc.activation(s_r[:], s_ms[:], AF.Exp, scale=-0.5).then_inc(sAC.h, 1)
            on("scalar", _)
            sAC.n += 1
            rstd2_done = sAC.n

            def _(v, rstd2_done=rstd2_done, xown_ready=xown_ready):
                v.wait_ge(sAC.h, rstd2_done)
                v.wait_ge(s_xown.h, xown_ready)
                v.tensor_copy(s_mb[:], s_m[:])
                v.tensor_copy(s_rb[:], s_r[:])
                for fs in range(DC):
                    v.tensor_tensor(uw[:, fs, :], w_b[:, fs, :], s_mb[:], ALU.subtract)
                    v.tensor_tensor(uw[:, fs, :], uw[:, fs, :], s_rb[:], ALU.mult)
                    mm = v.tensor_tensor(xown[:, fs, :], xown[:, fs, :], uw[:, fs, :], ALU.add)
                mm.then_inc(sDV.h, 1)
            on("vector", _)
            sDV.n += 1
            wb_dv_free = sDV.n
            x2_done = sDV.n

            def _(v, x2_done=x2_done):
                v.wait_ge(sDV.h, x2_done)
                v.tensor_copy(uw[:], xown[:]).then_inc(sDV.h, 1)
            on("vector", _)
            sDV.n += 1
            x2b_done = sDV.n
            x2b_cast_prev[0] = x2b_done

            def _(sc, x2b_done=x2b_done):
                sc.wait_ge(sDV.h, x2b_done)
                sc.square(w_b[:], uw[:]).then_inc(sAC.h, 1)
            on("scalar", _)
            sAC.n += 1
            sq3q_done = sAC.n

            def _(t, x2b_done=x2b_done, sq3q_done=sq3q_done):
                t.wait_ge(sDV.h, x2b_done)
                for dc in range(DC):
                    t.matmul(PS[5][:], ones_bf, uw[:, dc, :], start=(dc == 0), stop=(dc == DC - 1))
                t.wait_ge(sAC.h, sq3q_done)
                for dc in range(DC):
                    mm = t.matmul(PS[6][:], ones_bf, w_b[:, dc, :], start=(dc == 0), stop=(dc == DC - 1))
                mm.then_inc(sPE.h, 1)
            on("tensor", _)
            sPE.n += 1
            st3q_done = sPE.n

            def _(v, qb=qb, st3q_done=st3q_done):
                v.wait_ge(sPE.h, st3q_done)
                v.tensor_scalar(mr3[qb][:, 0, :], PS[5][:], 1.0 / D, None, ALU.mult)
                v.tensor_scalar(s_ms[:], PS[6][:], 1.0 / D, None, ALU.mult)
                v.tensor_tensor(s_r[:], mr3[qb][:, 0, :], mr3[qb][:, 0, :], ALU.mult)
                v.tensor_tensor(s_ms[:], s_ms[:], s_r[:], ALU.subtract).then_inc(sDV.h, 1)
            on("vector", _)
            sDV.n += 1
            var3q = sDV.n

            def _(sc, qb=qb, var3q=var3q):
                sc.wait_ge(sDV.h, var3q)
                sc.activation(s_ms[:], s_ms[:], AF.Ln, bias=eps5[:])
                sc.activation(mr3[qb][:, 1, :], s_ms[:], AF.Exp, scale=-0.5).then_inc(sAC.h, 1)
            on("scalar", _)
            sAC.n += 1
            wb_dv_free = sDV.n
            MR3_AC = sAC.n

            if qb in (1, 3):
                def _(g, qb=qb, w=MR3_AC):
                    g.wait_ge(sAC.h, w)
                    g.dma_start(out=mr_hbm[:, (qb - 1) * 1024:(qb + 1) * 1024],
                                in_=mr3all[:, (qb - 1) * 1024:(qb + 1) * 1024]).then_inc(s_mr.h, 16)
                on("gpsimd", _)
                s_mr.n += 16

            def _(g, qb=qb, x2_done=x2_done):
                g.wait_ge(sDV.h, x2_done)
                g.dma_start(out=x2_hbm[:, :, qb * 512:(qb + 1) * 512].rearrange("c p t -> p c t"),
                             in_=xown[:]).then_inc(s_x2w.h, 16)
            on("gpsimd", _)
            s_x2w.n += 16
            nc.leave_named_scope("qb%d" % qb, _sqb, False)

            if stage.startswith("qb") and qb == int(stage[2:]):
                def _(g, w=s_x2w.n):
                    g.wait_ge(s_x2w.h, w)
                    g.dma_start(out=dbg["pt"], in_=pt.rearrange("p a b -> p (a b)")).then_inc(s_dbg.h, 16)
                    g.dma_start(out=dbg["wb"], in_=w_b.rearrange("p a b -> p (a b)")).then_inc(s_dbg.h, 16)
                    g.dma_start(out=dbg["eps"], in_=s_eps).then_inc(s_dbg.h, 16)
                    g.dma_start(out=dbg["sm"], in_=s_m).then_inc(s_dbg.h, 16)
                    g.dma_start(out=dbg["sms"], in_=s_ms).then_inc(s_dbg.h, 16)
                    g.dma_start(out=dbg["sr"], in_=s_r).then_inc(s_dbg.h, 16)
                    g.dma_start(out=dbg["xo"], in_=xown.rearrange("p a b -> p (a b)")).then_inc(s_dbg.h, 16)
                on("gpsimd", _)
                s_dbg.n += 16 * 7

                def _(sy, w=s_dbg.n):
                    sy.wait_ge(s_dbg.h, w)
                on("sync", _)
                return nc


        PT_FREE_QB3 = pt_pe_free
        ATTN_PE = sPE.n
        ATTN_DV = sDV.n
        ATTN_AC3 = sAC.n
        X2_STORED = s_x2w.n

        if stage == "attn":
            def _(g):
                g.wait_ge(s_x2w.h, X2_STORED)
                g.dma_start(out=dbg["x2"], in_=x2_hbm).then_inc(s_dbg.h, 16)
            on("gpsimd", _)
            s_dbg.n += 16

            def _(sy):
                sy.wait_ge(s_dbg.h, s_dbg.n)
            on("sync", _)
            return nc

        def pe_srcs(j):
            if j < 8:
                return (eg[j].rearrange("(c p) d -> p c d", p=128),
                        eu[j].rearrange("(c p) d -> p c d", p=128),
                        ed[j].rearrange("(c p) hh -> p c hh", p=128), j)
            jj = j - 8
            return (sg[:, jj * 512:(jj + 1) * 512].rearrange("(c p) d -> p c d", p=128),
                    su[:, jj * 512:(jj + 1) * 512].rearrange("(c p) d -> p c d", p=128),
                    sd[jj * 512:(jj + 1) * 512, :].rearrange("(c p) hh -> p c hh", p=128), None)

        groups = [[8, 9, 0, 1], [2, 3, 4, 5], [6, 7]]

        wg_free = [0, 0]
        wu_free = [0, 0]
        wd_free = [0] * 4
        crep_free = [0, 0]
        gu_evac = [(0, 0), (0, 0)]
        down_evac = [(0, 0), (0, 0)]

        first_ffn_load = [True]
        for hf in range(2):
            _shf = nc.enter_named_scope("ffn_pre%d" % hf, False)[0]
            x2_ready_ts = [0, 0]
            for ts in range(2):
                def _(sy, hf=hf, ts=ts, wy=s_y.n):
                    if hf == 0:
                        if ts == 0:
                            sy.wait_ge(s_x2w.h, 32)
                            sy.wait_ge(sPE.h, PT_FREE_QB3)
                    else:
                        if ts == 0:
                            sy.wait_ge(s_x2w.h, X2_STORED)
                            sy.wait_ge(s_y.h, wy)
                            sy.wait_ge(sDV.h, sDV.n)
                    sy.dma_start(out=x2sl[:, :, ts * 512:(ts + 1) * 512],
                                 in_=x2_hbm[:, :, hf * TH + ts * 512:hf * TH + (ts + 1) * 512]
                                 .rearrange("c p t -> p c t")).then_inc(s_x2r.h, 16)
                on("sync", _)
                s_x2r.n += 16
                x2_ready_ts[ts] = s_x2r.n
            x2_ready = s_x2r.n

            cast3_ts = [0, 0]
            for ts in range(2):
                def _(v, ts=ts, w=x2_ready_ts[ts], hf=hf):
                    v.wait_ge(s_x2r.h, w)
                    if hf == 0 and ts == 0:
                        v.memset(coef_fm[0:32, :], 0.0)
                    v.tensor_copy(x2ln[:, :, ts * 512:(ts + 1) * 512],
                                  x2sl[:, :, ts * 512:(ts + 1) * 512]).then_inc(sDV.h, 1)
                on("vector", _)
                sDV.n += 1
                cast3_ts[ts] = sDV.n
            cast3_done = sDV.n

            def _(g, hf=hf, w=(16 if hf == 0 else s_mr.n), wu=sDV.n):
                g.wait_ge(s_mr.h, w)
                g.wait_ge(sDV.h, wu)
                for ts in range(2):
                    g.dma_start(out=mrh[ts],
                                in_=mr_hbm[:, (hf * 2 + ts) * 1024:(hf * 2 + ts + 1) * 1024]
                                .rearrange("p (a b) -> p a b", a=2, b=512)).then_inc(s_mr.h, 16)
            on("gpsimd", _)
            s_mr.n += 32
            mr_ready = s_mr.n

            apply_ts = [0, 0]
            for ts in range(2):
                def _(v, ts=ts, cast3_done=cast3_ts[ts], mr_ready=mr_ready):
                    v.wait_ge(sDV.h, cast3_done)
                    v.wait_ge(s_mr.h, mr_ready)
                    for dc in range(DC):
                        sl = x2ln[:, dc, ts * 512:(ts + 1) * 512]
                        v.tensor_tensor(sl, sl, mrh[ts][:, 0, :], ALU.subtract)
                        mm = v.tensor_tensor(sl, sl, mrh[ts][:, 1, :], ALU.mult)
                    mm.then_inc(sDV.h, 1)
                on("vector", _)
                sDV.n += 1
                apply_ts[ts] = sDV.n
            x2ln_apply = sDV.n

            rst = {"coef_done_dv": 0, "router_ac": 0, "logits_pe": 0}
            top2_done = [0] * 8
            pexp_done_tt = [0] * 8

            def emit_logits(apply_ts=apply_ts):
                for half4 in range(2):
                    def _(t, half4=half4, w=apply_ts[half4]):
                        t.wait_ge(sDV.h, w)
                        n = 0
                        for tt in range(half4 * 4, half4 * 4 + 4):
                            for dc in range(DC):
                                n += 1
                                mm = t.matmul(PS[6][:, tt * 8:(tt + 1) * 8],
                                              x2ln[:, dc, tt * 128:(tt + 1) * 128],
                                              rout_b[:, dc, :], start=(half4 == 0 and n == 1),
                                              stop=(half4 == 1 and n == 32),
                                              skip_group_check=True)
                        mm.then_inc(sPE.h, 1)
                    on("tensor", _)
                    sPE.n += 1
                rst["logits_pe"] = sPE.n

            def make_chain(tt):
                def chain():
                    def _(v, tt=tt, lp=rst["logits_pe"], base=sDV.n,
                          pe_prev=(pexp_done_tt[tt - 1] if tt else 0)):
                        v.wait_ge(sPE.h, lp)
                        if pe_prev:
                            v.wait_ge(sAC.h, pe_prev)
                        v.reduce_max(lmax[:], PS[6][:, tt * 8:(tt + 1) * 8], axis=AX.X).then_inc(sDV.h, 1)
                        v.wait_ge(sDV.h, base + 1)
                        v.tensor_scalar(lmax[:], lmax[:], -1.0, None, ALU.mult).then_inc(sDV.h, 1)
                    on("vector", _)
                    sDV.n += 2
                    nmax_done = sDV.n

                    def _(sc, tt=tt, nmax_done=nmax_done,
                          t2p=(top2_done[tt - 1] if tt else 0)):
                        sc.wait_ge(sDV.h, nmax_done)
                        if t2p:
                            sc.wait_ge(sDV.h, t2p)
                        sc.activation(f_t[:, 0:8], PS[6][:, tt * 8:(tt + 1) * 8],
                                      AF.Exp, bias=lmax[:]).then_inc(sAC.h, 1)
                    on("scalar", _)
                    sAC.n += 1
                    pexp_done_tt[tt] = sAC.n

                    def _(v, tt=tt, pexp=pexp_done_tt[tt], base=sDV.n):
                        v.wait_ge(sAC.h, pexp)
                        n = base
                        ops = [
                            lambda: v.reduce_sum(lsum[:], f_t[:, 0:8], axis=AX.X),
                            lambda: v.reciprocal(lrec[:], lsum[:]),
                            lambda: v.tensor_scalar(f_t[:, 0:8], f_t[:, 0:8], lrec[:], None, ALU.mult),
                            lambda: v.reduce_max(lmax[:], f_t[:, 0:8], axis=AX.X),
                            lambda: v.tensor_scalar(f_t[:, 8:16], f_t[:, 0:8], lmax[:], None, ALU.is_ge),
                            lambda: v.tensor_scalar(f_t[:, 8:16], f_t[:, 8:16], 2.0, None, ALU.mult),
                            lambda: v.tensor_tensor(f_t[:, 8:16], f_t[:, 0:8], f_t[:, 8:16], ALU.subtract),
                            lambda: v.reduce_max(lmax[:], f_t[:, 8:16], axis=AX.X),
                            lambda: v.tensor_scalar(f_t[:, 8:16], f_t[:, 0:8], lmax[:], None, ALU.is_ge),
                            lambda: v.tensor_tensor(coef_tma[:, tt * 8:(tt + 1) * 8],
                                                    f_t[:, 0:8], f_t[:, 8:16], ALU.mult),
                        ]
                        for op in ops:
                            op().then_inc(sDV.h, 1)
                            n += 1
                            v.wait_ge(sDV.h, n)
                    on("vector", _)
                    sDV.n += 10
                    top2_done[tt] = sDV.n
                return chain

            chain_queue = [make_chain(tt) for tt in range(8)]

            def emit_router_tail():
                while chain_queue:
                    chain_queue.pop(0)()

                def _(t, w=top2_done, pe7=pexp_done_tt):
                    t.wait_ge(sDV.h, max(w))
                    t.wait_ge(sAC.h, max(pe7))
                    for tt in range(8):
                        bank = PS[6] if tt < 4 else PS[7]
                        t.matmul(bank[0:8, (tt % 4) * 128:(tt % 4 + 1) * 128],
                                 coef_tma[:, tt * 8:(tt + 1) * 8], eye_bf,
                                 start=(tt % 4 == 0), stop=(tt % 4 == 3),
                                 skip_group_check=True).then_inc(sPE.h, 1)
                on("tensor", _)
                sPE.n += 8
                ctr_done = sPE.n

                def _(v, ctr_done=ctr_done):
                    v.wait_ge(sPE.h, ctr_done)
                    v.tensor_copy(coef_fm[0:8, 0:512], PS[6][0:8, 0:512])
                    mm = v.tensor_copy(coef_fm[0:8, 512:1024], PS[7][0:8, 0:512])
                    mm.then_inc(sDV.h, 1)
                on("vector", _)
                sDV.n += 1
                rst["coef_done_dv"] = sDV.n
            coef_done_dv = None

            for gi, grp in enumerate(groups):
                _sgg = nc.enter_named_scope("ffn_h%dg%d_gu" % (hf, gi), False)[0]
                grp_crep = [0]

                def emit_group_creps(routed):
                    if not routed:
                        return
                    for wave in range(0, len(routed), 2):
                        pair = routed[wave:wave + 2]

                        def _(t, pair=pair, cd=rst["coef_done_dv"]):
                            t.wait_ge(sDV.h, max(cd, sDV.n))
                            for i, ce in enumerate(pair):
                                for ts2 in range(2):
                                    mm = t.matmul(PS[4 + 2 * i + ts2][:], sel_b[:, ce, :],
                                                  coef_fm[0:32, ts2 * 512:(ts2 + 1) * 512],
                                                  start=True, stop=True)
                            mm.then_inc(sPE.h, 1)
                        on("tensor", _)
                        sPE.n += 1
                        bc_done = sPE.n

                        def _(v, pair=pair, bc_done=bc_done):
                            v.wait_ge(sPE.h, bc_done)
                            for i, ce in enumerate(pair):
                                for ts2 in range(2):
                                    mm = v.tensor_copy(crep[ce % 4][:, ts2 * 512:(ts2 + 1) * 512],
                                                       PS[4 + 2 * i + ts2][:])
                            mm.then_inc(sDV.h, 1)
                        on("vector", _)
                        sDV.n += 1
                    grp_crep[0] = sDV.n

                if gi > 0:
                    emit_group_creps([p for p in grp if p < 8])
                for j, pj in enumerate(grp):
                    if gi == 0 and j == 0:
                        emit_logits()
                    if gi == 0 and j == 2:
                        emit_router_tail()
                        emit_group_creps([p for p in grp if p < 8])
                    pre_router = (gi == 0 and j < 2)
                    wsrc, usrc, dsrc, ce = pe_srcs(pj)
                    slot = j % 2

                    def _(g, slot=slot, j=j, wsrc=wsrc, usrc=usrc, dsrc=dsrc,
                          wgf=wg_free[slot], wuf=wu_free[slot], wdf=wd_free[j],
                          first=first_ffn_load[0]):
                        if first:
                            g.wait_ge(sPE.h, ATTN_PE)
                            g.wait_ge(sDV.h, ATTN_DV)
                        g.wait_ge(sPE.h, wgf)
                        g.dma_start(out=wgb[slot], in_=wsrc).then_inc(s_wg[slot].h, 16)
                        g.wait_ge(sPE.h, wuf)
                        g.dma_start(out=wub[slot], in_=usrc).then_inc(s_wu[slot].h, 16)
                        g.wait_ge(sPE.h, wdf)
                        g.dma_start(out=wdb[j], in_=dsrc).then_inc(s_wd[j].h, 16)
                    on("gpsimd", _)
                    first_ffn_load[0] = False
                    s_wg[slot].n += 16
                    s_wu[slot].n += 16
                    s_wd[j].n += 16
                    wg_ready = s_wg[slot].n
                    wu_ready = s_wu[slot].n

                    cslot = (ce % 4) if ce is not None else None
                    crep_ready = grp_crep[0] if ce is not None else 0

                    for dffc in range(4):
                        par = (dffc % 2) if not pre_router else 0
                        bg = 0 if par == 0 else 4
                        eac, edv = gu_evac[par]

                        def _(t, slot=slot, dffc=dffc, bg=bg, wg_ready=wg_ready,
                              wu_ready=wu_ready, eac=eac, edv=edv, x2ln_apply=x2ln_apply,
                              crep_ready=crep_ready):
                            if dffc == 0:
                                t.wait_ge(s_wg[slot].h, wg_ready)
                                t.wait_ge(s_wu[slot].h, wu_ready)
                                t.wait_ge(sDV.h, x2ln_apply)
                            if dffc == 1 and crep_ready:
                                t.wait_ge(sDV.h, crep_ready)
                            if eac:
                                t.wait_ge(sAC.h, eac)
                            if edv:
                                t.wait_ge(sDV.h, edv)
                            for dc in range(DC):
                                for ts2 in range(2):
                                    t.matmul(PS[bg + ts2][:], wgb[slot][:, dc, dffc * 128:(dffc + 1) * 128],
                                             x2ln[:, dc, ts2 * 512:(ts2 + 1) * 512],
                                             start=(dc == 0), stop=(dc == DC - 1))
                            for dc in range(DC):
                                for ts2 in range(2):
                                    mm = t.matmul(PS[bg + 2 + ts2][:], wub[slot][:, dc, dffc * 128:(dffc + 1) * 128],
                                                  x2ln[:, dc, ts2 * 512:(ts2 + 1) * 512],
                                                  start=(dc == 0), stop=(dc == DC - 1))
                            mm.then_inc(sPE.h, 1)
                        on("tensor", _)
                        sPE.n += 1
                        gu_done = sPE.n

                        def _(sc, j=j, dffc=dffc, bg=bg, gu_done=gu_done):
                            sc.wait_ge(sPE.h, gu_done)
                            for ts2 in range(2):
                                cp = sc.activation(hb[j][:, dffc, ts2 * 512:(ts2 + 1) * 512],
                                                   PS[bg + ts2][:], AF.Silu)
                            cp.then_inc(sAC.h, 1)
                        on("scalar", _)
                        sAC.n += 1
                        silu_done = sAC.n

                        def _(v, j=j, dffc=dffc, bg=bg, silu_done=silu_done,
                              cslot=cslot, crep_ready=crep_ready):
                            v.wait_ge(sAC.h, silu_done)
                            if crep_ready:
                                v.wait_ge(sDV.h, crep_ready)
                            for ts2 in range(2):
                                hsl = hb[j][:, dffc, ts2 * 512:(ts2 + 1) * 512]
                                if cslot is not None:
                                    v.tensor_tensor(uwtmp[:, ts2, :], PS[bg + 2 + ts2][:],
                                                    crep[cslot][:, ts2 * 512:(ts2 + 1) * 512], ALU.mult)
                                else:
                                    v.tensor_copy(uwtmp[:, ts2, :], PS[bg + 2 + ts2][:])
                                mm = v.tensor_tensor(hsl, hsl, uwtmp[:, ts2, :], ALU.mult)
                            mm.then_inc(sDV.h, 1)
                        on("vector", _)
                        sDV.n += 1
                        gu_evac[par] = (silu_done, sDV.n)
                        if pre_router and chain_queue:
                            chain_queue.pop(0)()
                    wg_free[slot] = sPE.n
                    wu_free[slot] = sPE.n

                nc.leave_named_scope("ffn_h%dg%d_gu" % (hf, gi), _sgg, False)
                _sgd = nc.enter_named_scope("ffn_h%dg%d_dn" % (hf, gi), False)[0]
                H_READY_DV = sDV.n
                WD_READY = [s_wd[j].n for j in range(len(grp))]

                for hc in range(DC):
                    par = hc % 2
                    bk = 0 if par == 0 else 2
                    eac, edv = down_evac[par]

                    def _(t, gi=gi, grp=grp, hc=hc, bk=bk, H_READY_DV=H_READY_DV,
                          WD_READY=WD_READY, eac=eac, edv=edv):
                        t.wait_ge(sDV.h, H_READY_DV)
                        if hc == 0:
                            for j in range(len(grp)):
                                t.wait_ge(s_wd[j].h, WD_READY[j])
                        if eac:
                            t.wait_ge(sAC.h, eac)
                        if edv:
                            t.wait_ge(sDV.h, edv)
                        nmm = 4 * len(grp)
                        for ts2 in range(2):
                            n = 0
                            for j in range(len(grp)):
                                for dffc in range(4):
                                    n += 1
                                    mm = t.matmul(PS[bk + ts2][:], wdb[j][:, dffc, hc * 128:(hc + 1) * 128],
                                                  hb[j][:, dffc, ts2 * 512:(ts2 + 1) * 512],
                                                  start=(n == 1), stop=(n == nmm))
                        mm.then_inc(sPE.h, 1)
                    on("tensor", _)
                    sPE.n += 1
                    down_done = sPE.n

                    if gi == 0:
                        def _(sc, hc=hc, bk=bk, down_done=down_done):
                            sc.wait_ge(sPE.h, down_done)
                            for ts2 in range(2):
                                cp = sc.copy(acc[:, hc, ts2 * 512:(ts2 + 1) * 512], PS[bk + ts2][:])
                            cp.then_inc(sAC.h, 1)
                        on("scalar", _)
                        sAC.n += 1
                        down_evac[par] = (sAC.n, down_evac[par][1])
                    else:
                        def _(v, hc=hc, bk=bk, down_done=down_done):
                            v.wait_ge(sPE.h, down_done)
                            for ts2 in range(2):
                                asl = acc[:, hc, ts2 * 512:(ts2 + 1) * 512]
                                mm = v.tensor_tensor(asl, asl, PS[bk + ts2][:], ALU.add)
                            mm.then_inc(sDV.h, 1)
                        on("vector", _)
                        sDV.n += 1
                        down_evac[par] = (down_evac[par][0], sDV.n)
                for j in range(len(grp)):
                    wd_free[j] = sPE.n
                gu_evac[0] = (max(gu_evac[0][0], down_evac[0][0], down_evac[1][0]),
                              max(gu_evac[0][1], down_evac[0][1], down_evac[1][1]))
                nc.leave_named_scope("ffn_h%dg%d_dn" % (hf, gi), _sgd, False)
            DOWN_PE = sPE.n
            DOWN_DV = sDV.n
            DOWN_AC = sAC.n

            if stage == "ffn0":
                def _(g, a=DOWN_PE, b=DOWN_DV, c=DOWN_AC):
                    g.wait_ge(sPE.h, a)
                    g.wait_ge(sDV.h, b)
                    g.wait_ge(sAC.h, c)
                    g.dma_start(out=dbg["x2ln"], in_=x2ln.rearrange("p a b -> p (a b)")).then_inc(s_dbg.h, 16)
                    g.dma_start(out=dbg["coef"], in_=coef_fm).then_inc(s_dbg.h, 16)
                    g.dma_start(out=dbg["acc"], in_=acc.rearrange("p a b -> p (a b)")).then_inc(s_dbg.h, 16)
                    g.dma_start(out=dbg["h0"], in_=hb[0].rearrange("p a b -> p (a b)")).then_inc(s_dbg.h, 16)
                    g.dma_start(out=dbg["crep0"], in_=crep[0]).then_inc(s_dbg.h, 16)
                on("gpsimd", _)
                s_dbg.n += 16 * 5

                def _(sy, w=s_dbg.n):
                    sy.wait_ge(s_dbg.h, w)
                on("sync", _)
                return nc

            _sl4 = nc.enter_named_scope("ffn_h%d_ln4" % hf, False)[0]
            ln4_stats_pe = 0
            for ts in range(2):
                def _(v, ts=ts, DOWN_DV=DOWN_DV, DOWN_PE=DOWN_PE, DOWN_AC=DOWN_AC,
                      ln4_stats_pe=ln4_stats_pe):
                    v.wait_ge(sDV.h, DOWN_DV)
                    v.wait_ge(sAC.h, DOWN_AC)
                    v.wait_ge(sPE.h, DOWN_PE if ts == 0 else ln4_stats_pe)
                    for dc in range(DC):
                        v.tensor_copy(castb[:, dc, :], acc[:, dc, ts * 512:(ts + 1) * 512])
                        mm = v.tensor_tensor(sqb[:, dc, :], castb[:, dc, :], castb[:, dc, :], ALU.mult)
                    mm.then_inc(sDV.h, 1)
                on("vector", _)
                sDV.n += 1
                sq4_done = sDV.n

                def _(t, sq4_done=sq4_done):
                    t.wait_ge(sDV.h, sq4_done)
                    for dc in range(DC):
                        t.matmul(PS[6][:], ones_bf, castb[:, dc, :], start=(dc == 0), stop=(dc == DC - 1))
                    for dc in range(DC):
                        mm = t.matmul(PS[7][:], ones_bf, sqb[:, dc, :], start=(dc == 0), stop=(dc == DC - 1))
                    mm.then_inc(sPE.h, 1)
                on("tensor", _)
                sPE.n += 1
                ln4_stats_pe = sPE.n

                def _(v, done=ln4_stats_pe):
                    v.wait_ge(sPE.h, done)
                    v.tensor_scalar(f_m[:], PS[6][:], 1.0 / D, None, ALU.mult)
                    v.tensor_scalar(f_ms[:], PS[7][:], 1.0 / D, None, ALU.mult)
                    v.tensor_tensor(f_r[:], f_m[:], f_m[:], ALU.mult)
                    v.tensor_tensor(f_ms[:], f_ms[:], f_r[:], ALU.subtract).then_inc(sDV.h, 1)
                on("vector", _)
                sDV.n += 1
                var4 = sDV.n

                def _(sc, var4=var4):
                    sc.wait_ge(sDV.h, var4)
                    sc.activation(f_ms[:], f_ms[:], AF.Ln, bias=eps5[:])
                    sc.activation(f_r[:], f_ms[:], AF.Exp, scale=-0.5).then_inc(sAC.h, 1)
                on("scalar", _)
                sAC.n += 1
                rstd4 = sAC.n

                def _(v, ts=ts, rstd4=rstd4):
                    v.wait_ge(sAC.h, rstd4)
                    for dc in range(DC):
                        asl = acc[:, dc, ts * 512:(ts + 1) * 512]
                        xsl = x2sl[:, dc, ts * 512:(ts + 1) * 512]
                        v.tensor_tensor(f_t2[:], asl, f_m[:], ALU.subtract)
                        v.tensor_tensor(f_t2[:], f_t2[:], f_r[:], ALU.mult)
                        mm = v.tensor_tensor(xsl, xsl, f_t2[:], ALU.add)
                    mm.then_inc(sDV.h, 1)
                on("vector", _)
                sDV.n += 1
            OUT_DONE = sDV.n

            def _(sy, hf=hf, OUT_DONE=OUT_DONE):
                sy.wait_ge(sDV.h, OUT_DONE)
                sy.dma_start(out=y_r[:, :, hf * TH:(hf + 1) * TH], in_=x2sl).then_inc(s_y.h, 16)
            on("sync", _)
            s_y.n += 16
            nc.leave_named_scope("ffn_h%d_ln4" % hf, _sl4, False)

        def _(sy, w=s_y.n):
            sy.wait_ge(s_y.h, w)
        on("sync", _)

    return nc


_NC_CACHE = {}


def _get_nc(stage):
    if stage not in _NC_CACHE:
        _NC_CACHE[stage] = build(stage)
    return _NC_CACHE[stage]


def make_in_maps(L0_attn_x, L0_attn_W_qkv, L0_ffn_router, L0_ffn_experts_gate,
                 L0_ffn_experts_up, L0_ffn_experts_down, L0_ffn_shared_gate,
                 L0_ffn_shared_up, L0_ffn_shared_down):
    x = np.asarray(L0_attn_x, np.float32)
    sel8 = np.zeros((32, 1024), dtype=np.float32)
    for e in range(8):
        sel8[e, e * 128:(e + 1) * 128] = 1.0
    shared = dict(
        wqkvT=np.ascontiguousarray(np.asarray(L0_attn_W_qkv, np.float32).T),
        router=np.ascontiguousarray(np.asarray(L0_ffn_router, np.float32)),
        eg=np.ascontiguousarray(np.asarray(L0_ffn_experts_gate, np.float32)),
        eu=np.ascontiguousarray(np.asarray(L0_ffn_experts_up, np.float32)),
        ed=np.ascontiguousarray(np.asarray(L0_ffn_experts_down, np.float32)),
        sg=np.ascontiguousarray(np.asarray(L0_ffn_shared_gate, np.float32)),
        su=np.ascontiguousarray(np.asarray(L0_ffn_shared_up, np.float32)),
        sd=np.ascontiguousarray(np.asarray(L0_ffn_shared_down, np.float32)),
        eye=np.eye(128, dtype=np.float32),
        sel8=sel8,
    )
    in_maps = []
    for c in range(8):
        b, h = c // 2, c % 2
        xb = x[b]
        if h:
            xb = np.concatenate([xb[2048:], xb[:2048]], axis=0)
        m = dict(shared)
        m["xb"] = np.ascontiguousarray(xb.T)
        in_maps.append(m)
    return in_maps


def kernel(**inputs):
    in_maps = make_in_maps(**inputs)
    nc = _get_nc("full")
    res = run_bass_kernel_spmd(nc, in_maps, core_ids=list(range(8)))
    out = np.empty((4, 4096, 1024), np.float32)
    for c in range(8):
        b, h = c // 2, c % 2
        out[b, h * 2048:(h + 1) * 2048, :] = np.asarray(res.results[c]["y"]).T
    return out


# version 11
# speedup vs baseline: 1.0470x; 1.0026x over previous
import numpy as np
from contextlib import ExitStack

import concourse.bass as bass
import concourse.mybir as mybir
from concourse.bass_utils import run_bass_kernel_spmd

F32 = mybir.dt.float32
BF16 = mybir.dt.bfloat16
AF = mybir.ActivationFunctionType
ALU = mybir.AluOpType
AX = mybir.AxisListType

KB = 1024
D = 1024
DC = 8
TALL = 4096
TOWN = 2048
NQB = 4
NKC = 32
TH = 1024

O_ONES = 0
O_EYE = 256
O_ROUT = 512
O_EPS5 = 640
O_EPS35 = 648
O_WQ = 1 * KB
O_XLN = 49 * KB
O_SPILL = [177 * KB + 512, 181 * KB + 512]
O_KSQ = 113 * KB
O_XS = [121 * KB, 137 * KB]
O_U = 153 * KB
O_ASTATS = 161 * KB
O_KMAX = 177 * KB
O_QSQ = 185 * KB + 512
O_LNQ = 189 * KB + 512
O_MREP = 177 * KB + 512
O_PT = 1 * KB
O_KST = [33 * KB, 41 * KB]
O_QST = [49 * KB, 57 * KB]
O_XOWN = 65 * KB
O_WB = 81 * KB
O_UW = 89 * KB
O_STATS = 97 * KB
O_NEGM = 107 * KB
O_VBF = 113 * KB
O_X2SL = 1 * KB
O_X2LN = 33 * KB
O_ACC = 49 * KB
O_WG = [186 * KB, 81 * KB]
O_WU = [89 * KB, 97 * KB]
O_FM = 105 * KB
O_FMS = 107 * KB
O_FR = 109 * KB
O_FT2 = 111 * KB
O_FT = 113 * KB
O_COEF = 114 * KB
O_SEL = 204 * KB
O_CREP = 118 * KB
O_H = 122 * KB
O_WD = 154 * KB
O_UWT = 194 * KB
ARENA = 206 * KB


class C:

    def __init__(self, stack, nc, name):
        self.h = stack.enter_context(nc.semaphore(name))
        self.n = 0
        self.name = name


def build(stage="full"):
    nc = bass.Bass("TRN2", target_bir_lowering=False)

    xb = nc.dram_tensor("xb", [D, TALL], F32, kind="ExternalInput").ap()
    wqkvT = nc.dram_tensor("wqkvT", [D, 3 * D], F32, kind="ExternalInput").ap()
    router = nc.dram_tensor("router", [D, 8], F32, kind="ExternalInput").ap()
    eg = nc.dram_tensor("eg", [8, D, 512], F32, kind="ExternalInput").ap()
    eu = nc.dram_tensor("eu", [8, D, 512], F32, kind="ExternalInput").ap()
    ed = nc.dram_tensor("ed", [8, 512, D], F32, kind="ExternalInput").ap()
    sg = nc.dram_tensor("sg", [D, D], F32, kind="ExternalInput").ap()
    su = nc.dram_tensor("su", [D, D], F32, kind="ExternalInput").ap()
    sd = nc.dram_tensor("sd", [D, D], F32, kind="ExternalInput").ap()
    eye_in = nc.dram_tensor("eye", [128, 128], F32, kind="ExternalInput").ap()
    sel_in = nc.dram_tensor("sel8", [32, 1024], F32, kind="ExternalInput").ap()

    k_hbm = nc.dram_tensor("k_hbm", [DC, 128, TALL], BF16, kind="Internal").ap()
    q_hbm = nc.dram_tensor("q_hbm", [DC, 128, TOWN], BF16, kind="Internal").ap()
    x2_hbm = nc.dram_tensor("x2_hbm", [DC, 128, TOWN], F32, kind="Internal").ap()
    mr_hbm = nc.dram_tensor("mr_hbm", [128, 4096], F32, kind="Internal").ap()

    y = nc.dram_tensor("y", [D, TOWN], F32, kind="ExternalOutput").ap()
    y_r = y.rearrange("(c p) t -> p c t", p=128)

    dbg = {}
    if stage == "ln1":
        dbg["xln"] = nc.dram_tensor("dbg_xln", [D, TALL], F32, kind="ExternalOutput").ap()
    if stage == "qkv":
        dbg["k"] = nc.dram_tensor("dbg_k", [DC, 128, TALL], F32, kind="ExternalOutput").ap()
        dbg["q"] = nc.dram_tensor("dbg_q", [DC, 128, TOWN], F32, kind="ExternalOutput").ap()
        dbg["v"] = nc.dram_tensor("dbg_v", [TALL, D], F32, kind="ExternalOutput").ap()
        dbg["negm"] = nc.dram_tensor("dbg_negm", [128, TOWN], F32, kind="ExternalOutput").ap()
    if stage == "attn":
        dbg["x2"] = nc.dram_tensor("dbg_x2", [DC, 128, TOWN], F32, kind="ExternalOutput").ap()
    if stage == "ffn0":
        dbg["x2ln"] = nc.dram_tensor("dbg_x2ln", [128, DC * TH], F32, kind="ExternalOutput").ap()
        dbg["coef"] = nc.dram_tensor("dbg_coef", [128, 1024], F32, kind="ExternalOutput").ap()
        dbg["acc"] = nc.dram_tensor("dbg_acc", [128, DC * TH], F32, kind="ExternalOutput").ap()
        dbg["h0"] = nc.dram_tensor("dbg_h0", [128, 4 * TH], F32, kind="ExternalOutput").ap()
        dbg["crep0"] = nc.dram_tensor("dbg_crep0", [128, TH], F32, kind="ExternalOutput").ap()
    if stage.startswith("qb"):
        for nm, shp in [("pt", [128, NKC * 512]), ("wb", [128, DC * 512]),
                        ("eps", [128, 512]), ("sm", [128, 512]), ("sms", [128, 512]),
                        ("sr", [128, 512]), ("xo", [128, DC * 512])]:
            dbg[nm] = nc.dram_tensor("dbg_" + nm, shp, F32, kind="ExternalOutput").ap()

    with ExitStack() as stack:
        arena = stack.enter_context(nc.sbuf_tensor("arena", [128, ARENA], mybir.dt.uint8))

        def carve(off, dtype, shape):
            esz = 2 if dtype == BF16 else 4
            nfree = 1
            for s in shape[1:]:
                nfree *= s
            v = arena[0:shape[0], off:off + nfree * esz].bitcast(dtype)
            if len(shape) == 2:
                return v
            return v.rearrange("p (a b) -> p a b", a=shape[1], b=shape[2])

        ones_bf = carve(O_ONES, BF16, [128, 128])
        eye_bf = carve(O_EYE, BF16, [128, 128])
        rout_b = carve(O_ROUT, BF16, [128, 8, 8])
        eps5 = carve(O_EPS5, F32, [128, 1])
        eps35 = carve(O_EPS35, F32, [128, 1])
        wq = carve(O_WQ, BF16, [128, DC, 3 * D])
        xln = carve(O_XLN, BF16, [128, DC, TALL])
        spill = [carve(O_SPILL[i], BF16, [128, 4, 512]) for i in range(2)]
        spill_f = [carve(O_SPILL[i], BF16, [128, 2048]) for i in range(2)]
        ksq = carve(O_KSQ, BF16, [128, 2048])
        xs = [carve(O_XS[i], F32, [128, DC, 512]) for i in range(2)]
        ubuf = carve(O_U, BF16, [128, DC, 512])
        a_m = carve(O_ASTATS, F32, [128, 512])
        a_ms = carve(O_ASTATS + 2 * KB, F32, [128, 512])
        a_t = carve(O_ASTATS + 4 * KB, F32, [128, 512])
        a_r = carve(O_ASTATS + 6 * KB, F32, [128, 512])
        a_rb = carve(O_ASTATS + 8 * KB, BF16, [128, 512])
        a_bb = carve(O_ASTATS + 9 * KB, BF16, [128, 512])
        kmax = carve(O_KMAX, F32, [128, 1])
        klog = carve(O_KMAX + 128, F32, [128, 1])
        qsq = carve(O_QSQ, BF16, [128, 2048])
        lnq = carve(O_LNQ, F32, [128, 2048])
        mrep = carve(O_MREP, F32, [128, 2048])
        negm = carve(O_NEGM, BF16, [128, 2048])
        mr3 = [carve(177 * KB + 512 + 4 * KB * i, F32, [128, 2, 512]) for i in range(4)]
        mr3all = carve(177 * KB + 512, F32, [128, 4096])
        mrh = [carve(200 * KB + 2 * KB * i, BF16, [128, 2, 512]) for i in range(2)]
        pt = carve(O_PT, BF16, [128, NKC, 512])
        kst = [carve(O_KST[i], BF16, [128, DC, 512]) for i in range(2)]
        qst = [carve(O_QST[i], BF16, [128, DC, 512]) for i in range(2)]
        xown = carve(O_XOWN, F32, [128, DC, 512])
        w_b = carve(O_WB, BF16, [128, DC, 512])
        uw = carve(O_UW, BF16, [128, DC, 512])
        s_m = carve(O_STATS, F32, [128, 512])
        s_ms = carve(O_STATS + 2 * KB, F32, [128, 512])
        s_eps = carve(O_STATS + 4 * KB, F32, [128, 512])
        s_r = carve(O_STATS + 6 * KB, F32, [128, 512])
        s_mb = carve(O_STATS + 8 * KB, BF16, [128, 512])
        s_rb = carve(O_STATS + 9 * KB, BF16, [128, 512])
        vbf = carve(O_VBF, BF16, [128, NKC, 1024])
        x2sl = carve(O_X2SL, F32, [128, DC, TH])
        x2ln = carve(O_X2LN, BF16, [128, DC, TH])
        acc = carve(O_ACC, F32, [128, DC, TH])
        wgb = [carve(O_WG[i], BF16, [128, DC, 512]) for i in range(2)]
        wub = [carve(O_WU[i], BF16, [128, DC, 512]) for i in range(2)]
        f_m = carve(O_FM, F32, [128, 512])
        f_ms = carve(O_FMS, F32, [128, 512])
        f_r = carve(O_FR, F32, [128, 512])
        f_t2 = carve(O_FT2, F32, [128, 512])
        f_t = carve(O_FT, F32, [128, 16])
        coef_tma = carve(O_FT + 64, BF16, [128, 64])
        lmax = carve(O_FT + 192, F32, [128, 1])
        lsum = carve(O_FT + 224, F32, [128, 1])
        lrec = carve(O_FT + 256, F32, [128, 1])
        coef_fm = carve(O_COEF, BF16, [128, 1024])
        sel_b = carve(O_SEL, BF16, [32, 8, 128])
        crep = [carve(O_CREP + 2 * KB * i, BF16, [128, 1024]) for i in range(2)]
        crep += [carve(196 * KB + 2 * KB * i, BF16, [128, 1024]) for i in range(2)]
        hb = [carve(O_H + 8 * KB * i, BF16, [128, 4, 1024]) for i in range(4)]
        wdb = [carve(O_WD + 8 * KB * i, BF16, [128, 4, 1024]) for i in range(4)]
        sqb = carve(O_H, BF16, [128, DC, 512])
        castb = carve(O_H + 8 * KB, BF16, [128, DC, 512])
        uwtmp = carve(O_UWT, BF16, [128, 2, 512])

        PS = [stack.enter_context(nc.psum_tensor("ps%d" % i, [128, 512], F32)) for i in range(8)]

        sPE = C(stack, nc, "sPE")
        sDV = C(stack, nc, "sDV")
        sAC = C(stack, nc, "sAC")
        sPL = C(stack, nc, "sPL")
        s_wdma = C(stack, nc, "s_wdma")
        s_xs = [C(stack, nc, "s_xs0"), C(stack, nc, "s_xs1")]
        s_ksp = C(stack, nc, "s_ksp")
        s_qsp = C(stack, nc, "s_qsp")
        s_kst = [C(stack, nc, "s_kst0"), C(stack, nc, "s_kst1")]
        s_qst = [C(stack, nc, "s_qst0"), C(stack, nc, "s_qst1")]
        s_xown = C(stack, nc, "s_xown")
        s_x2w = C(stack, nc, "s_x2w")
        s_x2r = C(stack, nc, "s_x2r")
        s_wg = [C(stack, nc, "s_wg0"), C(stack, nc, "s_wg1")]
        s_wu = [C(stack, nc, "s_wu0"), C(stack, nc, "s_wu1")]
        s_wd = [C(stack, nc, "s_wd%d" % i) for i in range(4)]
        s_y = C(stack, nc, "s_y")
        s_mr = C(stack, nc, "s_mr")
        s_dbg = C(stack, nc, "s_dbg")

        block = stack.enter_context(nc.Block())

        def on(name, f):
            getattr(block, name)(f)

        def _(g):
            g.memset(ones_bf, 1.0)
            g.memset(eps5, 1e-5)
            g.memset(eps35, 1e-35)
            g.engine_nop().then_inc(sPL.h, 1)
            g.dma_start(out=eye_bf, in_=eye_in).then_inc(s_wdma.h, 16)
            g.dma_start(out=rout_b, in_=router.rearrange("(c p) e -> p c e", p=128)).then_inc(s_wdma.h, 16)
            g.dma_start(out=sel_b, in_=sel_in.rearrange("p (a b) -> p a b", a=8, b=128)).then_inc(s_wdma.h, 16)
            for c in range(DC):
                g.dma_start(out=wq[:, c, :], in_=wqkvT[c * 128:(c + 1) * 128, :]).then_inc(s_wdma.h, 16)
        on("gpsimd", _)
        sPL.n += 1
        s_wdma.n += 16 * (3 + DC)
        WQ_READY = s_wdma.n
        ONES_READY = sPL.n

        _sid = nc.enter_named_scope("phA", False)[0]
        xb_r = xb.rearrange("(c p) t -> p c t", p=128)
        apply_done = [0] * 8
        prev_stats = [0]

        for s in range(8):
            p = s % 2

            def _(sy, s=s, p=p):
                if s >= 2:
                    sy.wait_ge(sDV.h, apply_done[s - 2])
                sy.dma_start(out=xs[p], in_=xb_r[:, :, s * 512:(s + 1) * 512]).then_inc(s_xs[p].h, 16)
            on("sync", _)
            s_xs[p].n += 16

            def _(v, s=s, p=p, w=s_xs[p].n):
                v.wait_ge(s_xs[p].h, w)
                if s == 0:
                    v.wait_ge(sPL.h, ONES_READY)
                xls = xln[:, :, s * 512:(s + 1) * 512]
                v.tensor_copy(xls, xs[p]).then_inc(sDV.h, 1)
            on("vector", _)
            sDV.n += 1
            cast_done = sDV.n

            def _(sc, s=s, cast_done=cast_done, pstats=prev_stats[0]):
                sc.wait_ge(sDV.h, cast_done)
                if pstats:
                    sc.wait_ge(sPE.h, pstats)
                xls = xln[:, :, s * 512:(s + 1) * 512]
                sc.square(ubuf[:], xls).then_inc(sAC.h, 1)
            on("scalar", _)
            sAC.n += 1
            sq_done = sAC.n

            def _(t, s=s, cast_done=cast_done, sq_done=sq_done):
                t.wait_ge(sDV.h, cast_done)
                xls = xln[:, :, s * 512:(s + 1) * 512]
                for dc in range(DC):
                    t.matmul(PS[0][:], ones_bf, xls[:, dc, :], start=(dc == 0), stop=(dc == DC - 1))
                t.wait_ge(sAC.h, sq_done)
                for dc in range(DC):
                    mm = t.matmul(PS[1][:], ones_bf, ubuf[:, dc, :], start=(dc == 0), stop=(dc == DC - 1))
                mm.then_inc(sPE.h, 1)
            on("tensor", _)
            sPE.n += 1
            stats_done = sPE.n
            prev_stats[0] = stats_done

            def _(v, stats_done=stats_done):
                v.wait_ge(sPE.h, stats_done)
                v.tensor_scalar(a_m[:], PS[0][:], 1.0 / D, None, ALU.mult)
                v.tensor_scalar(a_ms[:], PS[1][:], 1.0 / D, None, ALU.mult)
                v.tensor_tensor(a_t[:], a_m[:], a_m[:], ALU.mult)
                v.tensor_tensor(a_ms[:], a_ms[:], a_t[:], ALU.subtract).then_inc(sDV.h, 1)
            on("vector", _)
            sDV.n += 1
            var_done = sDV.n

            def _(sc, var_done=var_done):
                sc.wait_ge(sDV.h, var_done)
                if var_done <= 3:
                    sc.wait_ge(sPL.h, 1)
                sc.activation(a_t[:], a_ms[:], AF.Ln, bias=eps5[:])
                sc.activation(a_r[:], a_t[:], AF.Exp, scale=-0.5).then_inc(sAC.h, 1)
            on("scalar", _)
            sAC.n += 1
            rstd_done = sAC.n

            def _(v, s=s, rstd_done=rstd_done):
                v.wait_ge(sAC.h, rstd_done)
                v.tensor_copy(a_rb[:], a_r[:])
                v.tensor_tensor(a_t[:], a_m[:], a_r[:], ALU.mult)
                v.tensor_scalar(a_bb[:], a_t[:], -1.0, None, ALU.mult)
                xls = xln[:, :, s * 512:(s + 1) * 512]
                for dc in range(DC):
                    v.tensor_tensor(xls[:, dc, :], xls[:, dc, :], a_rb[:], ALU.mult)
                    mm = v.tensor_tensor(xls[:, dc, :], xls[:, dc, :], a_bb[:], ALU.add)
                mm.then_inc(sDV.h, 1)
            on("vector", _)
            sDV.n += 1
            apply_done[s] = sDV.n

        nc.leave_named_scope("phA", _sid, False)
        LN1_DONE = apply_done[7]

        if stage == "ln1":
            def _(g):
                g.wait_ge(sDV.h, LN1_DONE)
                g.dma_start(out=dbg["xln"].rearrange("(c p) t -> p c t", p=128),
                            in_=xln[:]).then_inc(s_dbg.h, 16)
            on("gpsimd", _)
            s_dbg.n += 16

            def _(sy):
                sy.wait_ge(s_dbg.h, s_dbg.n)
            on("sync", _)
            return nc

        _sid = nc.enter_named_scope("phK", False)[0]
        kss_free_dv = 0
        kq_evac_pair = [0, 0]
        pending_kss = [None]
        for kg in range(2):
            for fc in range(DC):
                b = fc % 2
                for half in range(2):
                    def _(t, kg=kg, fc=fc, half=half, ge=kq_evac_pair[half]):
                        if kg == 0 and fc == 0 and half == 0:
                            t.wait_ge(s_wdma.h, WQ_READY)
                            t.wait_ge(sDV.h, LN1_DONE)
                        if ge:
                            t.wait_ge(sAC.h, ge)
                        for dc in range(DC):
                            for k2 in range(2):
                                ks = half * 2 + k2
                                mm = t.matmul(PS[ks][:], wq[:, dc, D + fc * 128:D + (fc + 1) * 128],
                                              xln[:, dc, kg * 2048 + ks * 512:kg * 2048 + (ks + 1) * 512],
                                              start=(dc == 0), stop=(dc == DC - 1))
                        mm.then_inc(sPE.h, 1)
                    on("tensor", _)
                    sPE.n += 1
                    prod_done = sPE.n
                    if half == 0 and pending_kss[0] is not None:
                        pending_kss[0]()
                        pending_kss[0] = None

                    def _(sc, b=b, half=half, prod_done=prod_done,
                          ws=(s_ksp.n if b == 0 else s_qsp.n)):
                        sc.wait_ge(sPE.h, prod_done)
                        if half == 0:
                            sc.wait_ge((s_ksp if b == 0 else s_qsp).h, ws)
                        for k2 in range(2):
                            ks = half * 2 + k2
                            cp = sc.copy(spill[b][:, ks, :], PS[ks][:])
                        cp.then_inc(sAC.h, 1)
                    on("scalar", _)
                    sAC.n += 1
                    kq_evac_pair[half] = sAC.n
                evac_done = sAC.n

                def _(sy, kg=kg, fc=fc, b=b, evac_done=evac_done):
                    sy.wait_ge(sAC.h, evac_done)
                    sy.wait_ge(sDV.h, sDV.n)
                    sy.dma_start(out=k_hbm[fc, :, kg * 2048:(kg + 1) * 2048],
                                 in_=spill_f[b]).then_inc((s_ksp if b == 0 else s_qsp).h, 16)
                on("sync", _)
                if b == 0:
                    s_ksp.n += 16
                else:
                    s_qsp.n += 16

                def _(v, b=b, evac_done=evac_done):
                    v.wait_ge(sAC.h, evac_done)
                    v.wait_ge(sPE.h, sPE.n)
                    v.tensor_tensor(ksq[:], spill_f[b], spill_f[b], ALU.mult).then_inc(sDV.h, 1)
                on("vector", _)
                sDV.n += 1
                ksq_done = sDV.n

                def emit_kss(fc=fc, ksq_done=ksq_done, kss_free_dv=kss_free_dv):
                    def _(t):
                        t.wait_ge(sDV.h, ksq_done)
                        if fc == 0:
                            t.wait_ge(sDV.h, kss_free_dv)
                        for ks in range(4):
                            mm = t.matmul(PS[4 + ks][:], ones_bf, ksq[:, ks * 512:(ks + 1) * 512],
                                          start=(fc == 0), stop=(fc == DC - 1))
                        mm.then_inc(sPE.h, 1)
                    on("tensor", _)
                    sPE.n += 1
                pending_kss[0] = emit_kss

            pending_kss[0]()
            pending_kss[0] = None
            kss_mm_done = sPE.n

            def _(v, kg=kg, kss_mm_done=kss_mm_done, base=sDV.n):
                v.wait_ge(sPE.h, kss_mm_done)
                n = base
                for ks in range(4):
                    v.reduce_max(a_m[:, 0:1], PS[4 + ks][:], axis=AX.X).then_inc(sDV.h, 1)
                    n += 1
                    v.wait_ge(sDV.h, n)
                    if kg == 0 and ks == 0:
                        op = v.tensor_copy(kmax[:], a_m[:, 0:1])
                    else:
                        op = v.tensor_tensor(kmax[:], kmax[:], a_m[:, 0:1], ALU.max)
                    op.then_inc(sDV.h, 1)
                    n += 1
                    v.wait_ge(sDV.h, n)
            on("vector", _)
            sDV.n += 8
            kss_free_dv = sDV.n

        nc.leave_named_scope("phK", _sid, False)
        _sid = nc.enter_named_scope("phV", False)[0]
        v_evac = [0, 0]
        for tt in range(NKC):
            bk = (tt % 2) * 2
            def _(t, tt=tt, bk=bk, ge=v_evac[tt % 2]):
                if ge:
                    t.wait_ge(sAC.h, ge)
                if tt < 2:
                    t.wait_ge(sAC.h, sAC.n)
                    t.wait_ge(sDV.h, sDV.n)
                for dc in range(DC):
                    for fs in range(2):
                        mm = t.matmul(PS[bk + fs][:], xln[:, dc, tt * 128:(tt + 1) * 128],
                                      wq[:, dc, 2 * D + fs * 512:2 * D + (fs + 1) * 512],
                                      start=(dc == 0), stop=(dc == DC - 1))
                mm.then_inc(sPE.h, 1)
            on("tensor", _)
            sPE.n += 1
            vprod = sPE.n

            def _(sc, tt=tt, bk=bk, vprod=vprod, wk=s_ksp.n):
                sc.wait_ge(sPE.h, vprod)
                if tt < 2:
                    sc.wait_ge(s_ksp.h, wk)
                for fs in range(2):
                    cp = sc.copy(vbf[:, tt, fs * 512:(fs + 1) * 512], PS[bk + fs][:])
                cp.then_inc(sAC.h, 1)
            on("scalar", _)
            sAC.n += 1
            v_evac[tt % 2] = sAC.n

        nc.leave_named_scope("phV", _sid, False)
        V_DONE_AC = sAC.n

        _sid = nc.enter_named_scope("phQ", False)[0]
        for fc in range(DC):
            b = fc % 2
            for half in range(2):
                def _(t, fc=fc, half=half, ge=kq_evac_pair[half]):
                    if fc == 0 and half == 0:
                        t.wait_ge(sAC.h, sAC.n)
                    if ge:
                        t.wait_ge(sAC.h, ge)
                    if half == 0 and fc > 0:
                        t.wait_ge(sDV.h, sDV.n)
                    for dc in range(DC):
                        for k2 in range(2):
                            qs = half * 2 + k2
                            mm = t.matmul(PS[qs][:], wq[:, dc, fc * 128:(fc + 1) * 128],
                                          xln[:, dc, qs * 512:(qs + 1) * 512],
                                          start=(dc == 0), stop=(dc == DC - 1))
                    mm.then_inc(sPE.h, 1)
                on("tensor", _)
                sPE.n += 1
                prod_done = sPE.n
                if half == 0 and pending_kss[0] is not None:
                    pending_kss[0]()
                    pending_kss[0] = None

                def _(sc, b=b, half=half, prod_done=prod_done,
                      ws=(s_ksp.n if b == 0 else s_qsp.n)):
                    sc.wait_ge(sPE.h, prod_done)
                    if half == 0:
                        sc.wait_ge((s_ksp if b == 0 else s_qsp).h, ws)
                    for k2 in range(2):
                        qs = half * 2 + k2
                        cp = sc.activation(spill[b][:, qs, :], PS[qs][:], AF.Copy, scale=0.03125)
                    cp.then_inc(sAC.h, 1)
                on("scalar", _)
                sAC.n += 1
                kq_evac_pair[half] = sAC.n
            evac_done = sAC.n

            def _(sy, fc=fc, b=b, evac_done=evac_done):
                sy.wait_ge(sAC.h, evac_done)
                sy.wait_ge(sDV.h, sDV.n)
                sy.dma_start(out=q_hbm[fc, :, :], in_=spill_f[b]).then_inc((s_ksp if b == 0 else s_qsp).h, 16)
            on("sync", _)
            if b == 0:
                s_ksp.n += 16
            else:
                s_qsp.n += 16

            def _(v, b=b, evac_done=evac_done):
                v.wait_ge(sAC.h, evac_done)
                v.wait_ge(sPE.h, sPE.n)
                v.tensor_tensor(qsq[:], spill_f[b], spill_f[b], ALU.mult).then_inc(sDV.h, 1)
            on("vector", _)
            sDV.n += 1
            qsq_done = sDV.n

            def emit_qss(fc=fc, qsq_done=qsq_done):
                def _(t):
                    t.wait_ge(sDV.h, qsq_done)
                    for qs in range(4):
                        mm = t.matmul(PS[4 + qs][:], ones_bf, qsq[:, qs * 512:(qs + 1) * 512],
                                      start=(fc == 0), stop=(fc == DC - 1))
                    mm.then_inc(sPE.h, 1)
                on("tensor", _)
                sPE.n += 1
            pending_kss[0] = emit_qss

        pending_kss[0]()
        pending_kss[0] = None
        Q_DONE_PE = sPE.n

        nc.leave_named_scope("phQ", _sid, False)
        _sid = nc.enter_named_scope("phNegm", False)[0]
        def _(sc, w0=s_ksp.n, w1=s_qsp.n):
            sc.wait_ge(sPE.h, Q_DONE_PE)
            sc.wait_ge(s_ksp.h, w0)
            sc.wait_ge(s_qsp.h, w1)
            for qs in range(4):
                sc.activation(lnq[:, qs * 512:(qs + 1) * 512], PS[4 + qs][:], AF.Ln)
            sc.activation(klog[:], kmax[:], AF.Ln).then_inc(sAC.h, 1)
        on("scalar", _)
        sAC.n += 1
        lnq_done = sAC.n

        def _(v, lnq_done=lnq_done):
            v.wait_ge(sAC.h, lnq_done)
            v.tensor_scalar(klog[:], klog[:], 0.5, None, ALU.mult)
            v.engine_nop().then_inc(sDV.h, 1)
        on("vector", _)
        sDV.n += 1
        klog_done = sDV.n

        def _(sc, klog_done=klog_done):
            sc.wait_ge(sDV.h, klog_done)
            sc.activation(mrep[:], lnq[:], AF.Exp, scale=0.5, bias=klog[:]).then_inc(sAC.h, 1)
        on("scalar", _)
        sAC.n += 1
        m_done = sAC.n

        def _(v, m_done=m_done):
            v.wait_ge(sAC.h, m_done)
            v.tensor_scalar(negm[:], mrep[:], -1.0, 19.0, ALU.mult, ALU.add)
            v.engine_nop().then_inc(sDV.h, 1)
        on("vector", _)
        sDV.n += 1
        NEGM_DONE = sDV.n

        nc.leave_named_scope("phNegm", _sid, False)
        if stage == "qkv":
            def _(g, wk=s_ksp.n, wqq=s_qsp.n):
                g.wait_ge(sDV.h, NEGM_DONE)
                g.wait_ge(s_ksp.h, wk)
                g.wait_ge(s_qsp.h, wqq)
                g.dma_start(out=dbg["k"], in_=k_hbm).then_inc(s_dbg.h, 16)
                g.dma_start(out=dbg["q"], in_=q_hbm).then_inc(s_dbg.h, 16)
                g.dma_start(out=dbg["v"].rearrange("(a p) f -> p a f", p=128), in_=vbf[:]).then_inc(s_dbg.h, 16)
                g.dma_start(out=dbg["negm"], in_=negm[:]).then_inc(s_dbg.h, 16)
            on("gpsimd", _)
            s_dbg.n += 64

            def _(sy):
                sy.wait_ge(s_dbg.h, s_dbg.n)
            on("sync", _)
            return nc

        K_SPILLED = s_ksp.n
        Q_SPILLED = s_qsp.n
        kst_pe_free = [0, 0]
        qst_pe_free = [0, 0]
        pt_pe_free = 0
        wb_dv_free = 0
        x2b_cast_prev = [0]
        bank_evac = [0, 0, 0]
        ST_BANKS = [0, 1, 7]
        pv_bank_evac = [0, 0]

        for qb in range(NQB):
            _sqb = nc.enter_named_scope("qb%d" % qb, False)[0]
            def _(sy, qb=qb, w=qst_pe_free[qb % 2]):
                if qb == 0:
                    sy.wait_ge(sDV.h, NEGM_DONE)
                    sy.wait_ge(s_ksp.h, K_SPILLED)
                    sy.wait_ge(s_qsp.h, Q_SPILLED)
                sy.wait_ge(sPE.h, w)
                sy.dma_start(out=qst[qb % 2],
                             in_=q_hbm[:, :, qb * 512:(qb + 1) * 512].rearrange("c p t -> p c t")
                             ).then_inc(s_qst[qb % 2].h, 16)
            on("sync", _)
            s_qst[qb % 2].n += 16
            qst_ready = s_qst[qb % 2].n

            for kh in range(8):
                def _(sy, kh=kh, w=kst_pe_free[kh % 2]):
                    sy.wait_ge(sPE.h, w)
                    sy.dma_start(out=kst[kh % 2],
                                 in_=k_hbm[:, :, kh * 512:(kh + 1) * 512].rearrange("c p t -> p c t")
                                 ).then_inc(s_kst[kh % 2].h, 16)
                on("sync", _)
                s_kst[kh % 2].n += 16
                kr = s_kst[kh % 2].n

                for kk in range(4):
                    kc = kh * 4 + kk
                    bank_guard = bank_evac[kc % 3]

                    def _(t, qb=qb, kh=kh, kk=kk, kc=kc, qst_ready=qst_ready, kr=kr,
                          bank_guard=bank_guard, pt_guard=pt_pe_free):
                        if kk == 0:
                            t.wait_ge(s_kst[kh % 2].h, kr)
                            if kh == 0:
                                t.wait_ge(s_qst[qb % 2].h, qst_ready)
                        if bank_guard:
                            t.wait_ge(sAC.h, bank_guard)
                        bk = PS[ST_BANKS[kc % 3]]
                        for dc in range(DC):
                            t.matmul(bk[:], kst[kh % 2][:, dc, kk * 128:(kk + 1) * 128],
                                     qst[qb % 2][:, dc, :], start=(dc == 0), stop=False)
                        mm = t.matmul(bk[:], ones_bf[0:1, :],
                                      negm[0:1, qb * 512:(qb + 1) * 512], start=False, stop=True)
                        mm.then_inc(sPE.h, 1)
                    on("tensor", _)
                    sPE.n += 1
                    st_done = sPE.n

                    def _(sc, kc=kc, st_done=st_done, pt_guard=pt_pe_free):
                        sc.wait_ge(sPE.h, st_done)
                        if kc == 0 and pt_guard:
                            sc.wait_ge(sPE.h, pt_guard)
                        sc.activation(pt[:, kc, :], PS[ST_BANKS[kc % 3]][:], AF.Exp).then_inc(sAC.h, 1)
                    on("scalar", _)
                    sAC.n += 1
                    bank_evac[kc % 3] = sAC.n
                kst_pe_free[kh % 2] = sPE.n
            qst_pe_free[qb % 2] = sPE.n
            EXP_DONE = sAC.n

            def _(t, EXP_DONE=EXP_DONE):
                t.wait_ge(sAC.h, EXP_DONE)
                t.wait_ge(sDV.h, sDV.n)
                for kc in range(NKC):
                    mm = t.matmul(PS[2][:], ones_bf, pt[:, kc, :], start=(kc == 0), stop=(kc == NKC - 1))
                mm.then_inc(sPE.h, 1)
            on("tensor", _)
            sPE.n += 1
            sum_done = sPE.n

            def _(v, sum_done=sum_done, wb_dv_free=wb_dv_free):
                v.wait_ge(sPE.h, sum_done)
                v.wait_ge(sDV.h, wb_dv_free)
                v.tensor_scalar(s_r[:], PS[2][:], 1.0, None, ALU.mult).then_inc(sDV.h, 1)
            on("vector", _)
            sDV.n += 1
            sum_ev = sDV.n

            def _(v, sum_ev=sum_ev):
                v.wait_ge(sDV.h, sum_ev)
                v.reciprocal(s_eps[:], s_r[:])
                v.engine_nop().then_inc(sDV.h, 1)
            on("vector", _)
            sDV.n += 1
            eps_done = sDV.n

            def _(sy, qb=qb, w=s_x2w.n, wc=x2b_cast_prev[0]):
                sy.wait_ge(s_x2w.h, w)
                if wc:
                    sy.wait_ge(sDV.h, wc)
                sy.dma_start(out=xown, in_=xb_r[:, :, qb * 512:(qb + 1) * 512]).then_inc(s_xown.h, 16)
            on("sync", _)
            s_xown.n += 16
            xown_ready = s_xown.n

            sq_done_fs = [0] * DC
            for fs in range(DC):
                def _(t, fs=fs, eps_done=eps_done, bg=pv_bank_evac[fs % 2]):
                    if fs == 0:
                        t.wait_ge(sDV.h, eps_done)
                        t.wait_ge(sAC.h, V_DONE_AC)
                    if bg:
                        t.wait_ge(sDV.h, bg)
                    for kc in range(NKC):
                        mm = t.matmul(PS[3 + fs % 2][:], vbf[:, kc, fs * 128:(fs + 1) * 128],
                                      pt[:, kc, :], start=(kc == 0), stop=(kc == NKC - 1))
                    mm.then_inc(sPE.h, 1)
                on("tensor", _)
                sPE.n += 1
                pv_done = sPE.n

                def _(v, fs=fs, pv_done=pv_done):
                    v.wait_ge(sPE.h, pv_done)
                    v.tensor_tensor(w_b[:, fs, :], PS[3 + fs % 2][:], s_eps[:], ALU.mult)
                    v.tensor_tensor(uw[:, fs, :], w_b[:, fs, :], w_b[:, fs, :], ALU.mult).then_inc(sDV.h, 1)
                on("vector", _)
                sDV.n += 1
                pv_bank_evac[fs % 2] = sDV.n
                sq_done = sDV.n

                sq_done_fs[fs] = sq_done
                if fs > 0:
                    def _(t, fs=fs, sq_done=sq_done_fs[fs - 1]):
                        t.wait_ge(sDV.h, sq_done)
                        t.matmul(PS[5][:], ones_bf, w_b[:, fs - 1, :], start=(fs - 1 == 0), stop=False)
                        mm = t.matmul(PS[6][:], ones_bf, uw[:, fs - 1, :], start=(fs - 1 == 0), stop=False)
                        mm.then_inc(sPE.h, 1)
                    on("tensor", _)
                    sPE.n += 1

            def _(t, sq_done=sq_done_fs[DC - 1]):
                t.wait_ge(sDV.h, sq_done)
                t.matmul(PS[5][:], ones_bf, w_b[:, DC - 1, :], start=False, stop=True)
                mm = t.matmul(PS[6][:], ones_bf, uw[:, DC - 1, :], start=False, stop=True)
                mm.then_inc(sPE.h, 1)
            on("tensor", _)
            sPE.n += 1
            pt_pe_free = sPE.n
            ln2_mm_done = sPE.n

            def _(v, ln2_mm_done=ln2_mm_done):
                v.wait_ge(sPE.h, ln2_mm_done)
                v.tensor_scalar(s_m[:], PS[5][:], 1.0 / D, None, ALU.mult)
                v.tensor_scalar(s_ms[:], PS[6][:], 1.0 / D, None, ALU.mult)
                v.tensor_tensor(s_r[:], s_m[:], s_m[:], ALU.mult)
                v.tensor_tensor(s_ms[:], s_ms[:], s_r[:], ALU.subtract)
                v.engine_nop().then_inc(sDV.h, 1)
            on("vector", _)
            sDV.n += 1
            var2_done = sDV.n

            def _(sc, var2_done=var2_done):
                sc.wait_ge(sDV.h, var2_done)
                sc.activation(s_ms[:], s_ms[:], AF.Ln, bias=eps5[:])
                sc.activation(s_r[:], s_ms[:], AF.Exp, scale=-0.5).then_inc(sAC.h, 1)
            on("scalar", _)
            sAC.n += 1
            rstd2_done = sAC.n

            def _(v, rstd2_done=rstd2_done, xown_ready=xown_ready):
                v.wait_ge(sAC.h, rstd2_done)
                v.wait_ge(s_xown.h, xown_ready)
                v.tensor_copy(s_mb[:], s_m[:])
                v.tensor_copy(s_rb[:], s_r[:])
                for fs in range(DC):
                    v.tensor_tensor(uw[:, fs, :], w_b[:, fs, :], s_mb[:], ALU.subtract)
                    v.tensor_tensor(uw[:, fs, :], uw[:, fs, :], s_rb[:], ALU.mult)
                    mm = v.tensor_tensor(xown[:, fs, :], xown[:, fs, :], uw[:, fs, :], ALU.add)
                mm.then_inc(sDV.h, 1)
            on("vector", _)
            sDV.n += 1
            wb_dv_free = sDV.n
            x2_done = sDV.n

            def _(v, x2_done=x2_done):
                v.wait_ge(sDV.h, x2_done)
                v.tensor_copy(uw[:], xown[:]).then_inc(sDV.h, 1)
            on("vector", _)
            sDV.n += 1
            x2b_done = sDV.n
            x2b_cast_prev[0] = x2b_done

            def _(sc, x2b_done=x2b_done):
                sc.wait_ge(sDV.h, x2b_done)
                sc.square(w_b[:], uw[:]).then_inc(sAC.h, 1)
            on("scalar", _)
            sAC.n += 1
            sq3q_done = sAC.n

            def _(t, x2b_done=x2b_done, sq3q_done=sq3q_done):
                t.wait_ge(sDV.h, x2b_done)
                for dc in range(DC):
                    t.matmul(PS[5][:], ones_bf, uw[:, dc, :], start=(dc == 0), stop=(dc == DC - 1))
                t.wait_ge(sAC.h, sq3q_done)
                for dc in range(DC):
                    mm = t.matmul(PS[6][:], ones_bf, w_b[:, dc, :], start=(dc == 0), stop=(dc == DC - 1))
                mm.then_inc(sPE.h, 1)
            on("tensor", _)
            sPE.n += 1
            st3q_done = sPE.n

            def _(v, qb=qb, st3q_done=st3q_done):
                v.wait_ge(sPE.h, st3q_done)
                v.tensor_scalar(mr3[qb][:, 0, :], PS[5][:], 1.0 / D, None, ALU.mult)
                v.tensor_scalar(s_ms[:], PS[6][:], 1.0 / D, None, ALU.mult)
                v.tensor_tensor(s_r[:], mr3[qb][:, 0, :], mr3[qb][:, 0, :], ALU.mult)
                v.tensor_tensor(s_ms[:], s_ms[:], s_r[:], ALU.subtract).then_inc(sDV.h, 1)
            on("vector", _)
            sDV.n += 1
            var3q = sDV.n

            def _(sc, qb=qb, var3q=var3q):
                sc.wait_ge(sDV.h, var3q)
                sc.activation(s_ms[:], s_ms[:], AF.Ln, bias=eps5[:])
                sc.activation(mr3[qb][:, 1, :], s_ms[:], AF.Exp, scale=-0.5).then_inc(sAC.h, 1)
            on("scalar", _)
            sAC.n += 1
            wb_dv_free = sDV.n
            MR3_AC = sAC.n

            if qb in (1, 3):
                def _(g, qb=qb, w=MR3_AC):
                    g.wait_ge(sAC.h, w)
                    g.dma_start(out=mr_hbm[:, (qb - 1) * 1024:(qb + 1) * 1024],
                                in_=mr3all[:, (qb - 1) * 1024:(qb + 1) * 1024]).then_inc(s_mr.h, 16)
                on("gpsimd", _)
                s_mr.n += 16

            def _(g, qb=qb, x2_done=x2_done):
                g.wait_ge(sDV.h, x2_done)
                g.dma_start(out=x2_hbm[:, :, qb * 512:(qb + 1) * 512].rearrange("c p t -> p c t"),
                             in_=xown[:]).then_inc(s_x2w.h, 16)
            on("gpsimd", _)
            s_x2w.n += 16
            nc.leave_named_scope("qb%d" % qb, _sqb, False)

            if stage.startswith("qb") and qb == int(stage[2:]):
                def _(g, w=s_x2w.n):
                    g.wait_ge(s_x2w.h, w)
                    g.dma_start(out=dbg["pt"], in_=pt.rearrange("p a b -> p (a b)")).then_inc(s_dbg.h, 16)
                    g.dma_start(out=dbg["wb"], in_=w_b.rearrange("p a b -> p (a b)")).then_inc(s_dbg.h, 16)
                    g.dma_start(out=dbg["eps"], in_=s_eps).then_inc(s_dbg.h, 16)
                    g.dma_start(out=dbg["sm"], in_=s_m).then_inc(s_dbg.h, 16)
                    g.dma_start(out=dbg["sms"], in_=s_ms).then_inc(s_dbg.h, 16)
                    g.dma_start(out=dbg["sr"], in_=s_r).then_inc(s_dbg.h, 16)
                    g.dma_start(out=dbg["xo"], in_=xown.rearrange("p a b -> p (a b)")).then_inc(s_dbg.h, 16)
                on("gpsimd", _)
                s_dbg.n += 16 * 7

                def _(sy, w=s_dbg.n):
                    sy.wait_ge(s_dbg.h, w)
                on("sync", _)
                return nc


        PT_FREE_QB3 = pt_pe_free
        ATTN_PE = sPE.n
        ATTN_DV = sDV.n
        ATTN_AC3 = sAC.n
        X2_STORED = s_x2w.n

        if stage == "attn":
            def _(g):
                g.wait_ge(s_x2w.h, X2_STORED)
                g.dma_start(out=dbg["x2"], in_=x2_hbm).then_inc(s_dbg.h, 16)
            on("gpsimd", _)
            s_dbg.n += 16

            def _(sy):
                sy.wait_ge(s_dbg.h, s_dbg.n)
            on("sync", _)
            return nc

        def pe_srcs(j):
            if j < 8:
                return (eg[j].rearrange("(c p) d -> p c d", p=128),
                        eu[j].rearrange("(c p) d -> p c d", p=128),
                        ed[j].rearrange("(c p) hh -> p c hh", p=128), j)
            jj = j - 8
            return (sg[:, jj * 512:(jj + 1) * 512].rearrange("(c p) d -> p c d", p=128),
                    su[:, jj * 512:(jj + 1) * 512].rearrange("(c p) d -> p c d", p=128),
                    sd[jj * 512:(jj + 1) * 512, :].rearrange("(c p) hh -> p c hh", p=128), None)

        groups = [[8, 9, 0, 1], [2, 3, 4, 5], [6, 7]]

        wg_free = [0, 0]
        wu_free = [0, 0]
        wd_free = [0] * 4
        crep_free = [0, 0]
        gu_evac = [(0, 0), (0, 0)]
        down_evac = [(0, 0), (0, 0)]

        first_ffn_load = [True]
        for hf in range(2):
            _shf = nc.enter_named_scope("ffn_pre%d" % hf, False)[0]
            x2_ready_ts = [0, 0]
            for ts in range(2):
                def _(sy, hf=hf, ts=ts, wy=s_y.n):
                    if hf == 0:
                        if ts == 0:
                            sy.wait_ge(s_x2w.h, 32)
                            sy.wait_ge(sPE.h, PT_FREE_QB3)
                    else:
                        if ts == 0:
                            sy.wait_ge(s_x2w.h, X2_STORED)
                            sy.wait_ge(s_y.h, wy)
                            sy.wait_ge(sDV.h, sDV.n)
                    sy.dma_start(out=x2sl[:, :, ts * 512:(ts + 1) * 512],
                                 in_=x2_hbm[:, :, hf * TH + ts * 512:hf * TH + (ts + 1) * 512]
                                 .rearrange("c p t -> p c t")).then_inc(s_x2r.h, 16)
                on("sync", _)
                s_x2r.n += 16
                x2_ready_ts[ts] = s_x2r.n
            x2_ready = s_x2r.n

            cast3_ts = [0, 0]
            for ts in range(2):
                def _(v, ts=ts, w=x2_ready_ts[ts], hf=hf):
                    v.wait_ge(s_x2r.h, w)
                    if hf == 0 and ts == 0:
                        v.memset(coef_fm[0:32, :], 0.0)
                    v.tensor_copy(x2ln[:, :, ts * 512:(ts + 1) * 512],
                                  x2sl[:, :, ts * 512:(ts + 1) * 512]).then_inc(sDV.h, 1)
                on("vector", _)
                sDV.n += 1
                cast3_ts[ts] = sDV.n
            cast3_done = sDV.n

            def _(g, hf=hf, w=(16 if hf == 0 else s_mr.n), wu=sDV.n):
                g.wait_ge(s_mr.h, w)
                g.wait_ge(sDV.h, wu)
                for ts in range(2):
                    g.dma_start(out=mrh[ts],
                                in_=mr_hbm[:, (hf * 2 + ts) * 1024:(hf * 2 + ts + 1) * 1024]
                                .rearrange("p (a b) -> p a b", a=2, b=512)).then_inc(s_mr.h, 16)
            on("gpsimd", _)
            s_mr.n += 32
            mr_ready = s_mr.n

            apply_ts = [0, 0]
            for ts in range(2):
                def _(v, ts=ts, cast3_done=cast3_ts[ts], mr_ready=mr_ready):
                    v.wait_ge(sDV.h, cast3_done)
                    v.wait_ge(s_mr.h, mr_ready)
                    for dc in range(DC):
                        sl = x2ln[:, dc, ts * 512:(ts + 1) * 512]
                        v.tensor_tensor(sl, sl, mrh[ts][:, 0, :], ALU.subtract)
                        mm = v.tensor_tensor(sl, sl, mrh[ts][:, 1, :], ALU.mult)
                    mm.then_inc(sDV.h, 1)
                on("vector", _)
                sDV.n += 1
                apply_ts[ts] = sDV.n
            x2ln_apply = sDV.n

            rst = {"coef_done_dv": 0, "router_ac": 0, "logits_pe": 0}
            top2_done = [0] * 8
            pexp_done_tt = [0] * 8

            def emit_logits(apply_ts=apply_ts):
                for half4 in range(2):
                    def _(t, half4=half4, w=apply_ts[half4]):
                        t.wait_ge(sDV.h, w)
                        n = 0
                        for tt in range(half4 * 4, half4 * 4 + 4):
                            for dc in range(DC):
                                n += 1
                                mm = t.matmul(PS[6][:, tt * 8:(tt + 1) * 8],
                                              x2ln[:, dc, tt * 128:(tt + 1) * 128],
                                              rout_b[:, dc, :], start=(half4 == 0 and n == 1),
                                              stop=(half4 == 1 and n == 32),
                                              skip_group_check=True)
                        mm.then_inc(sPE.h, 1)
                    on("tensor", _)
                    sPE.n += 1
                rst["logits_pe"] = sPE.n

            def make_chain(tt):
                def chain():
                    def _(v, tt=tt, lp=rst["logits_pe"], base=sDV.n,
                          pe_prev=(pexp_done_tt[tt - 1] if tt else 0)):
                        v.wait_ge(sPE.h, lp)
                        if pe_prev:
                            v.wait_ge(sAC.h, pe_prev)
                        v.reduce_max(lmax[:], PS[6][:, tt * 8:(tt + 1) * 8], axis=AX.X).then_inc(sDV.h, 1)
                        v.wait_ge(sDV.h, base + 1)
                        v.tensor_scalar(lmax[:], lmax[:], -1.0, None, ALU.mult).then_inc(sDV.h, 1)
                    on("vector", _)
                    sDV.n += 2
                    nmax_done = sDV.n

                    def _(sc, tt=tt, nmax_done=nmax_done,
                          t2p=(top2_done[tt - 1] if tt else 0)):
                        sc.wait_ge(sDV.h, nmax_done)
                        if t2p:
                            sc.wait_ge(sDV.h, t2p)
                        sc.activation(f_t[:, 0:8], PS[6][:, tt * 8:(tt + 1) * 8],
                                      AF.Exp, bias=lmax[:]).then_inc(sAC.h, 1)
                    on("scalar", _)
                    sAC.n += 1
                    pexp_done_tt[tt] = sAC.n

                    def _(v, tt=tt, pexp=pexp_done_tt[tt], base=sDV.n):
                        v.wait_ge(sAC.h, pexp)
                        n = base
                        ops = [
                            lambda: v.reduce_sum(lsum[:], f_t[:, 0:8], axis=AX.X),
                            lambda: v.reciprocal(lrec[:], lsum[:]),
                            lambda: v.tensor_scalar(f_t[:, 0:8], f_t[:, 0:8], lrec[:], None, ALU.mult),
                            lambda: v.reduce_max(lmax[:], f_t[:, 0:8], axis=AX.X),
                            lambda: v.tensor_scalar(f_t[:, 8:16], f_t[:, 0:8], lmax[:], None, ALU.is_ge),
                            lambda: v.tensor_scalar(f_t[:, 8:16], f_t[:, 8:16], 2.0, None, ALU.mult),
                            lambda: v.tensor_tensor(f_t[:, 8:16], f_t[:, 0:8], f_t[:, 8:16], ALU.subtract),
                            lambda: v.reduce_max(lmax[:], f_t[:, 8:16], axis=AX.X),
                            lambda: v.tensor_scalar(f_t[:, 8:16], f_t[:, 0:8], lmax[:], None, ALU.is_ge),
                            lambda: v.tensor_tensor(coef_tma[:, tt * 8:(tt + 1) * 8],
                                                    f_t[:, 0:8], f_t[:, 8:16], ALU.mult),
                        ]
                        for op in ops:
                            op().then_inc(sDV.h, 1)
                            n += 1
                            v.wait_ge(sDV.h, n)
                    on("vector", _)
                    sDV.n += 10
                    top2_done[tt] = sDV.n
                return chain

            chain_queue = [make_chain(tt) for tt in range(8)]

            def emit_router_tail():
                while chain_queue:
                    chain_queue.pop(0)()

                def _(t, w=top2_done, pe7=pexp_done_tt):
                    t.wait_ge(sDV.h, max(w))
                    t.wait_ge(sAC.h, max(pe7))
                    for tt in range(8):
                        bank = PS[6] if tt < 4 else PS[7]
                        t.matmul(bank[0:8, (tt % 4) * 128:(tt % 4 + 1) * 128],
                                 coef_tma[:, tt * 8:(tt + 1) * 8], eye_bf,
                                 start=(tt % 4 == 0), stop=(tt % 4 == 3),
                                 skip_group_check=True).then_inc(sPE.h, 1)
                on("tensor", _)
                sPE.n += 8
                ctr_done = sPE.n

                def _(v, ctr_done=ctr_done):
                    v.wait_ge(sPE.h, ctr_done)
                    v.tensor_copy(coef_fm[0:8, 0:512], PS[6][0:8, 0:512])
                    mm = v.tensor_copy(coef_fm[0:8, 512:1024], PS[7][0:8, 0:512])
                    mm.then_inc(sDV.h, 1)
                on("vector", _)
                sDV.n += 1
                rst["coef_done_dv"] = sDV.n
            coef_done_dv = None

            for gi, grp in enumerate(groups):
                _sgg = nc.enter_named_scope("ffn_h%dg%d_gu" % (hf, gi), False)[0]
                grp_crep = [0]

                def emit_group_creps(routed):
                    if not routed:
                        return
                    for wave in range(0, len(routed), 2):
                        pair = routed[wave:wave + 2]

                        def _(t, pair=pair, cd=rst["coef_done_dv"]):
                            t.wait_ge(sDV.h, max(cd, sDV.n))
                            for i, ce in enumerate(pair):
                                for ts2 in range(2):
                                    mm = t.matmul(PS[4 + 2 * i + ts2][:], sel_b[:, ce, :],
                                                  coef_fm[0:32, ts2 * 512:(ts2 + 1) * 512],
                                                  start=True, stop=True)
                            mm.then_inc(sPE.h, 1)
                        on("tensor", _)
                        sPE.n += 1
                        bc_done = sPE.n

                        def _(v, pair=pair, bc_done=bc_done):
                            v.wait_ge(sPE.h, bc_done)
                            for i, ce in enumerate(pair):
                                for ts2 in range(2):
                                    mm = v.tensor_copy(crep[ce % 4][:, ts2 * 512:(ts2 + 1) * 512],
                                                       PS[4 + 2 * i + ts2][:])
                            mm.then_inc(sDV.h, 1)
                        on("vector", _)
                        sDV.n += 1
                    grp_crep[0] = sDV.n

                if gi > 0:
                    emit_group_creps([p for p in grp if p < 8])
                for j, pj in enumerate(grp):
                    if gi == 0 and j == 0:
                        emit_logits()
                    if gi == 0 and j == 2:
                        emit_router_tail()
                        emit_group_creps([p for p in grp if p < 8])
                    pre_router = (gi == 0 and j < 2)
                    wsrc, usrc, dsrc, ce = pe_srcs(pj)
                    slot = j % 2

                    def _(g, slot=slot, j=j, wsrc=wsrc, usrc=usrc, dsrc=dsrc,
                          wgf=wg_free[slot], wuf=wu_free[slot], wdf=wd_free[j],
                          first=first_ffn_load[0]):
                        if first:
                            g.wait_ge(sPE.h, ATTN_PE)
                            g.wait_ge(sDV.h, ATTN_DV)
                        g.wait_ge(sPE.h, wgf)
                        g.dma_start(out=wgb[slot][:, :, 0:256], in_=wsrc[:, :, 0:256]).then_inc(s_wg[slot].h, 16)
                        g.wait_ge(sPE.h, wuf)
                        g.dma_start(out=wub[slot][:, :, 0:256], in_=usrc[:, :, 0:256]).then_inc(s_wu[slot].h, 16)
                        g.dma_start(out=wgb[slot][:, :, 256:512], in_=wsrc[:, :, 256:512]).then_inc(s_wg[slot].h, 16)
                        g.dma_start(out=wub[slot][:, :, 256:512], in_=usrc[:, :, 256:512]).then_inc(s_wu[slot].h, 16)
                        g.wait_ge(sPE.h, wdf)
                        g.dma_start(out=wdb[j], in_=dsrc).then_inc(s_wd[j].h, 16)
                    on("gpsimd", _)
                    first_ffn_load[0] = False
                    s_wg[slot].n += 32
                    s_wu[slot].n += 32
                    s_wd[j].n += 16
                    wg_ready_h = s_wg[slot].n - 16
                    wu_ready_h = s_wu[slot].n - 16
                    wg_ready = s_wg[slot].n
                    wu_ready = s_wu[slot].n

                    cslot = (ce % 4) if ce is not None else None
                    crep_ready = grp_crep[0] if ce is not None else 0

                    for dffc in range(4):
                        par = (dffc % 2) if not pre_router else 0
                        bg = 0 if par == 0 else 4
                        eac, edv = gu_evac[par]

                        def _(t, slot=slot, dffc=dffc, bg=bg, wg_ready=wg_ready,
                              wu_ready=wu_ready, eac=eac, edv=edv, x2ln_apply=x2ln_apply,
                              crep_ready=crep_ready):
                            if dffc == 0:
                                t.wait_ge(s_wg[slot].h, wg_ready_h)
                                t.wait_ge(s_wu[slot].h, wu_ready_h)
                                t.wait_ge(sDV.h, x2ln_apply)
                            if dffc == 2:
                                t.wait_ge(s_wg[slot].h, wg_ready)
                                t.wait_ge(s_wu[slot].h, wu_ready)
                            if dffc == 1 and crep_ready:
                                t.wait_ge(sDV.h, crep_ready)
                            if eac:
                                t.wait_ge(sAC.h, eac)
                            if edv:
                                t.wait_ge(sDV.h, edv)
                            for dc in range(DC):
                                for ts2 in range(2):
                                    t.matmul(PS[bg + ts2][:], wgb[slot][:, dc, dffc * 128:(dffc + 1) * 128],
                                             x2ln[:, dc, ts2 * 512:(ts2 + 1) * 512],
                                             start=(dc == 0), stop=(dc == DC - 1))
                            for dc in range(DC):
                                for ts2 in range(2):
                                    mm = t.matmul(PS[bg + 2 + ts2][:], wub[slot][:, dc, dffc * 128:(dffc + 1) * 128],
                                                  x2ln[:, dc, ts2 * 512:(ts2 + 1) * 512],
                                                  start=(dc == 0), stop=(dc == DC - 1))
                            mm.then_inc(sPE.h, 1)
                        on("tensor", _)
                        sPE.n += 1
                        gu_done = sPE.n

                        def _(sc, j=j, dffc=dffc, bg=bg, gu_done=gu_done):
                            sc.wait_ge(sPE.h, gu_done)
                            for ts2 in range(2):
                                cp = sc.activation(hb[j][:, dffc, ts2 * 512:(ts2 + 1) * 512],
                                                   PS[bg + ts2][:], AF.Silu)
                            cp.then_inc(sAC.h, 1)
                        on("scalar", _)
                        sAC.n += 1
                        silu_done = sAC.n

                        def _(v, j=j, dffc=dffc, bg=bg, silu_done=silu_done,
                              cslot=cslot, crep_ready=crep_ready):
                            v.wait_ge(sAC.h, silu_done)
                            if crep_ready:
                                v.wait_ge(sDV.h, crep_ready)
                            for ts2 in range(2):
                                hsl = hb[j][:, dffc, ts2 * 512:(ts2 + 1) * 512]
                                if cslot is not None:
                                    v.tensor_tensor(uwtmp[:, ts2, :], PS[bg + 2 + ts2][:],
                                                    crep[cslot][:, ts2 * 512:(ts2 + 1) * 512], ALU.mult)
                                else:
                                    v.tensor_copy(uwtmp[:, ts2, :], PS[bg + 2 + ts2][:])
                                mm = v.tensor_tensor(hsl, hsl, uwtmp[:, ts2, :], ALU.mult)
                            mm.then_inc(sDV.h, 1)
                        on("vector", _)
                        sDV.n += 1
                        gu_evac[par] = (silu_done, sDV.n)
                        if pre_router and chain_queue:
                            chain_queue.pop(0)()
                    wg_free[slot] = sPE.n
                    wu_free[slot] = sPE.n

                nc.leave_named_scope("ffn_h%dg%d_gu" % (hf, gi), _sgg, False)
                _sgd = nc.enter_named_scope("ffn_h%dg%d_dn" % (hf, gi), False)[0]
                H_READY_DV = sDV.n
                WD_READY = [s_wd[j].n for j in range(len(grp))]

                for hc in range(DC):
                    par = hc % 2
                    bk = 0 if par == 0 else 2
                    eac, edv = down_evac[par]

                    def _(t, gi=gi, grp=grp, hc=hc, bk=bk, H_READY_DV=H_READY_DV,
                          WD_READY=WD_READY, eac=eac, edv=edv):
                        t.wait_ge(sDV.h, H_READY_DV)
                        if hc == 0:
                            for j in range(len(grp)):
                                t.wait_ge(s_wd[j].h, WD_READY[j])
                        if eac:
                            t.wait_ge(sAC.h, eac)
                        if edv:
                            t.wait_ge(sDV.h, edv)
                        nmm = 4 * len(grp)
                        for ts2 in range(2):
                            n = 0
                            for j in range(len(grp)):
                                for dffc in range(4):
                                    n += 1
                                    mm = t.matmul(PS[bk + ts2][:], wdb[j][:, dffc, hc * 128:(hc + 1) * 128],
                                                  hb[j][:, dffc, ts2 * 512:(ts2 + 1) * 512],
                                                  start=(n == 1), stop=(n == nmm))
                        mm.then_inc(sPE.h, 1)
                    on("tensor", _)
                    sPE.n += 1
                    down_done = sPE.n

                    if gi == 0:
                        def _(sc, hc=hc, bk=bk, down_done=down_done):
                            sc.wait_ge(sPE.h, down_done)
                            for ts2 in range(2):
                                cp = sc.copy(acc[:, hc, ts2 * 512:(ts2 + 1) * 512], PS[bk + ts2][:])
                            cp.then_inc(sAC.h, 1)
                        on("scalar", _)
                        sAC.n += 1
                        down_evac[par] = (sAC.n, down_evac[par][1])
                    else:
                        def _(v, hc=hc, bk=bk, down_done=down_done):
                            v.wait_ge(sPE.h, down_done)
                            for ts2 in range(2):
                                asl = acc[:, hc, ts2 * 512:(ts2 + 1) * 512]
                                mm = v.tensor_tensor(asl, asl, PS[bk + ts2][:], ALU.add)
                            mm.then_inc(sDV.h, 1)
                        on("vector", _)
                        sDV.n += 1
                        down_evac[par] = (down_evac[par][0], sDV.n)
                for j in range(len(grp)):
                    wd_free[j] = sPE.n
                gu_evac[0] = (max(gu_evac[0][0], down_evac[0][0], down_evac[1][0]),
                              max(gu_evac[0][1], down_evac[0][1], down_evac[1][1]))
                nc.leave_named_scope("ffn_h%dg%d_dn" % (hf, gi), _sgd, False)
            DOWN_PE = sPE.n
            DOWN_DV = sDV.n
            DOWN_AC = sAC.n

            if stage == "ffn0":
                def _(g, a=DOWN_PE, b=DOWN_DV, c=DOWN_AC):
                    g.wait_ge(sPE.h, a)
                    g.wait_ge(sDV.h, b)
                    g.wait_ge(sAC.h, c)
                    g.dma_start(out=dbg["x2ln"], in_=x2ln.rearrange("p a b -> p (a b)")).then_inc(s_dbg.h, 16)
                    g.dma_start(out=dbg["coef"], in_=coef_fm).then_inc(s_dbg.h, 16)
                    g.dma_start(out=dbg["acc"], in_=acc.rearrange("p a b -> p (a b)")).then_inc(s_dbg.h, 16)
                    g.dma_start(out=dbg["h0"], in_=hb[0].rearrange("p a b -> p (a b)")).then_inc(s_dbg.h, 16)
                    g.dma_start(out=dbg["crep0"], in_=crep[0]).then_inc(s_dbg.h, 16)
                on("gpsimd", _)
                s_dbg.n += 16 * 5

                def _(sy, w=s_dbg.n):
                    sy.wait_ge(s_dbg.h, w)
                on("sync", _)
                return nc

            _sl4 = nc.enter_named_scope("ffn_h%d_ln4" % hf, False)[0]
            ln4_stats_pe = 0
            for ts in range(2):
                def _(v, ts=ts, DOWN_DV=DOWN_DV, DOWN_PE=DOWN_PE, DOWN_AC=DOWN_AC,
                      ln4_stats_pe=ln4_stats_pe):
                    v.wait_ge(sDV.h, DOWN_DV)
                    v.wait_ge(sAC.h, DOWN_AC)
                    v.wait_ge(sPE.h, DOWN_PE if ts == 0 else ln4_stats_pe)
                    for dc in range(DC):
                        v.tensor_copy(castb[:, dc, :], acc[:, dc, ts * 512:(ts + 1) * 512])
                        mm = v.tensor_tensor(sqb[:, dc, :], castb[:, dc, :], castb[:, dc, :], ALU.mult)
                    mm.then_inc(sDV.h, 1)
                on("vector", _)
                sDV.n += 1
                sq4_done = sDV.n

                def _(t, sq4_done=sq4_done):
                    t.wait_ge(sDV.h, sq4_done)
                    for dc in range(DC):
                        t.matmul(PS[6][:], ones_bf, castb[:, dc, :], start=(dc == 0), stop=(dc == DC - 1))
                    for dc in range(DC):
                        mm = t.matmul(PS[7][:], ones_bf, sqb[:, dc, :], start=(dc == 0), stop=(dc == DC - 1))
                    mm.then_inc(sPE.h, 1)
                on("tensor", _)
                sPE.n += 1
                ln4_stats_pe = sPE.n

                def _(v, done=ln4_stats_pe):
                    v.wait_ge(sPE.h, done)
                    v.tensor_scalar(f_m[:], PS[6][:], 1.0 / D, None, ALU.mult)
                    v.tensor_scalar(f_ms[:], PS[7][:], 1.0 / D, None, ALU.mult)
                    v.tensor_tensor(f_r[:], f_m[:], f_m[:], ALU.mult)
                    v.tensor_tensor(f_ms[:], f_ms[:], f_r[:], ALU.subtract).then_inc(sDV.h, 1)
                on("vector", _)
                sDV.n += 1
                var4 = sDV.n

                def _(sc, var4=var4):
                    sc.wait_ge(sDV.h, var4)
                    sc.activation(f_ms[:], f_ms[:], AF.Ln, bias=eps5[:])
                    sc.activation(f_r[:], f_ms[:], AF.Exp, scale=-0.5).then_inc(sAC.h, 1)
                on("scalar", _)
                sAC.n += 1
                rstd4 = sAC.n

                def _(v, ts=ts, rstd4=rstd4):
                    v.wait_ge(sAC.h, rstd4)
                    for dc in range(DC):
                        asl = acc[:, dc, ts * 512:(ts + 1) * 512]
                        xsl = x2sl[:, dc, ts * 512:(ts + 1) * 512]
                        v.tensor_tensor(f_t2[:], asl, f_m[:], ALU.subtract)
                        v.tensor_tensor(f_t2[:], f_t2[:], f_r[:], ALU.mult)
                        mm = v.tensor_tensor(xsl, xsl, f_t2[:], ALU.add)
                    mm.then_inc(sDV.h, 1)
                on("vector", _)
                sDV.n += 1
            OUT_DONE = sDV.n

            def _(sy, hf=hf, OUT_DONE=OUT_DONE):
                sy.wait_ge(sDV.h, OUT_DONE)
                sy.dma_start(out=y_r[:, :, hf * TH:(hf + 1) * TH], in_=x2sl).then_inc(s_y.h, 16)
            on("sync", _)
            s_y.n += 16
            nc.leave_named_scope("ffn_h%d_ln4" % hf, _sl4, False)

        def _(sy, w=s_y.n):
            sy.wait_ge(s_y.h, w)
        on("sync", _)

    return nc


_NC_CACHE = {}


def _get_nc(stage):
    if stage not in _NC_CACHE:
        _NC_CACHE[stage] = build(stage)
    return _NC_CACHE[stage]


def make_in_maps(L0_attn_x, L0_attn_W_qkv, L0_ffn_router, L0_ffn_experts_gate,
                 L0_ffn_experts_up, L0_ffn_experts_down, L0_ffn_shared_gate,
                 L0_ffn_shared_up, L0_ffn_shared_down):
    x = np.asarray(L0_attn_x, np.float32)
    sel8 = np.zeros((32, 1024), dtype=np.float32)
    for e in range(8):
        sel8[e, e * 128:(e + 1) * 128] = 1.0
    shared = dict(
        wqkvT=np.ascontiguousarray(np.asarray(L0_attn_W_qkv, np.float32).T),
        router=np.ascontiguousarray(np.asarray(L0_ffn_router, np.float32)),
        eg=np.ascontiguousarray(np.asarray(L0_ffn_experts_gate, np.float32)),
        eu=np.ascontiguousarray(np.asarray(L0_ffn_experts_up, np.float32)),
        ed=np.ascontiguousarray(np.asarray(L0_ffn_experts_down, np.float32)),
        sg=np.ascontiguousarray(np.asarray(L0_ffn_shared_gate, np.float32)),
        su=np.ascontiguousarray(np.asarray(L0_ffn_shared_up, np.float32)),
        sd=np.ascontiguousarray(np.asarray(L0_ffn_shared_down, np.float32)),
        eye=np.eye(128, dtype=np.float32),
        sel8=sel8,
    )
    in_maps = []
    for c in range(8):
        b, h = c // 2, c % 2
        xb = x[b]
        if h:
            xb = np.concatenate([xb[2048:], xb[:2048]], axis=0)
        m = dict(shared)
        m["xb"] = np.ascontiguousarray(xb.T)
        in_maps.append(m)
    return in_maps


def kernel(**inputs):
    in_maps = make_in_maps(**inputs)
    nc = _get_nc("full")
    res = run_bass_kernel_spmd(nc, in_maps, core_ids=list(range(8)))
    out = np.empty((4, 4096, 1024), np.float32)
    for c in range(8):
        b, h = c // 2, c % 2
        out[b, h * 2048:(h + 1) * 2048, :] = np.asarray(res.results[c]["y"]).T
    return out


# version 12 (final)
# speedup vs baseline: 1.0524x; 1.0051x over previous
import numpy as np
from contextlib import ExitStack

import concourse.bass as bass
import concourse.mybir as mybir
from concourse.bass_utils import run_bass_kernel_spmd

F32 = mybir.dt.float32
BF16 = mybir.dt.bfloat16
AF = mybir.ActivationFunctionType
ALU = mybir.AluOpType
AX = mybir.AxisListType

KB = 1024
D = 1024
DC = 8
TALL = 4096
TOWN = 2048
NQB = 4
NKC = 32
TH = 1024

O_ONES = 0
O_EYE = 256
O_ROUT = 512
O_EPS5 = 640
O_EPS35 = 648
O_WQ = 1 * KB
O_XLN = 49 * KB
O_SPILL = [177 * KB + 512, 181 * KB + 512]
O_KSQ = 113 * KB
O_XS = [121 * KB, 137 * KB]
O_U = 153 * KB
O_ASTATS = 161 * KB
O_KMAX = 177 * KB
O_QSQ = 185 * KB + 512
O_LNQ = 189 * KB + 512
O_MREP = 177 * KB + 512
O_PT = 1 * KB
O_KST = [33 * KB, 41 * KB]
O_QST = [49 * KB, 57 * KB]
O_XOWN = 65 * KB
O_WB = 81 * KB
O_UW = 89 * KB
O_STATS = 97 * KB
O_NEGM = 107 * KB
O_VBF = 113 * KB
O_X2SL = 1 * KB
O_X2LN = 33 * KB
O_ACC = 49 * KB
O_WG = [186 * KB, 81 * KB]
O_WU = [89 * KB, 97 * KB]
O_FM = 105 * KB
O_FMS = 107 * KB
O_FR = 109 * KB
O_FT2 = 111 * KB
O_FT = 113 * KB
O_COEF = 114 * KB
O_SEL = 204 * KB
O_CREP = 118 * KB
O_H = 122 * KB
O_WD = 154 * KB
O_UWT = 194 * KB
ARENA = 206 * KB


class C:

    def __init__(self, stack, nc, name):
        self.h = stack.enter_context(nc.semaphore(name))
        self.n = 0
        self.name = name


def build(stage="full"):
    nc = bass.Bass("TRN2", target_bir_lowering=False)

    xb = nc.dram_tensor("xb", [D, TALL], F32, kind="ExternalInput").ap()
    wqkvT = nc.dram_tensor("wqkvT", [D, 3 * D], F32, kind="ExternalInput").ap()
    router = nc.dram_tensor("router", [D, 8], F32, kind="ExternalInput").ap()
    eg = nc.dram_tensor("eg", [8, D, 512], F32, kind="ExternalInput").ap()
    eu = nc.dram_tensor("eu", [8, D, 512], F32, kind="ExternalInput").ap()
    ed = nc.dram_tensor("ed", [8, 512, D], F32, kind="ExternalInput").ap()
    sg = nc.dram_tensor("sg", [D, D], F32, kind="ExternalInput").ap()
    su = nc.dram_tensor("su", [D, D], F32, kind="ExternalInput").ap()
    sd = nc.dram_tensor("sd", [D, D], F32, kind="ExternalInput").ap()
    eye_in = nc.dram_tensor("eye", [128, 128], F32, kind="ExternalInput").ap()
    sel_in = nc.dram_tensor("sel8", [32, 1024], F32, kind="ExternalInput").ap()

    k_hbm = nc.dram_tensor("k_hbm", [DC, 128, TALL], BF16, kind="Internal").ap()
    q_hbm = nc.dram_tensor("q_hbm", [DC, 128, TOWN], BF16, kind="Internal").ap()
    x2_hbm = nc.dram_tensor("x2_hbm", [DC, 128, TOWN], F32, kind="Internal").ap()
    mr_hbm = nc.dram_tensor("mr_hbm", [128, 4096], F32, kind="Internal").ap()

    y = nc.dram_tensor("y", [D, TOWN], F32, kind="ExternalOutput").ap()
    y_r = y.rearrange("(c p) t -> p c t", p=128)

    dbg = {}
    if stage == "ln1":
        dbg["xln"] = nc.dram_tensor("dbg_xln", [D, TALL], F32, kind="ExternalOutput").ap()
    if stage == "qkv":
        dbg["k"] = nc.dram_tensor("dbg_k", [DC, 128, TALL], F32, kind="ExternalOutput").ap()
        dbg["q"] = nc.dram_tensor("dbg_q", [DC, 128, TOWN], F32, kind="ExternalOutput").ap()
        dbg["v"] = nc.dram_tensor("dbg_v", [TALL, D], F32, kind="ExternalOutput").ap()
        dbg["negm"] = nc.dram_tensor("dbg_negm", [128, TOWN], F32, kind="ExternalOutput").ap()
    if stage == "attn":
        dbg["x2"] = nc.dram_tensor("dbg_x2", [DC, 128, TOWN], F32, kind="ExternalOutput").ap()
    if stage == "ffn0":
        dbg["x2ln"] = nc.dram_tensor("dbg_x2ln", [128, DC * TH], F32, kind="ExternalOutput").ap()
        dbg["coef"] = nc.dram_tensor("dbg_coef", [128, 1024], F32, kind="ExternalOutput").ap()
        dbg["acc"] = nc.dram_tensor("dbg_acc", [128, DC * TH], F32, kind="ExternalOutput").ap()
        dbg["h0"] = nc.dram_tensor("dbg_h0", [128, 4 * TH], F32, kind="ExternalOutput").ap()
        dbg["crep0"] = nc.dram_tensor("dbg_crep0", [128, TH], F32, kind="ExternalOutput").ap()
    if stage.startswith("qb"):
        for nm, shp in [("pt", [128, NKC * 512]), ("wb", [128, DC * 512]),
                        ("eps", [128, 512]), ("sm", [128, 512]), ("sms", [128, 512]),
                        ("sr", [128, 512]), ("xo", [128, DC * 512])]:
            dbg[nm] = nc.dram_tensor("dbg_" + nm, shp, F32, kind="ExternalOutput").ap()

    with ExitStack() as stack:
        arena = stack.enter_context(nc.sbuf_tensor("arena", [128, ARENA], mybir.dt.uint8))

        def carve(off, dtype, shape):
            esz = 2 if dtype == BF16 else 4
            nfree = 1
            for s in shape[1:]:
                nfree *= s
            v = arena[0:shape[0], off:off + nfree * esz].bitcast(dtype)
            if len(shape) == 2:
                return v
            return v.rearrange("p (a b) -> p a b", a=shape[1], b=shape[2])

        ones_bf = carve(O_ONES, BF16, [128, 128])
        eye_bf = carve(O_EYE, BF16, [128, 128])
        rout_b = carve(O_ROUT, BF16, [128, 8, 8])
        eps5 = carve(O_EPS5, F32, [128, 1])
        eps35 = carve(O_EPS35, F32, [128, 1])
        wq = carve(O_WQ, BF16, [128, DC, 3 * D])
        xln = carve(O_XLN, BF16, [128, DC, TALL])
        spill = [carve(O_SPILL[i], BF16, [128, 4, 512]) for i in range(2)]
        spill_f = [carve(O_SPILL[i], BF16, [128, 2048]) for i in range(2)]
        ksq = carve(O_KSQ, BF16, [128, 2048])
        xs = [carve(O_XS[i], F32, [128, DC, 512]) for i in range(2)]
        ubuf = carve(O_U, BF16, [128, DC, 512])
        a_m = carve(O_ASTATS, F32, [128, 512])
        a_ms = carve(O_ASTATS + 2 * KB, F32, [128, 512])
        a_t = carve(O_ASTATS + 4 * KB, F32, [128, 512])
        a_r = carve(O_ASTATS + 6 * KB, F32, [128, 512])
        a_rb = carve(O_ASTATS + 8 * KB, BF16, [128, 512])
        a_bb = carve(O_ASTATS + 9 * KB, BF16, [128, 512])
        kmax = carve(O_KMAX, F32, [128, 1])
        klog = carve(O_KMAX + 128, F32, [128, 1])
        qsq = carve(O_QSQ, BF16, [128, 2048])
        lnq = carve(O_LNQ, F32, [128, 2048])
        mrep = carve(O_MREP, F32, [128, 2048])
        negm = carve(O_NEGM, BF16, [128, 2048])
        mr3 = [carve(177 * KB + 512 + 4 * KB * i, F32, [128, 2, 512]) for i in range(4)]
        mr3all = carve(177 * KB + 512, F32, [128, 4096])
        mrh = [carve(200 * KB + 2 * KB * i, BF16, [128, 2, 512]) for i in range(2)]
        pt = carve(O_PT, BF16, [128, NKC, 512])
        kst = [carve(O_KST[i], BF16, [128, DC, 512]) for i in range(2)]
        qst = [carve(O_QST[i], BF16, [128, DC, 512]) for i in range(2)]
        xown = carve(O_XOWN, F32, [128, DC, 512])
        w_b = carve(O_WB, BF16, [128, DC, 512])
        uw = carve(O_UW, BF16, [128, DC, 512])
        s_m = carve(O_STATS, F32, [128, 512])
        s_ms = carve(O_STATS + 2 * KB, F32, [128, 512])
        s_eps = carve(O_STATS + 4 * KB, F32, [128, 512])
        s_r = carve(O_STATS + 6 * KB, F32, [128, 512])
        s_mb = carve(O_STATS + 8 * KB, BF16, [128, 512])
        s_rb = carve(O_STATS + 9 * KB, BF16, [128, 512])
        vbf = carve(O_VBF, BF16, [128, NKC, 1024])
        x2sl = carve(O_X2SL, F32, [128, DC, TH])
        x2ln = carve(O_X2LN, BF16, [128, DC, TH])
        acc = carve(O_ACC, F32, [128, DC, TH])
        wgb = [carve(O_WG[i], BF16, [128, DC, 512]) for i in range(2)]
        wub = [carve(O_WU[i], BF16, [128, DC, 512]) for i in range(2)]
        f_m = carve(O_FM, F32, [128, 512])
        f_ms = carve(O_FMS, F32, [128, 512])
        f_r = carve(O_FR, F32, [128, 512])
        f_t2 = carve(O_FT2, F32, [128, 512])
        f_t = carve(O_FT, F32, [128, 16])
        coef_tma = carve(O_FT + 64, BF16, [128, 64])
        lmax = carve(O_FT + 192, F32, [128, 1])
        lsum = carve(O_FT + 224, F32, [128, 1])
        lrec = carve(O_FT + 256, F32, [128, 1])
        coef_fm = carve(O_COEF, BF16, [128, 1024])
        sel_b = carve(O_SEL, BF16, [32, 8, 128])
        crep = [carve(O_CREP + 2 * KB * i, BF16, [128, 1024]) for i in range(2)]
        crep += [carve(196 * KB + 2 * KB * i, BF16, [128, 1024]) for i in range(2)]
        hb = [carve(O_H + 8 * KB * i, BF16, [128, 4, 1024]) for i in range(4)]
        wdb = [carve(O_WD + 8 * KB * i, BF16, [128, 4, 1024]) for i in range(4)]
        sqb = carve(O_H, BF16, [128, DC, 512])
        castb = carve(O_H + 8 * KB, BF16, [128, DC, 512])
        uwtmp = carve(O_UWT, BF16, [128, 2, 512])

        PS = [stack.enter_context(nc.psum_tensor("ps%d" % i, [128, 512], F32)) for i in range(8)]

        sPE = C(stack, nc, "sPE")
        sDV = C(stack, nc, "sDV")
        sAC = C(stack, nc, "sAC")
        sPL = C(stack, nc, "sPL")
        s_wdma = C(stack, nc, "s_wdma")
        s_xs = [C(stack, nc, "s_xs0"), C(stack, nc, "s_xs1")]
        s_ksp = C(stack, nc, "s_ksp")
        s_qsp = C(stack, nc, "s_qsp")
        s_kst = [C(stack, nc, "s_kst0"), C(stack, nc, "s_kst1")]
        s_qst = [C(stack, nc, "s_qst0"), C(stack, nc, "s_qst1")]
        s_xown = C(stack, nc, "s_xown")
        s_x2w = C(stack, nc, "s_x2w")
        s_x2r = C(stack, nc, "s_x2r")
        s_wg = [C(stack, nc, "s_wg0"), C(stack, nc, "s_wg1")]
        s_wu = [C(stack, nc, "s_wu0"), C(stack, nc, "s_wu1")]
        s_wd = [C(stack, nc, "s_wd%d" % i) for i in range(4)]
        s_y = C(stack, nc, "s_y")
        s_mr = C(stack, nc, "s_mr")
        s_dbg = C(stack, nc, "s_dbg")

        block = stack.enter_context(nc.Block())

        def on(name, f):
            getattr(block, name)(f)

        def _(g):
            g.memset(ones_bf, 1.0)
            g.memset(eps5, 1e-5)
            g.memset(eps35, 1e-35)
            g.engine_nop().then_inc(sPL.h, 1)
            g.dma_start(out=eye_bf, in_=eye_in).then_inc(s_wdma.h, 16)
            g.dma_start(out=rout_b, in_=router.rearrange("(c p) e -> p c e", p=128)).then_inc(s_wdma.h, 16)
            g.dma_start(out=sel_b, in_=sel_in.rearrange("p (a b) -> p a b", a=8, b=128)).then_inc(s_wdma.h, 16)
            for c in range(DC):
                g.dma_start(out=wq[:, c, :], in_=wqkvT[c * 128:(c + 1) * 128, :]).then_inc(s_wdma.h, 16)
        on("gpsimd", _)
        sPL.n += 1
        s_wdma.n += 16 * (3 + DC)
        WQ_READY = s_wdma.n
        ONES_READY = sPL.n

        _sid = nc.enter_named_scope("phA", False)[0]
        xb_r = xb.rearrange("(c p) t -> p c t", p=128)
        apply_done = [0] * 8
        prev_stats = [0]

        for s in range(8):
            p = s % 2

            def _(sy, s=s, p=p):
                if s >= 2:
                    sy.wait_ge(sDV.h, apply_done[s - 2])
                sy.dma_start(out=xs[p][:, 0:4, :],
                             in_=xb_r[:, 0:4, s * 512:(s + 1) * 512]).then_inc(s_xs[p].h, 16)
                sy.dma_start(out=xs[p][:, 4:8, :],
                             in_=xb_r[:, 4:8, s * 512:(s + 1) * 512]).then_inc(s_xs[p].h, 16)
            on("sync", _)
            s_xs[p].n += 32
            load_half = [s_xs[p].n - 16, s_xs[p].n]

            cast_half = [0, 0]
            for hh in range(2):
                def _(v, s=s, p=p, hh=hh, w=load_half[hh]):
                    v.wait_ge(s_xs[p].h, w)
                    if s == 0 and hh == 0:
                        v.wait_ge(sPL.h, ONES_READY)
                    v.tensor_copy(xln[:, 4 * hh:4 * hh + 4, s * 512:(s + 1) * 512],
                                  xs[p][:, 4 * hh:4 * hh + 4, :]).then_inc(sDV.h, 1)
                on("vector", _)
                sDV.n += 1
                cast_half[hh] = sDV.n
            cast_done = sDV.n

            sq_half = [0, 0]
            for hh in range(2):
                def _(sc, s=s, hh=hh, w=cast_half[hh], pstats=prev_stats[0]):
                    sc.wait_ge(sDV.h, w)
                    if hh == 0 and pstats:
                        sc.wait_ge(sPE.h, pstats)
                    sc.square(ubuf[:, 4 * hh:4 * hh + 4, :],
                              xln[:, 4 * hh:4 * hh + 4, s * 512:(s + 1) * 512]).then_inc(sAC.h, 1)
                on("scalar", _)
                sAC.n += 1
                sq_half[hh] = sAC.n
            sq_done = sAC.n

            def _(t, s=s, cast_half=tuple(cast_half), sq_half=tuple(sq_half)):
                xls = xln[:, :, s * 512:(s + 1) * 512]
                t.wait_ge(sDV.h, cast_half[0])
                for dc in range(4):
                    t.matmul(PS[0][:], ones_bf, xls[:, dc, :], start=(dc == 0), stop=False)
                t.wait_ge(sDV.h, cast_half[1])
                for dc in range(4, DC):
                    t.matmul(PS[0][:], ones_bf, xls[:, dc, :], start=False, stop=(dc == DC - 1))
                t.wait_ge(sAC.h, sq_half[0])
                for dc in range(4):
                    t.matmul(PS[1][:], ones_bf, ubuf[:, dc, :], start=(dc == 0), stop=False)
                t.wait_ge(sAC.h, sq_half[1])
                for dc in range(4, DC):
                    mm = t.matmul(PS[1][:], ones_bf, ubuf[:, dc, :], start=False, stop=(dc == DC - 1))
                mm.then_inc(sPE.h, 1)
            on("tensor", _)
            sPE.n += 1
            stats_done = sPE.n
            prev_stats[0] = stats_done

            def _(v, stats_done=stats_done):
                v.wait_ge(sPE.h, stats_done)
                v.tensor_scalar(a_m[:], PS[0][:], 1.0 / D, None, ALU.mult)
                v.tensor_scalar(a_ms[:], PS[1][:], 1.0 / D, None, ALU.mult)
                v.tensor_tensor(a_t[:], a_m[:], a_m[:], ALU.mult)
                v.tensor_tensor(a_ms[:], a_ms[:], a_t[:], ALU.subtract).then_inc(sDV.h, 1)
            on("vector", _)
            sDV.n += 1
            var_done = sDV.n

            def _(sc, var_done=var_done):
                sc.wait_ge(sDV.h, var_done)
                if var_done <= 3:
                    sc.wait_ge(sPL.h, 1)
                sc.activation(a_t[:], a_ms[:], AF.Ln, bias=eps5[:])
                sc.activation(a_r[:], a_t[:], AF.Exp, scale=-0.5).then_inc(sAC.h, 1)
            on("scalar", _)
            sAC.n += 1
            rstd_done = sAC.n

            def _(v, s=s, rstd_done=rstd_done):
                v.wait_ge(sAC.h, rstd_done)
                v.tensor_copy(a_rb[:], a_r[:])
                v.tensor_tensor(a_t[:], a_m[:], a_r[:], ALU.mult)
                v.tensor_scalar(a_bb[:], a_t[:], -1.0, None, ALU.mult)
                xls = xln[:, :, s * 512:(s + 1) * 512]
                for dc in range(DC):
                    v.tensor_tensor(xls[:, dc, :], xls[:, dc, :], a_rb[:], ALU.mult)
                    mm = v.tensor_tensor(xls[:, dc, :], xls[:, dc, :], a_bb[:], ALU.add)
                mm.then_inc(sDV.h, 1)
            on("vector", _)
            sDV.n += 1
            apply_done[s] = sDV.n

        nc.leave_named_scope("phA", _sid, False)
        LN1_DONE = apply_done[7]

        if stage == "ln1":
            def _(g):
                g.wait_ge(sDV.h, LN1_DONE)
                g.dma_start(out=dbg["xln"].rearrange("(c p) t -> p c t", p=128),
                            in_=xln[:]).then_inc(s_dbg.h, 16)
            on("gpsimd", _)
            s_dbg.n += 16

            def _(sy):
                sy.wait_ge(s_dbg.h, s_dbg.n)
            on("sync", _)
            return nc

        _sid = nc.enter_named_scope("phK", False)[0]
        kss_free_dv = 0
        kq_evac_pair = [0, 0]
        pending_kss = [None]
        for kg in range(2):
            for fc in range(DC):
                b = fc % 2
                for half in range(2):
                    def _(t, kg=kg, fc=fc, half=half, ge=kq_evac_pair[half]):
                        if kg == 0 and fc == 0 and half == 0:
                            t.wait_ge(s_wdma.h, WQ_READY)
                            t.wait_ge(sDV.h, LN1_DONE)
                        if ge:
                            t.wait_ge(sAC.h, ge)
                        for dc in range(DC):
                            for k2 in range(2):
                                ks = half * 2 + k2
                                mm = t.matmul(PS[ks][:], wq[:, dc, D + fc * 128:D + (fc + 1) * 128],
                                              xln[:, dc, kg * 2048 + ks * 512:kg * 2048 + (ks + 1) * 512],
                                              start=(dc == 0), stop=(dc == DC - 1))
                        mm.then_inc(sPE.h, 1)
                    on("tensor", _)
                    sPE.n += 1
                    prod_done = sPE.n
                    if half == 0 and pending_kss[0] is not None:
                        pending_kss[0]()
                        pending_kss[0] = None

                    def _(sc, b=b, half=half, prod_done=prod_done,
                          ws=(s_ksp.n if b == 0 else s_qsp.n)):
                        sc.wait_ge(sPE.h, prod_done)
                        if half == 0:
                            sc.wait_ge((s_ksp if b == 0 else s_qsp).h, ws)
                        for k2 in range(2):
                            ks = half * 2 + k2
                            cp = sc.copy(spill[b][:, ks, :], PS[ks][:])
                        cp.then_inc(sAC.h, 1)
                    on("scalar", _)
                    sAC.n += 1
                    kq_evac_pair[half] = sAC.n
                evac_done = sAC.n

                def _(sy, kg=kg, fc=fc, b=b, evac_done=evac_done):
                    sy.wait_ge(sAC.h, evac_done)
                    sy.wait_ge(sDV.h, sDV.n)
                    sy.dma_start(out=k_hbm[fc, :, kg * 2048:(kg + 1) * 2048],
                                 in_=spill_f[b]).then_inc((s_ksp if b == 0 else s_qsp).h, 16)
                on("sync", _)
                if b == 0:
                    s_ksp.n += 16
                else:
                    s_qsp.n += 16

                def _(v, b=b, evac_done=evac_done):
                    v.wait_ge(sAC.h, evac_done)
                    v.wait_ge(sPE.h, sPE.n)
                    v.tensor_tensor(ksq[:], spill_f[b], spill_f[b], ALU.mult).then_inc(sDV.h, 1)
                on("vector", _)
                sDV.n += 1
                ksq_done = sDV.n

                def emit_kss(fc=fc, ksq_done=ksq_done, kss_free_dv=kss_free_dv):
                    def _(t):
                        t.wait_ge(sDV.h, ksq_done)
                        if fc == 0:
                            t.wait_ge(sDV.h, kss_free_dv)
                        for ks in range(4):
                            mm = t.matmul(PS[4 + ks][:], ones_bf, ksq[:, ks * 512:(ks + 1) * 512],
                                          start=(fc == 0), stop=(fc == DC - 1))
                        mm.then_inc(sPE.h, 1)
                    on("tensor", _)
                    sPE.n += 1
                pending_kss[0] = emit_kss

            pending_kss[0]()
            pending_kss[0] = None
            kss_mm_done = sPE.n

            def _(v, kg=kg, kss_mm_done=kss_mm_done, base=sDV.n):
                v.wait_ge(sPE.h, kss_mm_done)
                n = base
                for ks in range(4):
                    v.reduce_max(a_m[:, 0:1], PS[4 + ks][:], axis=AX.X).then_inc(sDV.h, 1)
                    n += 1
                    v.wait_ge(sDV.h, n)
                    if kg == 0 and ks == 0:
                        op = v.tensor_copy(kmax[:], a_m[:, 0:1])
                    else:
                        op = v.tensor_tensor(kmax[:], kmax[:], a_m[:, 0:1], ALU.max)
                    op.then_inc(sDV.h, 1)
                    n += 1
                    v.wait_ge(sDV.h, n)
            on("vector", _)
            sDV.n += 8
            kss_free_dv = sDV.n

        nc.leave_named_scope("phK", _sid, False)
        _sid = nc.enter_named_scope("phV", False)[0]
        v_evac = [0, 0]
        for tt in range(NKC):
            bk = (tt % 2) * 2
            def _(t, tt=tt, bk=bk, ge=v_evac[tt % 2]):
                if ge:
                    t.wait_ge(sAC.h, ge)
                if tt < 2:
                    t.wait_ge(sAC.h, sAC.n)
                    t.wait_ge(sDV.h, sDV.n)
                for dc in range(DC):
                    for fs in range(2):
                        mm = t.matmul(PS[bk + fs][:], xln[:, dc, tt * 128:(tt + 1) * 128],
                                      wq[:, dc, 2 * D + fs * 512:2 * D + (fs + 1) * 512],
                                      start=(dc == 0), stop=(dc == DC - 1))
                mm.then_inc(sPE.h, 1)
            on("tensor", _)
            sPE.n += 1
            vprod = sPE.n

            def _(sc, tt=tt, bk=bk, vprod=vprod, wk=s_ksp.n):
                sc.wait_ge(sPE.h, vprod)
                if tt < 2:
                    sc.wait_ge(s_ksp.h, wk)
                for fs in range(2):
                    cp = sc.copy(vbf[:, tt, fs * 512:(fs + 1) * 512], PS[bk + fs][:])
                cp.then_inc(sAC.h, 1)
            on("scalar", _)
            sAC.n += 1
            v_evac[tt % 2] = sAC.n

        nc.leave_named_scope("phV", _sid, False)
        V_DONE_AC = sAC.n

        _sid = nc.enter_named_scope("phQ", False)[0]
        for fc in range(DC):
            b = fc % 2
            for half in range(2):
                def _(t, fc=fc, half=half, ge=kq_evac_pair[half]):
                    if fc == 0 and half == 0:
                        t.wait_ge(sAC.h, sAC.n)
                    if ge:
                        t.wait_ge(sAC.h, ge)
                    if half == 0 and fc > 0:
                        t.wait_ge(sDV.h, sDV.n)
                    for dc in range(DC):
                        for k2 in range(2):
                            qs = half * 2 + k2
                            mm = t.matmul(PS[qs][:], wq[:, dc, fc * 128:(fc + 1) * 128],
                                          xln[:, dc, qs * 512:(qs + 1) * 512],
                                          start=(dc == 0), stop=(dc == DC - 1))
                    mm.then_inc(sPE.h, 1)
                on("tensor", _)
                sPE.n += 1
                prod_done = sPE.n
                if half == 0 and pending_kss[0] is not None:
                    pending_kss[0]()
                    pending_kss[0] = None

                def _(sc, b=b, half=half, prod_done=prod_done,
                      ws=(s_ksp.n if b == 0 else s_qsp.n)):
                    sc.wait_ge(sPE.h, prod_done)
                    if half == 0:
                        sc.wait_ge((s_ksp if b == 0 else s_qsp).h, ws)
                    for k2 in range(2):
                        qs = half * 2 + k2
                        cp = sc.activation(spill[b][:, qs, :], PS[qs][:], AF.Copy, scale=0.03125)
                    cp.then_inc(sAC.h, 1)
                on("scalar", _)
                sAC.n += 1
                kq_evac_pair[half] = sAC.n
            evac_done = sAC.n

            def _(sy, fc=fc, b=b, evac_done=evac_done):
                sy.wait_ge(sAC.h, evac_done)
                sy.wait_ge(sDV.h, sDV.n)
                sy.dma_start(out=q_hbm[fc, :, :], in_=spill_f[b]).then_inc((s_ksp if b == 0 else s_qsp).h, 16)
            on("sync", _)
            if b == 0:
                s_ksp.n += 16
            else:
                s_qsp.n += 16

            def _(v, b=b, evac_done=evac_done):
                v.wait_ge(sAC.h, evac_done)
                v.wait_ge(sPE.h, sPE.n)
                v.tensor_tensor(qsq[:], spill_f[b], spill_f[b], ALU.mult).then_inc(sDV.h, 1)
            on("vector", _)
            sDV.n += 1
            qsq_done = sDV.n

            def emit_qss(fc=fc, qsq_done=qsq_done):
                def _(t):
                    t.wait_ge(sDV.h, qsq_done)
                    for qs in range(4):
                        mm = t.matmul(PS[4 + qs][:], ones_bf, qsq[:, qs * 512:(qs + 1) * 512],
                                      start=(fc == 0), stop=(fc == DC - 1))
                    mm.then_inc(sPE.h, 1)
                on("tensor", _)
                sPE.n += 1
            pending_kss[0] = emit_qss

        pending_kss[0]()
        pending_kss[0] = None
        Q_DONE_PE = sPE.n

        nc.leave_named_scope("phQ", _sid, False)
        _sid = nc.enter_named_scope("phNegm", False)[0]
        def _(sc, w0=s_ksp.n, w1=s_qsp.n):
            sc.wait_ge(sPE.h, Q_DONE_PE)
            sc.wait_ge(s_ksp.h, w0)
            sc.wait_ge(s_qsp.h, w1)
            for qs in range(4):
                sc.activation(lnq[:, qs * 512:(qs + 1) * 512], PS[4 + qs][:], AF.Ln)
            sc.activation(klog[:], kmax[:], AF.Ln).then_inc(sAC.h, 1)
        on("scalar", _)
        sAC.n += 1
        lnq_done = sAC.n

        def _(v, lnq_done=lnq_done):
            v.wait_ge(sAC.h, lnq_done)
            v.tensor_scalar(klog[:], klog[:], 0.5, None, ALU.mult)
            v.engine_nop().then_inc(sDV.h, 1)
        on("vector", _)
        sDV.n += 1
        klog_done = sDV.n

        def _(sc, klog_done=klog_done):
            sc.wait_ge(sDV.h, klog_done)
            sc.activation(mrep[:], lnq[:], AF.Exp, scale=0.5, bias=klog[:]).then_inc(sAC.h, 1)
        on("scalar", _)
        sAC.n += 1
        m_done = sAC.n

        def _(v, m_done=m_done):
            v.wait_ge(sAC.h, m_done)
            v.tensor_scalar(negm[:], mrep[:], -1.0, 19.0, ALU.mult, ALU.add)
            v.engine_nop().then_inc(sDV.h, 1)
        on("vector", _)
        sDV.n += 1
        NEGM_DONE = sDV.n

        nc.leave_named_scope("phNegm", _sid, False)
        if stage == "qkv":
            def _(g, wk=s_ksp.n, wqq=s_qsp.n):
                g.wait_ge(sDV.h, NEGM_DONE)
                g.wait_ge(s_ksp.h, wk)
                g.wait_ge(s_qsp.h, wqq)
                g.dma_start(out=dbg["k"], in_=k_hbm).then_inc(s_dbg.h, 16)
                g.dma_start(out=dbg["q"], in_=q_hbm).then_inc(s_dbg.h, 16)
                g.dma_start(out=dbg["v"].rearrange("(a p) f -> p a f", p=128), in_=vbf[:]).then_inc(s_dbg.h, 16)
                g.dma_start(out=dbg["negm"], in_=negm[:]).then_inc(s_dbg.h, 16)
            on("gpsimd", _)
            s_dbg.n += 64

            def _(sy):
                sy.wait_ge(s_dbg.h, s_dbg.n)
            on("sync", _)
            return nc

        K_SPILLED = s_ksp.n
        Q_SPILLED = s_qsp.n
        kst_pe_free = [0, 0]
        qst_pe_free = [0, 0]
        pt_pe_free = 0
        wb_dv_free = 0
        x2b_cast_prev = [0]
        bank_evac = [0, 0, 0]
        ST_BANKS = [0, 1, 7]
        pv_bank_evac = [0, 0]

        for qb in range(NQB):
            _sqb = nc.enter_named_scope("qb%d" % qb, False)[0]
            def _(sy, qb=qb, w=qst_pe_free[qb % 2]):
                if qb == 0:
                    sy.wait_ge(sDV.h, NEGM_DONE)
                    sy.wait_ge(s_ksp.h, K_SPILLED)
                    sy.wait_ge(s_qsp.h, Q_SPILLED)
                sy.wait_ge(sPE.h, w)
                sy.dma_start(out=qst[qb % 2],
                             in_=q_hbm[:, :, qb * 512:(qb + 1) * 512].rearrange("c p t -> p c t")
                             ).then_inc(s_qst[qb % 2].h, 16)
            on("sync", _)
            s_qst[qb % 2].n += 16
            qst_ready = s_qst[qb % 2].n

            for kh in range(8):
                def _(sy, kh=kh, w=kst_pe_free[kh % 2]):
                    sy.wait_ge(sPE.h, w)
                    sy.dma_start(out=kst[kh % 2],
                                 in_=k_hbm[:, :, kh * 512:(kh + 1) * 512].rearrange("c p t -> p c t")
                                 ).then_inc(s_kst[kh % 2].h, 16)
                on("sync", _)
                s_kst[kh % 2].n += 16
                kr = s_kst[kh % 2].n

                for kk in range(4):
                    kc = kh * 4 + kk
                    bank_guard = bank_evac[kc % 3]

                    def _(t, qb=qb, kh=kh, kk=kk, kc=kc, qst_ready=qst_ready, kr=kr,
                          bank_guard=bank_guard, pt_guard=pt_pe_free):
                        if kk == 0:
                            t.wait_ge(s_kst[kh % 2].h, kr)
                            if kh == 0:
                                t.wait_ge(s_qst[qb % 2].h, qst_ready)
                        if bank_guard:
                            t.wait_ge(sAC.h, bank_guard)
                        bk = PS[ST_BANKS[kc % 3]]
                        for dc in range(DC):
                            t.matmul(bk[:], kst[kh % 2][:, dc, kk * 128:(kk + 1) * 128],
                                     qst[qb % 2][:, dc, :], start=(dc == 0), stop=False)
                        mm = t.matmul(bk[:], ones_bf[0:1, :],
                                      negm[0:1, qb * 512:(qb + 1) * 512], start=False, stop=True)
                        mm.then_inc(sPE.h, 1)
                    on("tensor", _)
                    sPE.n += 1
                    st_done = sPE.n

                    def _(sc, kc=kc, st_done=st_done, pt_guard=pt_pe_free):
                        sc.wait_ge(sPE.h, st_done)
                        if kc == 0 and pt_guard:
                            sc.wait_ge(sPE.h, pt_guard)
                        sc.activation(pt[:, kc, :], PS[ST_BANKS[kc % 3]][:], AF.Exp).then_inc(sAC.h, 1)
                    on("scalar", _)
                    sAC.n += 1
                    bank_evac[kc % 3] = sAC.n
                kst_pe_free[kh % 2] = sPE.n
            qst_pe_free[qb % 2] = sPE.n
            EXP_DONE = sAC.n

            def _(t, EXP_DONE=EXP_DONE):
                t.wait_ge(sAC.h, EXP_DONE)
                t.wait_ge(sDV.h, sDV.n)
                for kc in range(NKC):
                    mm = t.matmul(PS[2][:], ones_bf, pt[:, kc, :], start=(kc == 0), stop=(kc == NKC - 1))
                mm.then_inc(sPE.h, 1)
            on("tensor", _)
            sPE.n += 1
            sum_done = sPE.n

            def _(v, sum_done=sum_done, wb_dv_free=wb_dv_free):
                v.wait_ge(sPE.h, sum_done)
                v.wait_ge(sDV.h, wb_dv_free)
                v.tensor_scalar(s_r[:], PS[2][:], 1.0, None, ALU.mult).then_inc(sDV.h, 1)
            on("vector", _)
            sDV.n += 1
            sum_ev = sDV.n

            def _(v, sum_ev=sum_ev):
                v.wait_ge(sDV.h, sum_ev)
                v.reciprocal(s_eps[:], s_r[:])
                v.engine_nop().then_inc(sDV.h, 1)
            on("vector", _)
            sDV.n += 1
            eps_done = sDV.n

            def _(sy, qb=qb, w=s_x2w.n, wc=x2b_cast_prev[0]):
                sy.wait_ge(s_x2w.h, w)
                if wc:
                    sy.wait_ge(sDV.h, wc)
                sy.dma_start(out=xown, in_=xb_r[:, :, qb * 512:(qb + 1) * 512]).then_inc(s_xown.h, 16)
            on("sync", _)
            s_xown.n += 16
            xown_ready = s_xown.n

            sq_done_fs = [0] * DC
            for fs in range(DC):
                def _(t, fs=fs, eps_done=eps_done, bg=pv_bank_evac[fs % 2]):
                    if fs == 0:
                        t.wait_ge(sDV.h, eps_done)
                        t.wait_ge(sAC.h, V_DONE_AC)
                    if bg:
                        t.wait_ge(sDV.h, bg)
                    for kc in range(NKC):
                        mm = t.matmul(PS[3 + fs % 2][:], vbf[:, kc, fs * 128:(fs + 1) * 128],
                                      pt[:, kc, :], start=(kc == 0), stop=(kc == NKC - 1))
                    mm.then_inc(sPE.h, 1)
                on("tensor", _)
                sPE.n += 1
                pv_done = sPE.n

                def _(v, fs=fs, pv_done=pv_done):
                    v.wait_ge(sPE.h, pv_done)
                    v.tensor_tensor(w_b[:, fs, :], PS[3 + fs % 2][:], s_eps[:], ALU.mult)
                    v.tensor_tensor(uw[:, fs, :], w_b[:, fs, :], w_b[:, fs, :], ALU.mult).then_inc(sDV.h, 1)
                on("vector", _)
                sDV.n += 1
                pv_bank_evac[fs % 2] = sDV.n
                sq_done = sDV.n

                sq_done_fs[fs] = sq_done
                if fs > 0:
                    def _(t, fs=fs, sq_done=sq_done_fs[fs - 1]):
                        t.wait_ge(sDV.h, sq_done)
                        t.matmul(PS[5][:], ones_bf, w_b[:, fs - 1, :], start=(fs - 1 == 0), stop=False)
                        mm = t.matmul(PS[6][:], ones_bf, uw[:, fs - 1, :], start=(fs - 1 == 0), stop=False)
                        mm.then_inc(sPE.h, 1)
                    on("tensor", _)
                    sPE.n += 1

            def _(t, sq_done=sq_done_fs[DC - 1]):
                t.wait_ge(sDV.h, sq_done)
                t.matmul(PS[5][:], ones_bf, w_b[:, DC - 1, :], start=False, stop=True)
                mm = t.matmul(PS[6][:], ones_bf, uw[:, DC - 1, :], start=False, stop=True)
                mm.then_inc(sPE.h, 1)
            on("tensor", _)
            sPE.n += 1
            pt_pe_free = sPE.n
            ln2_mm_done = sPE.n

            def _(v, ln2_mm_done=ln2_mm_done):
                v.wait_ge(sPE.h, ln2_mm_done)
                v.tensor_scalar(s_m[:], PS[5][:], 1.0 / D, None, ALU.mult)
                v.tensor_scalar(s_ms[:], PS[6][:], 1.0 / D, None, ALU.mult)
                v.tensor_tensor(s_r[:], s_m[:], s_m[:], ALU.mult)
                v.tensor_tensor(s_ms[:], s_ms[:], s_r[:], ALU.subtract)
                v.engine_nop().then_inc(sDV.h, 1)
            on("vector", _)
            sDV.n += 1
            var2_done = sDV.n

            def _(sc, var2_done=var2_done):
                sc.wait_ge(sDV.h, var2_done)
                sc.activation(s_ms[:], s_ms[:], AF.Ln, bias=eps5[:])
                sc.activation(s_r[:], s_ms[:], AF.Exp, scale=-0.5).then_inc(sAC.h, 1)
            on("scalar", _)
            sAC.n += 1
            rstd2_done = sAC.n

            def _(v, rstd2_done=rstd2_done, xown_ready=xown_ready):
                v.wait_ge(sAC.h, rstd2_done)
                v.wait_ge(s_xown.h, xown_ready)
                v.tensor_copy(s_mb[:], s_m[:])
                v.tensor_copy(s_rb[:], s_r[:])
                for fs in range(DC):
                    v.tensor_tensor(uw[:, fs, :], w_b[:, fs, :], s_mb[:], ALU.subtract)
                    v.tensor_tensor(uw[:, fs, :], uw[:, fs, :], s_rb[:], ALU.mult)
                    mm = v.tensor_tensor(xown[:, fs, :], xown[:, fs, :], uw[:, fs, :], ALU.add)
                mm.then_inc(sDV.h, 1)
            on("vector", _)
            sDV.n += 1
            wb_dv_free = sDV.n
            x2_done = sDV.n

            def _(v, x2_done=x2_done):
                v.wait_ge(sDV.h, x2_done)
                v.tensor_copy(uw[:], xown[:]).then_inc(sDV.h, 1)
            on("vector", _)
            sDV.n += 1
            x2b_done = sDV.n
            x2b_cast_prev[0] = x2b_done

            def _(sc, x2b_done=x2b_done):
                sc.wait_ge(sDV.h, x2b_done)
                sc.square(w_b[:], uw[:]).then_inc(sAC.h, 1)
            on("scalar", _)
            sAC.n += 1
            sq3q_done = sAC.n

            def _(t, x2b_done=x2b_done, sq3q_done=sq3q_done):
                t.wait_ge(sDV.h, x2b_done)
                for dc in range(DC):
                    t.matmul(PS[5][:], ones_bf, uw[:, dc, :], start=(dc == 0), stop=(dc == DC - 1))
                t.wait_ge(sAC.h, sq3q_done)
                for dc in range(DC):
                    mm = t.matmul(PS[6][:], ones_bf, w_b[:, dc, :], start=(dc == 0), stop=(dc == DC - 1))
                mm.then_inc(sPE.h, 1)
            on("tensor", _)
            sPE.n += 1
            st3q_done = sPE.n

            def _(v, qb=qb, st3q_done=st3q_done):
                v.wait_ge(sPE.h, st3q_done)
                v.tensor_scalar(mr3[qb][:, 0, :], PS[5][:], 1.0 / D, None, ALU.mult)
                v.tensor_scalar(s_ms[:], PS[6][:], 1.0 / D, None, ALU.mult)
                v.tensor_tensor(s_r[:], mr3[qb][:, 0, :], mr3[qb][:, 0, :], ALU.mult)
                v.tensor_tensor(s_ms[:], s_ms[:], s_r[:], ALU.subtract).then_inc(sDV.h, 1)
            on("vector", _)
            sDV.n += 1
            var3q = sDV.n

            def _(sc, qb=qb, var3q=var3q):
                sc.wait_ge(sDV.h, var3q)
                sc.activation(s_ms[:], s_ms[:], AF.Ln, bias=eps5[:])
                sc.activation(mr3[qb][:, 1, :], s_ms[:], AF.Exp, scale=-0.5).then_inc(sAC.h, 1)
            on("scalar", _)
            sAC.n += 1
            wb_dv_free = sDV.n
            MR3_AC = sAC.n

            if qb in (1, 3):
                def _(g, qb=qb, w=MR3_AC):
                    g.wait_ge(sAC.h, w)
                    g.dma_start(out=mr_hbm[:, (qb - 1) * 1024:(qb + 1) * 1024],
                                in_=mr3all[:, (qb - 1) * 1024:(qb + 1) * 1024]).then_inc(s_mr.h, 16)
                on("gpsimd", _)
                s_mr.n += 16

            def _(g, qb=qb, x2_done=x2_done):
                g.wait_ge(sDV.h, x2_done)
                g.dma_start(out=x2_hbm[:, :, qb * 512:(qb + 1) * 512].rearrange("c p t -> p c t"),
                             in_=xown[:]).then_inc(s_x2w.h, 16)
            on("gpsimd", _)
            s_x2w.n += 16
            nc.leave_named_scope("qb%d" % qb, _sqb, False)

            if stage.startswith("qb") and qb == int(stage[2:]):
                def _(g, w=s_x2w.n):
                    g.wait_ge(s_x2w.h, w)
                    g.dma_start(out=dbg["pt"], in_=pt.rearrange("p a b -> p (a b)")).then_inc(s_dbg.h, 16)
                    g.dma_start(out=dbg["wb"], in_=w_b.rearrange("p a b -> p (a b)")).then_inc(s_dbg.h, 16)
                    g.dma_start(out=dbg["eps"], in_=s_eps).then_inc(s_dbg.h, 16)
                    g.dma_start(out=dbg["sm"], in_=s_m).then_inc(s_dbg.h, 16)
                    g.dma_start(out=dbg["sms"], in_=s_ms).then_inc(s_dbg.h, 16)
                    g.dma_start(out=dbg["sr"], in_=s_r).then_inc(s_dbg.h, 16)
                    g.dma_start(out=dbg["xo"], in_=xown.rearrange("p a b -> p (a b)")).then_inc(s_dbg.h, 16)
                on("gpsimd", _)
                s_dbg.n += 16 * 7

                def _(sy, w=s_dbg.n):
                    sy.wait_ge(s_dbg.h, w)
                on("sync", _)
                return nc


        PT_FREE_QB3 = pt_pe_free
        ATTN_PE = sPE.n
        ATTN_DV = sDV.n
        ATTN_AC3 = sAC.n
        X2_STORED = s_x2w.n

        if stage == "attn":
            def _(g):
                g.wait_ge(s_x2w.h, X2_STORED)
                g.dma_start(out=dbg["x2"], in_=x2_hbm).then_inc(s_dbg.h, 16)
            on("gpsimd", _)
            s_dbg.n += 16

            def _(sy):
                sy.wait_ge(s_dbg.h, s_dbg.n)
            on("sync", _)
            return nc

        def pe_srcs(j):
            if j < 8:
                return (eg[j].rearrange("(c p) d -> p c d", p=128),
                        eu[j].rearrange("(c p) d -> p c d", p=128),
                        ed[j].rearrange("(c p) hh -> p c hh", p=128), j)
            jj = j - 8
            return (sg[:, jj * 512:(jj + 1) * 512].rearrange("(c p) d -> p c d", p=128),
                    su[:, jj * 512:(jj + 1) * 512].rearrange("(c p) d -> p c d", p=128),
                    sd[jj * 512:(jj + 1) * 512, :].rearrange("(c p) hh -> p c hh", p=128), None)

        groups = [[8, 9, 0, 1], [2, 3, 4, 5], [6, 7]]

        wg_free = [0, 0]
        wu_free = [0, 0]
        wd_free = [0] * 4
        crep_free = [0, 0]
        gu_evac = [(0, 0), (0, 0)]
        down_evac = [(0, 0), (0, 0)]

        first_ffn_load = [True]
        for hf in range(2):
            _shf = nc.enter_named_scope("ffn_pre%d" % hf, False)[0]
            x2_ready_ts = [0, 0]
            for ts in range(2):
                def _(sy, hf=hf, ts=ts, wy=s_y.n):
                    if hf == 0:
                        if ts == 0:
                            sy.wait_ge(s_x2w.h, 32)
                            sy.wait_ge(sPE.h, PT_FREE_QB3)
                    else:
                        if ts == 0:
                            sy.wait_ge(s_x2w.h, X2_STORED)
                            sy.wait_ge(s_y.h, wy)
                            sy.wait_ge(sDV.h, sDV.n)
                    sy.dma_start(out=x2sl[:, :, ts * 512:(ts + 1) * 512],
                                 in_=x2_hbm[:, :, hf * TH + ts * 512:hf * TH + (ts + 1) * 512]
                                 .rearrange("c p t -> p c t")).then_inc(s_x2r.h, 16)
                on("sync", _)
                s_x2r.n += 16
                x2_ready_ts[ts] = s_x2r.n
            x2_ready = s_x2r.n

            cast3_ts = [0, 0]
            for ts in range(2):
                def _(v, ts=ts, w=x2_ready_ts[ts], hf=hf):
                    v.wait_ge(s_x2r.h, w)
                    if hf == 0 and ts == 0:
                        v.memset(coef_fm[0:32, :], 0.0)
                    v.tensor_copy(x2ln[:, :, ts * 512:(ts + 1) * 512],
                                  x2sl[:, :, ts * 512:(ts + 1) * 512]).then_inc(sDV.h, 1)
                on("vector", _)
                sDV.n += 1
                cast3_ts[ts] = sDV.n
            cast3_done = sDV.n

            def _(g, hf=hf, w=(16 if hf == 0 else s_mr.n), wu=sDV.n):
                g.wait_ge(s_mr.h, w)
                g.wait_ge(sDV.h, wu)
                for ts in range(2):
                    g.dma_start(out=mrh[ts],
                                in_=mr_hbm[:, (hf * 2 + ts) * 1024:(hf * 2 + ts + 1) * 1024]
                                .rearrange("p (a b) -> p a b", a=2, b=512)).then_inc(s_mr.h, 16)
            on("gpsimd", _)
            s_mr.n += 32
            mr_ready = s_mr.n

            apply_ts = [0, 0]
            for ts in range(2):
                def _(v, ts=ts, cast3_done=cast3_ts[ts], mr_ready=mr_ready):
                    v.wait_ge(sDV.h, cast3_done)
                    v.wait_ge(s_mr.h, mr_ready)
                    for dc in range(DC):
                        sl = x2ln[:, dc, ts * 512:(ts + 1) * 512]
                        v.tensor_tensor(sl, sl, mrh[ts][:, 0, :], ALU.subtract)
                        mm = v.tensor_tensor(sl, sl, mrh[ts][:, 1, :], ALU.mult)
                    mm.then_inc(sDV.h, 1)
                on("vector", _)
                sDV.n += 1
                apply_ts[ts] = sDV.n
            x2ln_apply = sDV.n

            rst = {"coef_done_dv": 0, "router_ac": 0, "logits_pe": 0}
            top2_done = [0] * 8
            pexp_done_tt = [0] * 8

            def emit_logits(apply_ts=apply_ts):
                for half4 in range(2):
                    def _(t, half4=half4, w=apply_ts[half4]):
                        t.wait_ge(sDV.h, w)
                        n = 0
                        for tt in range(half4 * 4, half4 * 4 + 4):
                            for dc in range(DC):
                                n += 1
                                mm = t.matmul(PS[6][:, tt * 8:(tt + 1) * 8],
                                              x2ln[:, dc, tt * 128:(tt + 1) * 128],
                                              rout_b[:, dc, :], start=(half4 == 0 and n == 1),
                                              stop=(half4 == 1 and n == 32),
                                              skip_group_check=True)
                        mm.then_inc(sPE.h, 1)
                    on("tensor", _)
                    sPE.n += 1
                rst["logits_pe"] = sPE.n

            def make_chain(tt):
                def chain():
                    def _(v, tt=tt, lp=rst["logits_pe"], base=sDV.n,
                          pe_prev=(pexp_done_tt[tt - 1] if tt else 0)):
                        v.wait_ge(sPE.h, lp)
                        if pe_prev:
                            v.wait_ge(sAC.h, pe_prev)
                        v.reduce_max(lmax[:], PS[6][:, tt * 8:(tt + 1) * 8], axis=AX.X).then_inc(sDV.h, 1)
                        v.wait_ge(sDV.h, base + 1)
                        v.tensor_scalar(lmax[:], lmax[:], -1.0, None, ALU.mult).then_inc(sDV.h, 1)
                    on("vector", _)
                    sDV.n += 2
                    nmax_done = sDV.n

                    def _(sc, tt=tt, nmax_done=nmax_done,
                          t2p=(top2_done[tt - 1] if tt else 0)):
                        sc.wait_ge(sDV.h, nmax_done)
                        if t2p:
                            sc.wait_ge(sDV.h, t2p)
                        sc.activation(f_t[:, 0:8], PS[6][:, tt * 8:(tt + 1) * 8],
                                      AF.Exp, bias=lmax[:]).then_inc(sAC.h, 1)
                    on("scalar", _)
                    sAC.n += 1
                    pexp_done_tt[tt] = sAC.n

                    def _(v, tt=tt, pexp=pexp_done_tt[tt], base=sDV.n):
                        v.wait_ge(sAC.h, pexp)
                        n = base
                        ops = [
                            lambda: v.reduce_sum(lsum[:], f_t[:, 0:8], axis=AX.X),
                            lambda: v.reciprocal(lrec[:], lsum[:]),
                            lambda: v.tensor_scalar(f_t[:, 0:8], f_t[:, 0:8], lrec[:], None, ALU.mult),
                            lambda: v.reduce_max(lmax[:], f_t[:, 0:8], axis=AX.X),
                            lambda: v.tensor_scalar(f_t[:, 8:16], f_t[:, 0:8], lmax[:], None, ALU.is_ge),
                            lambda: v.tensor_scalar(f_t[:, 8:16], f_t[:, 8:16], 2.0, None, ALU.mult),
                            lambda: v.tensor_tensor(f_t[:, 8:16], f_t[:, 0:8], f_t[:, 8:16], ALU.subtract),
                            lambda: v.reduce_max(lmax[:], f_t[:, 8:16], axis=AX.X),
                            lambda: v.tensor_scalar(f_t[:, 8:16], f_t[:, 0:8], lmax[:], None, ALU.is_ge),
                            lambda: v.tensor_tensor(coef_tma[:, tt * 8:(tt + 1) * 8],
                                                    f_t[:, 0:8], f_t[:, 8:16], ALU.mult),
                        ]
                        for op in ops:
                            op().then_inc(sDV.h, 1)
                            n += 1
                            v.wait_ge(sDV.h, n)
                    on("vector", _)
                    sDV.n += 10
                    top2_done[tt] = sDV.n
                return chain

            chain_queue = [make_chain(tt) for tt in range(8)]

            def emit_router_tail():
                while chain_queue:
                    chain_queue.pop(0)()

                def _(t, w=top2_done, pe7=pexp_done_tt):
                    t.wait_ge(sDV.h, max(w))
                    t.wait_ge(sAC.h, max(pe7))
                    for tt in range(8):
                        bank = PS[6] if tt < 4 else PS[7]
                        t.matmul(bank[0:8, (tt % 4) * 128:(tt % 4 + 1) * 128],
                                 coef_tma[:, tt * 8:(tt + 1) * 8], eye_bf,
                                 start=(tt % 4 == 0), stop=(tt % 4 == 3),
                                 skip_group_check=True).then_inc(sPE.h, 1)
                on("tensor", _)
                sPE.n += 8
                ctr_done = sPE.n

                def _(v, ctr_done=ctr_done):
                    v.wait_ge(sPE.h, ctr_done)
                    v.tensor_copy(coef_fm[0:8, 0:512], PS[6][0:8, 0:512])
                    mm = v.tensor_copy(coef_fm[0:8, 512:1024], PS[7][0:8, 0:512])
                    mm.then_inc(sDV.h, 1)
                on("vector", _)
                sDV.n += 1
                rst["coef_done_dv"] = sDV.n
            coef_done_dv = None

            for gi, grp in enumerate(groups):
                _sgg = nc.enter_named_scope("ffn_h%dg%d_gu" % (hf, gi), False)[0]
                grp_crep = [0]

                def emit_group_creps(routed):
                    if not routed:
                        return
                    for wave in range(0, len(routed), 2):
                        pair = routed[wave:wave + 2]

                        def _(t, pair=pair, cd=rst["coef_done_dv"]):
                            t.wait_ge(sDV.h, max(cd, sDV.n))
                            for i, ce in enumerate(pair):
                                for ts2 in range(2):
                                    mm = t.matmul(PS[4 + 2 * i + ts2][:], sel_b[:, ce, :],
                                                  coef_fm[0:32, ts2 * 512:(ts2 + 1) * 512],
                                                  start=True, stop=True)
                            mm.then_inc(sPE.h, 1)
                        on("tensor", _)
                        sPE.n += 1
                        bc_done = sPE.n

                        def _(v, pair=pair, bc_done=bc_done):
                            v.wait_ge(sPE.h, bc_done)
                            for i, ce in enumerate(pair):
                                for ts2 in range(2):
                                    mm = v.tensor_copy(crep[ce % 4][:, ts2 * 512:(ts2 + 1) * 512],
                                                       PS[4 + 2 * i + ts2][:])
                            mm.then_inc(sDV.h, 1)
                        on("vector", _)
                        sDV.n += 1
                    grp_crep[0] = sDV.n

                if gi > 0:
                    emit_group_creps([p for p in grp if p < 8])
                for j, pj in enumerate(grp):
                    if gi == 0 and j == 0:
                        emit_logits()
                    if gi == 0 and j == 2:
                        emit_router_tail()
                        emit_group_creps([p for p in grp if p < 8])
                    pre_router = (gi == 0 and j < 2)
                    wsrc, usrc, dsrc, ce = pe_srcs(pj)
                    slot = j % 2

                    def _(g, slot=slot, j=j, wsrc=wsrc, usrc=usrc, dsrc=dsrc,
                          wgf=wg_free[slot], wuf=wu_free[slot], wdf=wd_free[j],
                          first=first_ffn_load[0]):
                        if first:
                            g.wait_ge(sPE.h, ATTN_PE)
                            g.wait_ge(sDV.h, ATTN_DV)
                        g.wait_ge(sPE.h, wgf)
                        g.dma_start(out=wgb[slot], in_=wsrc).then_inc(s_wg[slot].h, 16)
                        g.wait_ge(sPE.h, wuf)
                        g.dma_start(out=wub[slot], in_=usrc).then_inc(s_wu[slot].h, 16)
                        g.wait_ge(sPE.h, wdf)
                        g.dma_start(out=wdb[j], in_=dsrc).then_inc(s_wd[j].h, 16)
                    on("gpsimd", _)
                    first_ffn_load[0] = False
                    s_wg[slot].n += 16
                    s_wu[slot].n += 16
                    s_wd[j].n += 16
                    wg_ready = s_wg[slot].n
                    wu_ready = s_wu[slot].n

                    cslot = (ce % 4) if ce is not None else None
                    crep_ready = grp_crep[0] if ce is not None else 0

                    for dffc in range(4):
                        par = (dffc % 2) if not pre_router else 0
                        bg = 0 if par == 0 else 4
                        eac, edv = gu_evac[par]

                        def _(t, slot=slot, dffc=dffc, bg=bg, wg_ready=wg_ready,
                              wu_ready=wu_ready, eac=eac, edv=edv, x2ln_apply=x2ln_apply,
                              crep_ready=crep_ready):
                            if dffc == 0:
                                t.wait_ge(s_wg[slot].h, wg_ready)
                                t.wait_ge(s_wu[slot].h, wu_ready)
                                t.wait_ge(sDV.h, x2ln_apply)
                            if dffc == 1 and crep_ready:
                                t.wait_ge(sDV.h, crep_ready)
                            if eac:
                                t.wait_ge(sAC.h, eac)
                            if edv:
                                t.wait_ge(sDV.h, edv)
                            for dc in range(DC):
                                for ts2 in range(2):
                                    t.matmul(PS[bg + ts2][:], wgb[slot][:, dc, dffc * 128:(dffc + 1) * 128],
                                             x2ln[:, dc, ts2 * 512:(ts2 + 1) * 512],
                                             start=(dc == 0), stop=(dc == DC - 1))
                            for dc in range(DC):
                                for ts2 in range(2):
                                    mm = t.matmul(PS[bg + 2 + ts2][:], wub[slot][:, dc, dffc * 128:(dffc + 1) * 128],
                                                  x2ln[:, dc, ts2 * 512:(ts2 + 1) * 512],
                                                  start=(dc == 0), stop=(dc == DC - 1))
                            mm.then_inc(sPE.h, 1)
                        on("tensor", _)
                        sPE.n += 1
                        gu_done = sPE.n

                        def _(sc, j=j, dffc=dffc, bg=bg, gu_done=gu_done):
                            sc.wait_ge(sPE.h, gu_done)
                            for ts2 in range(2):
                                cp = sc.activation(hb[j][:, dffc, ts2 * 512:(ts2 + 1) * 512],
                                                   PS[bg + ts2][:], AF.Silu)
                            cp.then_inc(sAC.h, 1)
                        on("scalar", _)
                        sAC.n += 1
                        silu_done = sAC.n

                        def _(v, j=j, dffc=dffc, bg=bg, silu_done=silu_done,
                              cslot=cslot, crep_ready=crep_ready):
                            v.wait_ge(sAC.h, silu_done)
                            if crep_ready:
                                v.wait_ge(sDV.h, crep_ready)
                            for ts2 in range(2):
                                hsl = hb[j][:, dffc, ts2 * 512:(ts2 + 1) * 512]
                                if cslot is not None:
                                    v.tensor_tensor(uwtmp[:, ts2, :], PS[bg + 2 + ts2][:],
                                                    crep[cslot][:, ts2 * 512:(ts2 + 1) * 512], ALU.mult)
                                else:
                                    v.tensor_copy(uwtmp[:, ts2, :], PS[bg + 2 + ts2][:])
                                mm = v.tensor_tensor(hsl, hsl, uwtmp[:, ts2, :], ALU.mult)
                            mm.then_inc(sDV.h, 1)
                        on("vector", _)
                        sDV.n += 1
                        gu_evac[par] = (silu_done, sDV.n)
                        if pre_router and chain_queue:
                            chain_queue.pop(0)()
                    wg_free[slot] = sPE.n
                    wu_free[slot] = sPE.n

                nc.leave_named_scope("ffn_h%dg%d_gu" % (hf, gi), _sgg, False)
                _sgd = nc.enter_named_scope("ffn_h%dg%d_dn" % (hf, gi), False)[0]
                H_READY_DV = sDV.n
                WD_READY = [s_wd[j].n for j in range(len(grp))]

                for hc in range(DC):
                    par = hc % 2
                    bk = 0 if par == 0 else 2
                    eac, edv = down_evac[par]

                    def _(t, gi=gi, grp=grp, hc=hc, bk=bk, H_READY_DV=H_READY_DV,
                          WD_READY=WD_READY, eac=eac, edv=edv):
                        t.wait_ge(sDV.h, H_READY_DV)
                        if hc == 0:
                            for j in range(len(grp)):
                                t.wait_ge(s_wd[j].h, WD_READY[j])
                        if eac:
                            t.wait_ge(sAC.h, eac)
                        if edv:
                            t.wait_ge(sDV.h, edv)
                        nmm = 4 * len(grp)
                        for ts2 in range(2):
                            n = 0
                            for j in range(len(grp)):
                                for dffc in range(4):
                                    n += 1
                                    mm = t.matmul(PS[bk + ts2][:], wdb[j][:, dffc, hc * 128:(hc + 1) * 128],
                                                  hb[j][:, dffc, ts2 * 512:(ts2 + 1) * 512],
                                                  start=(n == 1), stop=(n == nmm))
                        mm.then_inc(sPE.h, 1)
                    on("tensor", _)
                    sPE.n += 1
                    down_done = sPE.n

                    if gi == 0:
                        def _(sc, hc=hc, bk=bk, down_done=down_done):
                            sc.wait_ge(sPE.h, down_done)
                            for ts2 in range(2):
                                cp = sc.copy(acc[:, hc, ts2 * 512:(ts2 + 1) * 512], PS[bk + ts2][:])
                            cp.then_inc(sAC.h, 1)
                        on("scalar", _)
                        sAC.n += 1
                        down_evac[par] = (sAC.n, down_evac[par][1])
                    else:
                        def _(v, hc=hc, bk=bk, down_done=down_done):
                            v.wait_ge(sPE.h, down_done)
                            for ts2 in range(2):
                                asl = acc[:, hc, ts2 * 512:(ts2 + 1) * 512]
                                mm = v.tensor_tensor(asl, asl, PS[bk + ts2][:], ALU.add)
                            mm.then_inc(sDV.h, 1)
                        on("vector", _)
                        sDV.n += 1
                        down_evac[par] = (down_evac[par][0], sDV.n)
                for j in range(len(grp)):
                    wd_free[j] = sPE.n
                gu_evac[0] = (max(gu_evac[0][0], down_evac[0][0], down_evac[1][0]),
                              max(gu_evac[0][1], down_evac[0][1], down_evac[1][1]))
                nc.leave_named_scope("ffn_h%dg%d_dn" % (hf, gi), _sgd, False)
            DOWN_PE = sPE.n
            DOWN_DV = sDV.n
            DOWN_AC = sAC.n

            if stage == "ffn0":
                def _(g, a=DOWN_PE, b=DOWN_DV, c=DOWN_AC):
                    g.wait_ge(sPE.h, a)
                    g.wait_ge(sDV.h, b)
                    g.wait_ge(sAC.h, c)
                    g.dma_start(out=dbg["x2ln"], in_=x2ln.rearrange("p a b -> p (a b)")).then_inc(s_dbg.h, 16)
                    g.dma_start(out=dbg["coef"], in_=coef_fm).then_inc(s_dbg.h, 16)
                    g.dma_start(out=dbg["acc"], in_=acc.rearrange("p a b -> p (a b)")).then_inc(s_dbg.h, 16)
                    g.dma_start(out=dbg["h0"], in_=hb[0].rearrange("p a b -> p (a b)")).then_inc(s_dbg.h, 16)
                    g.dma_start(out=dbg["crep0"], in_=crep[0]).then_inc(s_dbg.h, 16)
                on("gpsimd", _)
                s_dbg.n += 16 * 5

                def _(sy, w=s_dbg.n):
                    sy.wait_ge(s_dbg.h, w)
                on("sync", _)
                return nc

            _sl4 = nc.enter_named_scope("ffn_h%d_ln4" % hf, False)[0]
            ln4_stats_pe = 0
            for ts in range(2):
                def _(v, ts=ts, DOWN_DV=DOWN_DV, DOWN_PE=DOWN_PE, DOWN_AC=DOWN_AC,
                      ln4_stats_pe=ln4_stats_pe):
                    v.wait_ge(sDV.h, DOWN_DV)
                    v.wait_ge(sAC.h, DOWN_AC)
                    v.wait_ge(sPE.h, DOWN_PE if ts == 0 else ln4_stats_pe)
                    for dc in range(DC):
                        v.tensor_copy(castb[:, dc, :], acc[:, dc, ts * 512:(ts + 1) * 512])
                        mm = v.tensor_tensor(sqb[:, dc, :], castb[:, dc, :], castb[:, dc, :], ALU.mult)
                    mm.then_inc(sDV.h, 1)
                on("vector", _)
                sDV.n += 1
                sq4_done = sDV.n

                def _(t, sq4_done=sq4_done):
                    t.wait_ge(sDV.h, sq4_done)
                    for dc in range(DC):
                        t.matmul(PS[6][:], ones_bf, castb[:, dc, :], start=(dc == 0), stop=(dc == DC - 1))
                    for dc in range(DC):
                        mm = t.matmul(PS[7][:], ones_bf, sqb[:, dc, :], start=(dc == 0), stop=(dc == DC - 1))
                    mm.then_inc(sPE.h, 1)
                on("tensor", _)
                sPE.n += 1
                ln4_stats_pe = sPE.n

                def _(v, done=ln4_stats_pe):
                    v.wait_ge(sPE.h, done)
                    v.tensor_scalar(f_m[:], PS[6][:], 1.0 / D, None, ALU.mult)
                    v.tensor_scalar(f_ms[:], PS[7][:], 1.0 / D, None, ALU.mult)
                    v.tensor_tensor(f_r[:], f_m[:], f_m[:], ALU.mult)
                    v.tensor_tensor(f_ms[:], f_ms[:], f_r[:], ALU.subtract).then_inc(sDV.h, 1)
                on("vector", _)
                sDV.n += 1
                var4 = sDV.n

                def _(sc, var4=var4):
                    sc.wait_ge(sDV.h, var4)
                    sc.activation(f_ms[:], f_ms[:], AF.Ln, bias=eps5[:])
                    sc.activation(f_r[:], f_ms[:], AF.Exp, scale=-0.5).then_inc(sAC.h, 1)
                on("scalar", _)
                sAC.n += 1
                rstd4 = sAC.n

                def _(v, ts=ts, rstd4=rstd4):
                    v.wait_ge(sAC.h, rstd4)
                    for dc in range(DC):
                        asl = acc[:, dc, ts * 512:(ts + 1) * 512]
                        xsl = x2sl[:, dc, ts * 512:(ts + 1) * 512]
                        v.tensor_tensor(f_t2[:], asl, f_m[:], ALU.subtract)
                        v.tensor_tensor(f_t2[:], f_t2[:], f_r[:], ALU.mult)
                        mm = v.tensor_tensor(xsl, xsl, f_t2[:], ALU.add)
                    mm.then_inc(sDV.h, 1)
                on("vector", _)
                sDV.n += 1
            OUT_DONE = sDV.n

            def _(sy, hf=hf, OUT_DONE=OUT_DONE):
                sy.wait_ge(sDV.h, OUT_DONE)
                sy.dma_start(out=y_r[:, :, hf * TH:(hf + 1) * TH], in_=x2sl).then_inc(s_y.h, 16)
            on("sync", _)
            s_y.n += 16
            nc.leave_named_scope("ffn_h%d_ln4" % hf, _sl4, False)

        def _(sy, w=s_y.n):
            sy.wait_ge(s_y.h, w)
        on("sync", _)

    return nc


_NC_CACHE = {}


def _get_nc(stage):
    if stage not in _NC_CACHE:
        _NC_CACHE[stage] = build(stage)
    return _NC_CACHE[stage]


def make_in_maps(L0_attn_x, L0_attn_W_qkv, L0_ffn_router, L0_ffn_experts_gate,
                 L0_ffn_experts_up, L0_ffn_experts_down, L0_ffn_shared_gate,
                 L0_ffn_shared_up, L0_ffn_shared_down):
    x = np.asarray(L0_attn_x, np.float32)
    sel8 = np.zeros((32, 1024), dtype=np.float32)
    for e in range(8):
        sel8[e, e * 128:(e + 1) * 128] = 1.0
    shared = dict(
        wqkvT=np.ascontiguousarray(np.asarray(L0_attn_W_qkv, np.float32).T),
        router=np.ascontiguousarray(np.asarray(L0_ffn_router, np.float32)),
        eg=np.ascontiguousarray(np.asarray(L0_ffn_experts_gate, np.float32)),
        eu=np.ascontiguousarray(np.asarray(L0_ffn_experts_up, np.float32)),
        ed=np.ascontiguousarray(np.asarray(L0_ffn_experts_down, np.float32)),
        sg=np.ascontiguousarray(np.asarray(L0_ffn_shared_gate, np.float32)),
        su=np.ascontiguousarray(np.asarray(L0_ffn_shared_up, np.float32)),
        sd=np.ascontiguousarray(np.asarray(L0_ffn_shared_down, np.float32)),
        eye=np.eye(128, dtype=np.float32),
        sel8=sel8,
    )
    in_maps = []
    for c in range(8):
        b, h = c // 2, c % 2
        xb = x[b]
        if h:
            xb = np.concatenate([xb[2048:], xb[:2048]], axis=0)
        m = dict(shared)
        m["xb"] = np.ascontiguousarray(xb.T)
        in_maps.append(m)
    return in_maps


def kernel(**inputs):
    in_maps = make_in_maps(**inputs)
    nc = _get_nc("full")
    res = run_bass_kernel_spmd(nc, in_maps, core_ids=list(range(8)))
    out = np.empty((4, 4096, 1024), np.float32)
    for c in range(8):
        b, h = c // 2, c % 2
        out[b, h * 2048:(h + 1) * 2048, :] = np.asarray(res.results[c]["y"]).T
    return out


# version 13
# speedup vs baseline: 1.0615x; 1.0087x over previous
import numpy as np
from contextlib import ExitStack

import concourse.bass as bass
import concourse.mybir as mybir
from concourse.bass_utils import run_bass_kernel_spmd

F32 = mybir.dt.float32
BF16 = mybir.dt.bfloat16
AF = mybir.ActivationFunctionType
ALU = mybir.AluOpType
AX = mybir.AxisListType

KB = 1024
D = 1024
DC = 8
TALL = 4096
TOWN = 2048
NQB = 4
NKC = 32
TH = 1024

O_ONES = 0
O_EYE = 256
O_ROUT = 512
O_EPS5 = 640
O_EPS35 = 648
O_WQ = 1 * KB
O_XLN = 49 * KB
O_SPILL = [177 * KB + 512, 181 * KB + 512]
O_KSQ = 113 * KB
O_XS = [121 * KB, 137 * KB]
O_U = 153 * KB
O_ASTATS = 161 * KB
O_KMAX = 177 * KB
O_QSQ = 185 * KB + 512
O_LNQ = 189 * KB + 512
O_MREP = 177 * KB + 512
O_PT = 1 * KB
O_KST = [33 * KB, 41 * KB]
O_QST = [49 * KB, 57 * KB]
O_XOWN = 65 * KB
O_WB = 81 * KB
O_UW = 89 * KB
O_STATS = 97 * KB
O_NEGM = 107 * KB
O_VBF = 113 * KB
O_X2SL = 1 * KB
O_X2LN = 33 * KB
O_ACC = 49 * KB
O_WG = [186 * KB, 81 * KB]
O_WU = [89 * KB, 97 * KB]
O_FM = 105 * KB
O_FMS = 107 * KB
O_FR = 109 * KB
O_FT2 = 111 * KB
O_FT = 113 * KB
O_COEF = 114 * KB
O_SEL = 204 * KB
O_CREP = 118 * KB
O_H = 122 * KB
O_WD = 154 * KB
O_UWT = 194 * KB
ARENA = 206 * KB


class C:

    def __init__(self, stack, nc, name):
        self.h = stack.enter_context(nc.semaphore(name))
        self.n = 0
        self.name = name


def build(stage="full"):
    nc = bass.Bass("TRN2", target_bir_lowering=False)

    xb = nc.dram_tensor("xb", [D, TALL], F32, kind="ExternalInput").ap()
    wqkvT = nc.dram_tensor("wqkvT", [D, 3 * D], F32, kind="ExternalInput").ap()
    router = nc.dram_tensor("router", [D, 8], F32, kind="ExternalInput").ap()
    eg = nc.dram_tensor("eg", [8, D, 512], F32, kind="ExternalInput").ap()
    eu = nc.dram_tensor("eu", [8, D, 512], F32, kind="ExternalInput").ap()
    ed = nc.dram_tensor("ed", [8, 512, D], F32, kind="ExternalInput").ap()
    sg = nc.dram_tensor("sg", [D, D], F32, kind="ExternalInput").ap()
    su = nc.dram_tensor("su", [D, D], F32, kind="ExternalInput").ap()
    sd = nc.dram_tensor("sd", [D, D], F32, kind="ExternalInput").ap()
    eye_in = nc.dram_tensor("eye", [128, 128], F32, kind="ExternalInput").ap()
    sel_in = nc.dram_tensor("sel8", [32, 1024], F32, kind="ExternalInput").ap()

    k_hbm = nc.dram_tensor("k_hbm", [DC, 128, TALL], BF16, kind="Internal").ap()
    q_hbm = nc.dram_tensor("q_hbm", [DC, 128, TOWN], BF16, kind="Internal").ap()
    x2_hbm = nc.dram_tensor("x2_hbm", [DC, 128, TOWN], F32, kind="Internal").ap()
    mr_hbm = nc.dram_tensor("mr_hbm", [128, 4096], F32, kind="Internal").ap()

    y = nc.dram_tensor("y", [D, TOWN], F32, kind="ExternalOutput").ap()
    y_r = y.rearrange("(c p) t -> p c t", p=128)

    dbg = {}
    if stage == "ln1":
        dbg["xln"] = nc.dram_tensor("dbg_xln", [D, TALL], F32, kind="ExternalOutput").ap()
    if stage == "qkv":
        dbg["k"] = nc.dram_tensor("dbg_k", [DC, 128, TALL], F32, kind="ExternalOutput").ap()
        dbg["q"] = nc.dram_tensor("dbg_q", [DC, 128, TOWN], F32, kind="ExternalOutput").ap()
        dbg["v"] = nc.dram_tensor("dbg_v", [TALL, D], F32, kind="ExternalOutput").ap()
        dbg["negm"] = nc.dram_tensor("dbg_negm", [128, TOWN], F32, kind="ExternalOutput").ap()
    if stage == "attn":
        dbg["x2"] = nc.dram_tensor("dbg_x2", [DC, 128, TOWN], F32, kind="ExternalOutput").ap()
    if stage == "ffn0":
        dbg["x2ln"] = nc.dram_tensor("dbg_x2ln", [128, DC * TH], F32, kind="ExternalOutput").ap()
        dbg["coef"] = nc.dram_tensor("dbg_coef", [128, 1024], F32, kind="ExternalOutput").ap()
        dbg["acc"] = nc.dram_tensor("dbg_acc", [128, DC * TH], F32, kind="ExternalOutput").ap()
        dbg["h0"] = nc.dram_tensor("dbg_h0", [128, 4 * TH], F32, kind="ExternalOutput").ap()
        dbg["crep0"] = nc.dram_tensor("dbg_crep0", [128, TH], F32, kind="ExternalOutput").ap()
    if stage.startswith("qb"):
        for nm, shp in [("pt", [128, NKC * 512]), ("wb", [128, DC * 512]),
                        ("eps", [128, 512]), ("sm", [128, 512]), ("sms", [128, 512]),
                        ("sr", [128, 512]), ("xo", [128, DC * 512])]:
            dbg[nm] = nc.dram_tensor("dbg_" + nm, shp, F32, kind="ExternalOutput").ap()

    with ExitStack() as stack:
        arena = stack.enter_context(nc.sbuf_tensor("arena", [128, ARENA], mybir.dt.uint8))

        def carve(off, dtype, shape):
            esz = 2 if dtype == BF16 else 4
            nfree = 1
            for s in shape[1:]:
                nfree *= s
            v = arena[0:shape[0], off:off + nfree * esz].bitcast(dtype)
            if len(shape) == 2:
                return v
            return v.rearrange("p (a b) -> p a b", a=shape[1], b=shape[2])

        ones_bf = carve(O_ONES, BF16, [128, 128])
        eye_bf = carve(O_EYE, BF16, [128, 128])
        rout_b = carve(O_ROUT, BF16, [128, 8, 8])
        eps5 = carve(O_EPS5, F32, [128, 1])
        eps35 = carve(O_EPS35, F32, [128, 1])
        wq = carve(O_WQ, BF16, [128, DC, 3 * D])
        xln = carve(O_XLN, BF16, [128, DC, TALL])
        spill = [carve(O_SPILL[i], BF16, [128, 4, 512]) for i in range(2)]
        spill_f = [carve(O_SPILL[i], BF16, [128, 2048]) for i in range(2)]
        ksq = carve(O_KSQ, BF16, [128, 2048])
        xs = [carve(O_XS[i], F32, [128, DC, 512]) for i in range(2)]
        ubuf = carve(O_U, BF16, [128, DC, 512])
        a_m = carve(O_ASTATS, F32, [128, 512])
        a_ms = carve(O_ASTATS + 2 * KB, F32, [128, 512])
        a_t = carve(O_ASTATS + 4 * KB, F32, [128, 512])
        a_r = carve(O_ASTATS + 6 * KB, F32, [128, 512])
        a_rb = carve(O_ASTATS + 8 * KB, BF16, [128, 512])
        a_bb = carve(O_ASTATS + 9 * KB, BF16, [128, 512])
        kmax = carve(O_KMAX, F32, [128, 1])
        klog = carve(O_KMAX + 128, F32, [128, 1])
        qsq = carve(O_QSQ, BF16, [128, 2048])
        lnq = carve(O_LNQ, F32, [128, 2048])
        mrep = carve(O_MREP, F32, [128, 2048])
        negm = carve(O_NEGM, BF16, [128, 2048])
        mr3 = [carve(177 * KB + 512 + 4 * KB * i, F32, [128, 2, 512]) for i in range(4)]
        mr3all = carve(177 * KB + 512, F32, [128, 4096])
        mrh = [carve(200 * KB + 2 * KB * i, BF16, [128, 2, 512]) for i in range(2)]
        pt = carve(O_PT, BF16, [128, NKC, 512])
        kst = [carve(O_KST[i], BF16, [128, DC, 512]) for i in range(2)]
        qst = [carve(O_QST[i], BF16, [128, DC, 512]) for i in range(2)]
        xown = carve(O_XOWN, F32, [128, DC, 512])
        w_b = carve(O_WB, BF16, [128, DC, 512])
        uw = carve(O_UW, BF16, [128, DC, 512])
        s_m = carve(O_STATS, F32, [128, 512])
        s_ms = carve(O_STATS + 2 * KB, F32, [128, 512])
        s_eps = carve(O_STATS + 4 * KB, F32, [128, 512])
        s_r = carve(O_STATS + 6 * KB, F32, [128, 512])
        s_mb = carve(O_STATS + 8 * KB, BF16, [128, 512])
        s_rb = carve(O_STATS + 9 * KB, BF16, [128, 512])
        vbf = carve(O_VBF, BF16, [128, NKC, 1024])
        x2sl = carve(O_X2SL, F32, [128, DC, TH])
        x2ln = carve(O_X2LN, BF16, [128, DC, TH])
        acc = carve(O_ACC, F32, [128, DC, TH])
        wgb = [carve(O_WG[i], BF16, [128, DC, 512]) for i in range(2)]
        wub = [carve(O_WU[i], BF16, [128, DC, 512]) for i in range(2)]
        f_m = carve(O_FM, F32, [128, 512])
        f_ms = carve(O_FMS, F32, [128, 512])
        f_r = carve(O_FR, F32, [128, 512])
        f_t2 = carve(O_FT2, F32, [128, 512])
        f_t = carve(O_FT, F32, [128, 16])
        coef_tma = carve(O_FT + 64, BF16, [128, 64])
        lmax = carve(O_FT + 192, F32, [128, 1])
        lsum = carve(O_FT + 224, F32, [128, 1])
        lrec = carve(O_FT + 256, F32, [128, 1])
        coef_fm = carve(O_COEF, BF16, [128, 1024])
        sel_b = carve(O_SEL, BF16, [32, 8, 128])
        crep = [carve(O_CREP + 2 * KB * i, BF16, [128, 1024]) for i in range(2)]
        crep += [carve(196 * KB + 2 * KB * i, BF16, [128, 1024]) for i in range(2)]
        hb = [carve(O_H + 8 * KB * i, BF16, [128, 4, 1024]) for i in range(4)]
        wdb = [carve(O_WD + 8 * KB * i, BF16, [128, 4, 1024]) for i in range(4)]
        sqb = carve(O_H, BF16, [128, DC, 512])
        castb = carve(O_H + 8 * KB, BF16, [128, DC, 512])
        uwtmp = carve(O_UWT, BF16, [128, 2, 512])

        PS = [stack.enter_context(nc.psum_tensor("ps%d" % i, [128, 512], F32)) for i in range(8)]

        sPE = C(stack, nc, "sPE")
        sDV = C(stack, nc, "sDV")
        sAC = C(stack, nc, "sAC")
        sPL = C(stack, nc, "sPL")
        s_wdma = C(stack, nc, "s_wdma")
        s_xs = [C(stack, nc, "s_xs0"), C(stack, nc, "s_xs1")]
        s_ksp = C(stack, nc, "s_ksp")
        s_qsp = C(stack, nc, "s_qsp")
        s_kst = [C(stack, nc, "s_kst0"), C(stack, nc, "s_kst1")]
        s_qst = [C(stack, nc, "s_qst0"), C(stack, nc, "s_qst1")]
        s_xown = C(stack, nc, "s_xown")
        s_x2w = C(stack, nc, "s_x2w")
        s_x2r = C(stack, nc, "s_x2r")
        s_wg = [C(stack, nc, "s_wg0"), C(stack, nc, "s_wg1")]
        s_wu = [C(stack, nc, "s_wu0"), C(stack, nc, "s_wu1")]
        s_wd = [C(stack, nc, "s_wd%d" % i) for i in range(4)]
        s_y = C(stack, nc, "s_y")
        s_mr = C(stack, nc, "s_mr")
        s_dbg = C(stack, nc, "s_dbg")

        block = stack.enter_context(nc.Block())

        def on(name, f):
            getattr(block, name)(f)

        def _(g):
            g.memset(ones_bf, 1.0)
            g.memset(eps5, 1e-5)
            g.memset(eps35, 1e-35)
            g.engine_nop().then_inc(sPL.h, 1)
            g.dma_start(out=eye_bf, in_=eye_in).then_inc(s_wdma.h, 16)
            g.dma_start(out=rout_b, in_=router.rearrange("(c p) e -> p c e", p=128)).then_inc(s_wdma.h, 16)
            g.dma_start(out=sel_b, in_=sel_in.rearrange("p (a b) -> p a b", a=8, b=128)).then_inc(s_wdma.h, 16)
            for c in range(DC):
                g.dma_start(out=wq[:, c, :], in_=wqkvT[c * 128:(c + 1) * 128, :]).then_inc(s_wdma.h, 16)
        on("gpsimd", _)
        sPL.n += 1
        s_wdma.n += 16 * (3 + DC)
        WQ_READY = s_wdma.n
        ONES_READY = sPL.n

        _sid = nc.enter_named_scope("phA", False)[0]
        xb_r = xb.rearrange("(c p) t -> p c t", p=128)
        apply_done = [0] * 8
        prev_stats = [0]

        for s in range(8):
            p = s % 2

            def _(sy, s=s, p=p):
                if s >= 2:
                    sy.wait_ge(sDV.h, apply_done[s - 2])
                sy.dma_start(out=xs[p][:, 0:4, :],
                             in_=xb_r[:, 0:4, s * 512:(s + 1) * 512]).then_inc(s_xs[p].h, 16)
                sy.dma_start(out=xs[p][:, 4:8, :],
                             in_=xb_r[:, 4:8, s * 512:(s + 1) * 512]).then_inc(s_xs[p].h, 16)
            on("sync", _)
            s_xs[p].n += 32
            load_half = [s_xs[p].n - 16, s_xs[p].n]

            cast_half = [0, 0]
            for hh in range(2):
                def _(v, s=s, p=p, hh=hh, w=load_half[hh]):
                    v.wait_ge(s_xs[p].h, w)
                    if s == 0 and hh == 0:
                        v.wait_ge(sPL.h, ONES_READY)
                    v.tensor_copy(xln[:, 4 * hh:4 * hh + 4, s * 512:(s + 1) * 512],
                                  xs[p][:, 4 * hh:4 * hh + 4, :]).then_inc(sDV.h, 1)
                on("vector", _)
                sDV.n += 1
                cast_half[hh] = sDV.n
            cast_done = sDV.n

            sq_half = [0, 0]
            for hh in range(2):
                def _(sc, s=s, hh=hh, w=cast_half[hh], pstats=prev_stats[0]):
                    sc.wait_ge(sDV.h, w)
                    if hh == 0 and pstats:
                        sc.wait_ge(sPE.h, pstats)
                    sc.square(ubuf[:, 4 * hh:4 * hh + 4, :],
                              xln[:, 4 * hh:4 * hh + 4, s * 512:(s + 1) * 512]).then_inc(sAC.h, 1)
                on("scalar", _)
                sAC.n += 1
                sq_half[hh] = sAC.n
            sq_done = sAC.n

            def _(t, s=s, cast_half=tuple(cast_half), sq_half=tuple(sq_half)):
                xls = xln[:, :, s * 512:(s + 1) * 512]
                t.wait_ge(sDV.h, cast_half[0])
                for dc in range(4):
                    t.matmul(PS[0][:], ones_bf, xls[:, dc, :], start=(dc == 0), stop=False)
                t.wait_ge(sDV.h, cast_half[1])
                for dc in range(4, DC):
                    t.matmul(PS[0][:], ones_bf, xls[:, dc, :], start=False, stop=(dc == DC - 1))
                t.wait_ge(sAC.h, sq_half[0])
                for dc in range(4):
                    t.matmul(PS[1][:], ones_bf, ubuf[:, dc, :], start=(dc == 0), stop=False)
                t.wait_ge(sAC.h, sq_half[1])
                for dc in range(4, DC):
                    mm = t.matmul(PS[1][:], ones_bf, ubuf[:, dc, :], start=False, stop=(dc == DC - 1))
                mm.then_inc(sPE.h, 1)
            on("tensor", _)
            sPE.n += 1
            stats_done = sPE.n
            prev_stats[0] = stats_done

            def _(v, stats_done=stats_done):
                v.wait_ge(sPE.h, stats_done)
                v.tensor_scalar(a_m[:], PS[0][:], 1.0 / D, None, ALU.mult)
                v.tensor_scalar(a_ms[:], PS[1][:], 1.0 / D, None, ALU.mult)
                v.tensor_tensor(a_t[:], a_m[:], a_m[:], ALU.mult)
                v.tensor_tensor(a_ms[:], a_ms[:], a_t[:], ALU.subtract).then_inc(sDV.h, 1)
            on("vector", _)
            sDV.n += 1
            var_done = sDV.n

            def _(sc, var_done=var_done):
                sc.wait_ge(sDV.h, var_done)
                if var_done <= 3:
                    sc.wait_ge(sPL.h, 1)
                sc.activation(a_t[:], a_ms[:], AF.Ln, bias=eps5[:])
                sc.activation(a_r[:], a_t[:], AF.Exp, scale=-0.5).then_inc(sAC.h, 1)
            on("scalar", _)
            sAC.n += 1
            rstd_done = sAC.n

            def _(v, s=s, rstd_done=rstd_done):
                v.wait_ge(sAC.h, rstd_done)
                v.tensor_copy(a_rb[:], a_r[:])
                v.tensor_tensor(a_t[:], a_m[:], a_r[:], ALU.mult)
                v.tensor_scalar(a_bb[:], a_t[:], -1.0, None, ALU.mult)
                xls = xln[:, :, s * 512:(s + 1) * 512]
                for dc in range(DC):
                    v.tensor_tensor(xls[:, dc, :], xls[:, dc, :], a_rb[:], ALU.mult)
                    mm = v.tensor_tensor(xls[:, dc, :], xls[:, dc, :], a_bb[:], ALU.add)
                mm.then_inc(sDV.h, 1)
            on("vector", _)
            sDV.n += 1
            apply_done[s] = sDV.n

        nc.leave_named_scope("phA", _sid, False)
        LN1_DONE = apply_done[7]

        if stage == "ln1":
            def _(g):
                g.wait_ge(sDV.h, LN1_DONE)
                g.dma_start(out=dbg["xln"].rearrange("(c p) t -> p c t", p=128),
                            in_=xln[:]).then_inc(s_dbg.h, 16)
            on("gpsimd", _)
            s_dbg.n += 16

            def _(sy):
                sy.wait_ge(s_dbg.h, s_dbg.n)
            on("sync", _)
            return nc

        _sid = nc.enter_named_scope("phK", False)[0]
        kss_free_dv = 0
        kq_evac_pair = [0, 0]
        pending_kss = [None]
        for kg in range(2):
            for fc in range(DC):
                b = fc % 2
                for half in range(2):
                    def _(t, kg=kg, fc=fc, half=half, ge=kq_evac_pair[half]):
                        if kg == 0 and fc == 0 and half == 0:
                            t.wait_ge(s_wdma.h, WQ_READY)
                            t.wait_ge(sDV.h, LN1_DONE)
                        if ge:
                            t.wait_ge(sAC.h, ge)
                        for dc in range(DC):
                            for k2 in range(2):
                                ks = half * 2 + k2
                                mm = t.matmul(PS[ks][:], wq[:, dc, D + fc * 128:D + (fc + 1) * 128],
                                              xln[:, dc, kg * 2048 + ks * 512:kg * 2048 + (ks + 1) * 512],
                                              start=(dc == 0), stop=(dc == DC - 1))
                        mm.then_inc(sPE.h, 1)
                    on("tensor", _)
                    sPE.n += 1
                    prod_done = sPE.n
                    if half == 0 and pending_kss[0] is not None:
                        pending_kss[0]()
                        pending_kss[0] = None

                    def _(sc, b=b, half=half, prod_done=prod_done,
                          ws=(s_ksp.n if b == 0 else s_qsp.n)):
                        sc.wait_ge(sPE.h, prod_done)
                        if half == 0:
                            sc.wait_ge((s_ksp if b == 0 else s_qsp).h, ws)
                        for k2 in range(2):
                            ks = half * 2 + k2
                            cp = sc.copy(spill[b][:, ks, :], PS[ks][:])
                        cp.then_inc(sAC.h, 1)
                    on("scalar", _)
                    sAC.n += 1
                    kq_evac_pair[half] = sAC.n
                evac_done = sAC.n

                def _(sy, kg=kg, fc=fc, b=b, evac_done=evac_done):
                    sy.wait_ge(sAC.h, evac_done)
                    sy.wait_ge(sDV.h, sDV.n)
                    sy.dma_start(out=k_hbm[fc, :, kg * 2048:(kg + 1) * 2048],
                                 in_=spill_f[b]).then_inc((s_ksp if b == 0 else s_qsp).h, 16)
                on("sync", _)
                if b == 0:
                    s_ksp.n += 16
                else:
                    s_qsp.n += 16

                def _(v, b=b, evac_done=evac_done):
                    v.wait_ge(sAC.h, evac_done)
                    v.wait_ge(sPE.h, sPE.n)
                    v.tensor_tensor(ksq[:], spill_f[b], spill_f[b], ALU.mult).then_inc(sDV.h, 1)
                on("vector", _)
                sDV.n += 1
                ksq_done = sDV.n

                def emit_kss(fc=fc, ksq_done=ksq_done, kss_free_dv=kss_free_dv):
                    def _(t):
                        t.wait_ge(sDV.h, ksq_done)
                        if fc == 0:
                            t.wait_ge(sDV.h, kss_free_dv)
                        for ks in range(4):
                            mm = t.matmul(PS[4 + ks][:], ones_bf, ksq[:, ks * 512:(ks + 1) * 512],
                                          start=(fc == 0), stop=(fc == DC - 1))
                        mm.then_inc(sPE.h, 1)
                    on("tensor", _)
                    sPE.n += 1
                pending_kss[0] = emit_kss

            pending_kss[0]()
            pending_kss[0] = None
            kss_mm_done = sPE.n

            def _(v, kg=kg, kss_mm_done=kss_mm_done, base=sDV.n):
                v.wait_ge(sPE.h, kss_mm_done)
                n = base
                for ks in range(4):
                    v.reduce_max(a_m[:, 0:1], PS[4 + ks][:], axis=AX.X).then_inc(sDV.h, 1)
                    n += 1
                    v.wait_ge(sDV.h, n)
                    if kg == 0 and ks == 0:
                        op = v.tensor_copy(kmax[:], a_m[:, 0:1])
                    else:
                        op = v.tensor_tensor(kmax[:], kmax[:], a_m[:, 0:1], ALU.max)
                    op.then_inc(sDV.h, 1)
                    n += 1
                    v.wait_ge(sDV.h, n)
            on("vector", _)
            sDV.n += 8
            kss_free_dv = sDV.n

        nc.leave_named_scope("phK", _sid, False)
        _sid = nc.enter_named_scope("phV", False)[0]
        v_evac = [0, 0]
        for tt in range(NKC):
            bk = (tt % 2) * 2
            def _(t, tt=tt, bk=bk, ge=v_evac[tt % 2]):
                if ge:
                    t.wait_ge(sAC.h, ge)
                if tt < 2:
                    t.wait_ge(sAC.h, sAC.n)
                    t.wait_ge(sDV.h, sDV.n)
                for dc in range(DC):
                    for fs in range(2):
                        mm = t.matmul(PS[bk + fs][:], xln[:, dc, tt * 128:(tt + 1) * 128],
                                      wq[:, dc, 2 * D + fs * 512:2 * D + (fs + 1) * 512],
                                      start=(dc == 0), stop=(dc == DC - 1))
                mm.then_inc(sPE.h, 1)
            on("tensor", _)
            sPE.n += 1
            vprod = sPE.n

            def _(sc, tt=tt, bk=bk, vprod=vprod, wk=s_ksp.n):
                sc.wait_ge(sPE.h, vprod)
                if tt < 2:
                    sc.wait_ge(s_ksp.h, wk)
                for fs in range(2):
                    cp = sc.copy(vbf[:, tt, fs * 512:(fs + 1) * 512], PS[bk + fs][:])
                cp.then_inc(sAC.h, 1)
            on("scalar", _)
            sAC.n += 1
            v_evac[tt % 2] = sAC.n

        nc.leave_named_scope("phV", _sid, False)
        V_DONE_AC = sAC.n

        _sid = nc.enter_named_scope("phQ", False)[0]
        for fc in range(DC):
            b = fc % 2
            for half in range(2):
                def _(t, fc=fc, half=half, ge=kq_evac_pair[half]):
                    if fc == 0 and half == 0:
                        t.wait_ge(sAC.h, sAC.n)
                    if ge:
                        t.wait_ge(sAC.h, ge)
                    if half == 0 and fc > 0:
                        t.wait_ge(sDV.h, sDV.n)
                    for dc in range(DC):
                        for k2 in range(2):
                            qs = half * 2 + k2
                            mm = t.matmul(PS[qs][:], wq[:, dc, fc * 128:(fc + 1) * 128],
                                          xln[:, dc, qs * 512:(qs + 1) * 512],
                                          start=(dc == 0), stop=(dc == DC - 1))
                    mm.then_inc(sPE.h, 1)
                on("tensor", _)
                sPE.n += 1
                prod_done = sPE.n
                if half == 0 and pending_kss[0] is not None:
                    pending_kss[0]()
                    pending_kss[0] = None

                def _(sc, b=b, half=half, prod_done=prod_done,
                      ws=(s_ksp.n if b == 0 else s_qsp.n)):
                    sc.wait_ge(sPE.h, prod_done)
                    if half == 0:
                        sc.wait_ge((s_ksp if b == 0 else s_qsp).h, ws)
                    for k2 in range(2):
                        qs = half * 2 + k2
                        cp = sc.activation(spill[b][:, qs, :], PS[qs][:], AF.Copy, scale=0.03125)
                    cp.then_inc(sAC.h, 1)
                on("scalar", _)
                sAC.n += 1
                kq_evac_pair[half] = sAC.n
            evac_done = sAC.n

            def _(sy, fc=fc, b=b, evac_done=evac_done):
                sy.wait_ge(sAC.h, evac_done)
                sy.wait_ge(sDV.h, sDV.n)
                sy.dma_start(out=q_hbm[fc, :, :], in_=spill_f[b]).then_inc((s_ksp if b == 0 else s_qsp).h, 16)
            on("sync", _)
            if b == 0:
                s_ksp.n += 16
            else:
                s_qsp.n += 16

            def _(v, b=b, evac_done=evac_done):
                v.wait_ge(sAC.h, evac_done)
                v.wait_ge(sPE.h, sPE.n)
                v.tensor_tensor(qsq[:], spill_f[b], spill_f[b], ALU.mult).then_inc(sDV.h, 1)
            on("vector", _)
            sDV.n += 1
            qsq_done = sDV.n

            def emit_qss(fc=fc, qsq_done=qsq_done):
                def _(t):
                    t.wait_ge(sDV.h, qsq_done)
                    for qs in range(4):
                        mm = t.matmul(PS[4 + qs][:], ones_bf, qsq[:, qs * 512:(qs + 1) * 512],
                                      start=(fc == 0), stop=(fc == DC - 1))
                    mm.then_inc(sPE.h, 1)
                on("tensor", _)
                sPE.n += 1
            pending_kss[0] = emit_qss

        pending_kss[0]()
        pending_kss[0] = None
        Q_DONE_PE = sPE.n

        nc.leave_named_scope("phQ", _sid, False)
        _sid = nc.enter_named_scope("phNegm", False)[0]
        def _(sc, w0=s_ksp.n, w1=s_qsp.n):
            sc.wait_ge(sPE.h, Q_DONE_PE)
            sc.wait_ge(s_ksp.h, w0)
            sc.wait_ge(s_qsp.h, w1)
            for qs in range(4):
                sc.activation(lnq[:, qs * 512:(qs + 1) * 512], PS[4 + qs][:], AF.Ln)
            sc.activation(klog[:], kmax[:], AF.Ln).then_inc(sAC.h, 1)
        on("scalar", _)
        sAC.n += 1
        lnq_done = sAC.n

        def _(v, lnq_done=lnq_done):
            v.wait_ge(sAC.h, lnq_done)
            v.tensor_scalar(klog[:], klog[:], 0.5, None, ALU.mult)
            v.engine_nop().then_inc(sDV.h, 1)
        on("vector", _)
        sDV.n += 1
        klog_done = sDV.n

        def _(sc, klog_done=klog_done):
            sc.wait_ge(sDV.h, klog_done)
            sc.activation(mrep[:], lnq[:], AF.Exp, scale=0.5, bias=klog[:]).then_inc(sAC.h, 1)
        on("scalar", _)
        sAC.n += 1
        m_done = sAC.n

        def _(v, m_done=m_done):
            v.wait_ge(sAC.h, m_done)
            v.tensor_scalar(negm[:], mrep[:], -1.0, 19.0, ALU.mult, ALU.add)
            v.engine_nop().then_inc(sDV.h, 1)
        on("vector", _)
        sDV.n += 1
        NEGM_DONE = sDV.n

        nc.leave_named_scope("phNegm", _sid, False)
        if stage == "qkv":
            def _(g, wk=s_ksp.n, wqq=s_qsp.n):
                g.wait_ge(sDV.h, NEGM_DONE)
                g.wait_ge(s_ksp.h, wk)
                g.wait_ge(s_qsp.h, wqq)
                g.dma_start(out=dbg["k"], in_=k_hbm).then_inc(s_dbg.h, 16)
                g.dma_start(out=dbg["q"], in_=q_hbm).then_inc(s_dbg.h, 16)
                g.dma_start(out=dbg["v"].rearrange("(a p) f -> p a f", p=128), in_=vbf[:]).then_inc(s_dbg.h, 16)
                g.dma_start(out=dbg["negm"], in_=negm[:]).then_inc(s_dbg.h, 16)
            on("gpsimd", _)
            s_dbg.n += 64

            def _(sy):
                sy.wait_ge(s_dbg.h, s_dbg.n)
            on("sync", _)
            return nc

        K_SPILLED = s_ksp.n
        Q_SPILLED = s_qsp.n
        kst_pe_free = [0, 0]
        qst_pe_free = [0, 0]
        pt_pe_free = 0
        wb_dv_free = 0
        x2b_cast_prev = [0]
        bank_evac = [0, 0, 0]
        ST_BANKS = [0, 1, 7]
        pv_bank_evac = [0, 0]

        for qb in range(NQB):
            _sqb = nc.enter_named_scope("qb%d" % qb, False)[0]
            def _(sy, qb=qb, w=qst_pe_free[qb % 2]):
                if qb == 0:
                    sy.wait_ge(sDV.h, NEGM_DONE)
                    sy.wait_ge(s_ksp.h, K_SPILLED)
                    sy.wait_ge(s_qsp.h, Q_SPILLED)
                sy.wait_ge(sPE.h, w)
                sy.dma_start(out=qst[qb % 2],
                             in_=q_hbm[:, :, qb * 512:(qb + 1) * 512].rearrange("c p t -> p c t")
                             ).then_inc(s_qst[qb % 2].h, 16)
            on("sync", _)
            s_qst[qb % 2].n += 16
            qst_ready = s_qst[qb % 2].n

            for kh in range(8):
                def _(sy, kh=kh, w=kst_pe_free[kh % 2]):
                    sy.wait_ge(sPE.h, w)
                    sy.dma_start(out=kst[kh % 2],
                                 in_=k_hbm[:, :, kh * 512:(kh + 1) * 512].rearrange("c p t -> p c t")
                                 ).then_inc(s_kst[kh % 2].h, 16)
                on("sync", _)
                s_kst[kh % 2].n += 16
                kr = s_kst[kh % 2].n

                for kk in range(4):
                    kc = kh * 4 + kk
                    bank_guard = bank_evac[kc % 3]

                    def _(t, qb=qb, kh=kh, kk=kk, kc=kc, qst_ready=qst_ready, kr=kr,
                          bank_guard=bank_guard, pt_guard=pt_pe_free):
                        if kk == 0:
                            t.wait_ge(s_kst[kh % 2].h, kr)
                            if kh == 0:
                                t.wait_ge(s_qst[qb % 2].h, qst_ready)
                        if bank_guard:
                            t.wait_ge(sAC.h, bank_guard)
                        bk = PS[ST_BANKS[kc % 3]]
                        for dc in range(DC):
                            t.matmul(bk[:], kst[kh % 2][:, dc, kk * 128:(kk + 1) * 128],
                                     qst[qb % 2][:, dc, :], start=(dc == 0), stop=False)
                        mm = t.matmul(bk[:], ones_bf[0:1, :],
                                      negm[0:1, qb * 512:(qb + 1) * 512], start=False, stop=True)
                        mm.then_inc(sPE.h, 1)
                    on("tensor", _)
                    sPE.n += 1
                    st_done = sPE.n

                    def _(sc, kc=kc, st_done=st_done, pt_guard=pt_pe_free):
                        sc.wait_ge(sPE.h, st_done)
                        if kc == 0 and pt_guard:
                            sc.wait_ge(sPE.h, pt_guard)
                        sc.activation(pt[:, kc, :], PS[ST_BANKS[kc % 3]][:], AF.Exp).then_inc(sAC.h, 1)
                    on("scalar", _)
                    sAC.n += 1
                    bank_evac[kc % 3] = sAC.n
                kst_pe_free[kh % 2] = sPE.n
            qst_pe_free[qb % 2] = sPE.n
            EXP_DONE = sAC.n

            def _(t, EXP_DONE=EXP_DONE):
                t.wait_ge(sAC.h, EXP_DONE)
                t.wait_ge(sDV.h, sDV.n)
                for kc in range(NKC):
                    mm = t.matmul(PS[2][:], ones_bf, pt[:, kc, :], start=(kc == 0), stop=(kc == NKC - 1))
                mm.then_inc(sPE.h, 1)
            on("tensor", _)
            sPE.n += 1
            sum_done = sPE.n

            def _(v, sum_done=sum_done, wb_dv_free=wb_dv_free):
                v.wait_ge(sPE.h, sum_done)
                v.wait_ge(sDV.h, wb_dv_free)
                v.tensor_scalar(s_r[:], PS[2][:], 1.0, None, ALU.mult).then_inc(sDV.h, 1)
            on("vector", _)
            sDV.n += 1
            sum_ev = sDV.n

            def _(v, sum_ev=sum_ev):
                v.wait_ge(sDV.h, sum_ev)
                v.reciprocal(s_eps[:], s_r[:])
                v.engine_nop().then_inc(sDV.h, 1)
            on("vector", _)
            sDV.n += 1
            eps_done = sDV.n

            def _(sy, qb=qb, w=s_x2w.n, wc=x2b_cast_prev[0]):
                sy.wait_ge(s_x2w.h, w)
                if wc:
                    sy.wait_ge(sDV.h, wc)
                sy.dma_start(out=xown, in_=xb_r[:, :, qb * 512:(qb + 1) * 512]).then_inc(s_xown.h, 16)
            on("sync", _)
            s_xown.n += 16
            xown_ready = s_xown.n

            sq_done_fs = [0] * DC
            for fs in range(DC):
                def _(t, fs=fs, eps_done=eps_done, bg=pv_bank_evac[fs % 2]):
                    if fs == 0:
                        t.wait_ge(sDV.h, eps_done)
                        t.wait_ge(sAC.h, V_DONE_AC)
                    if bg:
                        t.wait_ge(sDV.h, bg)
                    for kc in range(NKC):
                        mm = t.matmul(PS[3 + fs % 2][:], vbf[:, kc, fs * 128:(fs + 1) * 128],
                                      pt[:, kc, :], start=(kc == 0), stop=(kc == NKC - 1))
                    mm.then_inc(sPE.h, 1)
                on("tensor", _)
                sPE.n += 1
                pv_done = sPE.n

                def _(v, fs=fs, pv_done=pv_done):
                    v.wait_ge(sPE.h, pv_done)
                    v.tensor_tensor(w_b[:, fs, :], PS[3 + fs % 2][:], s_eps[:], ALU.mult)
                    v.tensor_tensor(uw[:, fs, :], w_b[:, fs, :], w_b[:, fs, :], ALU.mult).then_inc(sDV.h, 1)
                on("vector", _)
                sDV.n += 1
                pv_bank_evac[fs % 2] = sDV.n
                sq_done = sDV.n

                sq_done_fs[fs] = sq_done
                if fs > 0:
                    def _(t, fs=fs, sq_done=sq_done_fs[fs - 1]):
                        t.wait_ge(sDV.h, sq_done)
                        t.matmul(PS[5][:], ones_bf, w_b[:, fs - 1, :], start=(fs - 1 == 0), stop=False)
                        mm = t.matmul(PS[6][:], ones_bf, uw[:, fs - 1, :], start=(fs - 1 == 0), stop=False)
                        mm.then_inc(sPE.h, 1)
                    on("tensor", _)
                    sPE.n += 1

            def _(t, sq_done=sq_done_fs[DC - 1]):
                t.wait_ge(sDV.h, sq_done)
                t.matmul(PS[5][:], ones_bf, w_b[:, DC - 1, :], start=False, stop=True)
                mm = t.matmul(PS[6][:], ones_bf, uw[:, DC - 1, :], start=False, stop=True)
                mm.then_inc(sPE.h, 1)
            on("tensor", _)
            sPE.n += 1
            pt_pe_free = sPE.n
            ln2_mm_done = sPE.n

            def _(v, ln2_mm_done=ln2_mm_done):
                v.wait_ge(sPE.h, ln2_mm_done)
                v.tensor_scalar(s_m[:], PS[5][:], 1.0 / D, None, ALU.mult)
                v.tensor_scalar(s_ms[:], PS[6][:], 1.0 / D, None, ALU.mult)
                v.tensor_tensor(s_r[:], s_m[:], s_m[:], ALU.mult)
                v.tensor_tensor(s_ms[:], s_ms[:], s_r[:], ALU.subtract)
                v.engine_nop().then_inc(sDV.h, 1)
            on("vector", _)
            sDV.n += 1
            var2_done = sDV.n

            def _(sc, var2_done=var2_done):
                sc.wait_ge(sDV.h, var2_done)
                sc.activation(s_ms[:], s_ms[:], AF.Ln, bias=eps5[:])
                sc.activation(s_r[:], s_ms[:], AF.Exp, scale=-0.5).then_inc(sAC.h, 1)
            on("scalar", _)
            sAC.n += 1
            rstd2_done = sAC.n

            def _(v, rstd2_done=rstd2_done, xown_ready=xown_ready):
                v.wait_ge(sAC.h, rstd2_done)
                v.wait_ge(s_xown.h, xown_ready)
                v.tensor_copy(s_mb[:], s_m[:])
                v.tensor_copy(s_rb[:], s_r[:])
                for fs in range(DC):
                    v.tensor_tensor(uw[:, fs, :], w_b[:, fs, :], s_mb[:], ALU.subtract)
                    v.tensor_tensor(uw[:, fs, :], uw[:, fs, :], s_rb[:], ALU.mult)
                    mm = v.tensor_tensor(xown[:, fs, :], xown[:, fs, :], uw[:, fs, :], ALU.add)
                mm.then_inc(sDV.h, 1)
            on("vector", _)
            sDV.n += 1
            wb_dv_free = sDV.n
            x2_done = sDV.n

            cast_h = [0, 0]
            for hh in range(2):
                def _(v, hh=hh, x2_done=x2_done):
                    if hh == 0:
                        v.wait_ge(sDV.h, x2_done)
                    v.tensor_copy(uw[:, 4 * hh:4 * hh + 4, :],
                                  xown[:, 4 * hh:4 * hh + 4, :]).then_inc(sDV.h, 1)
                on("vector", _)
                sDV.n += 1
                cast_h[hh] = sDV.n
            x2b_done = sDV.n
            x2b_cast_prev[0] = x2b_done

            sq_h = [0, 0]
            for hh in range(2):
                def _(sc, hh=hh, w=cast_h[hh]):
                    sc.wait_ge(sDV.h, w)
                    sc.square(w_b[:, 4 * hh:4 * hh + 4, :],
                              uw[:, 4 * hh:4 * hh + 4, :]).then_inc(sAC.h, 1)
                on("scalar", _)
                sAC.n += 1
                sq_h[hh] = sAC.n
            sq3q_done = sAC.n

            def _(t, cast_h=tuple(cast_h), sq_h=tuple(sq_h)):
                t.wait_ge(sDV.h, cast_h[0])
                for dc in range(4):
                    t.matmul(PS[5][:], ones_bf, uw[:, dc, :], start=(dc == 0), stop=False)
                t.wait_ge(sDV.h, cast_h[1])
                for dc in range(4, DC):
                    t.matmul(PS[5][:], ones_bf, uw[:, dc, :], start=False, stop=(dc == DC - 1))
                t.wait_ge(sAC.h, sq_h[0])
                for dc in range(4):
                    t.matmul(PS[6][:], ones_bf, w_b[:, dc, :], start=(dc == 0), stop=False)
                t.wait_ge(sAC.h, sq_h[1])
                for dc in range(4, DC):
                    mm = t.matmul(PS[6][:], ones_bf, w_b[:, dc, :], start=False, stop=(dc == DC - 1))
                mm.then_inc(sPE.h, 1)
            on("tensor", _)
            sPE.n += 1
            st3q_done = sPE.n

            def _(v, qb=qb, st3q_done=st3q_done):
                v.wait_ge(sPE.h, st3q_done)
                v.tensor_scalar(mr3[qb][:, 0, :], PS[5][:], 1.0 / D, None, ALU.mult)
                v.tensor_scalar(s_ms[:], PS[6][:], 1.0 / D, None, ALU.mult)
                v.tensor_tensor(s_r[:], mr3[qb][:, 0, :], mr3[qb][:, 0, :], ALU.mult)
                v.tensor_tensor(s_ms[:], s_ms[:], s_r[:], ALU.subtract).then_inc(sDV.h, 1)
            on("vector", _)
            sDV.n += 1
            var3q = sDV.n

            def _(sc, qb=qb, var3q=var3q):
                sc.wait_ge(sDV.h, var3q)
                sc.activation(s_ms[:], s_ms[:], AF.Ln, bias=eps5[:])
                sc.activation(mr3[qb][:, 1, :], s_ms[:], AF.Exp, scale=-0.5).then_inc(sAC.h, 1)
            on("scalar", _)
            sAC.n += 1
            wb_dv_free = sDV.n
            MR3_AC = sAC.n

            if qb in (1, 3):
                def _(g, qb=qb, w=MR3_AC):
                    g.wait_ge(sAC.h, w)
                    g.dma_start(out=mr_hbm[:, (qb - 1) * 1024:(qb + 1) * 1024],
                                in_=mr3all[:, (qb - 1) * 1024:(qb + 1) * 1024]).then_inc(s_mr.h, 16)
                on("gpsimd", _)
                s_mr.n += 16

            def _(g, qb=qb, x2_done=x2_done):
                g.wait_ge(sDV.h, x2_done)
                g.dma_start(out=x2_hbm[:, :, qb * 512:(qb + 1) * 512].rearrange("c p t -> p c t"),
                             in_=xown[:]).then_inc(s_x2w.h, 16)
            on("gpsimd", _)
            s_x2w.n += 16
            nc.leave_named_scope("qb%d" % qb, _sqb, False)

            if stage.startswith("qb") and qb == int(stage[2:]):
                def _(g, w=s_x2w.n):
                    g.wait_ge(s_x2w.h, w)
                    g.dma_start(out=dbg["pt"], in_=pt.rearrange("p a b -> p (a b)")).then_inc(s_dbg.h, 16)
                    g.dma_start(out=dbg["wb"], in_=w_b.rearrange("p a b -> p (a b)")).then_inc(s_dbg.h, 16)
                    g.dma_start(out=dbg["eps"], in_=s_eps).then_inc(s_dbg.h, 16)
                    g.dma_start(out=dbg["sm"], in_=s_m).then_inc(s_dbg.h, 16)
                    g.dma_start(out=dbg["sms"], in_=s_ms).then_inc(s_dbg.h, 16)
                    g.dma_start(out=dbg["sr"], in_=s_r).then_inc(s_dbg.h, 16)
                    g.dma_start(out=dbg["xo"], in_=xown.rearrange("p a b -> p (a b)")).then_inc(s_dbg.h, 16)
                on("gpsimd", _)
                s_dbg.n += 16 * 7

                def _(sy, w=s_dbg.n):
                    sy.wait_ge(s_dbg.h, w)
                on("sync", _)
                return nc


        PT_FREE_QB3 = pt_pe_free
        ATTN_PE = sPE.n
        ATTN_DV = sDV.n
        ATTN_AC3 = sAC.n
        X2_STORED = s_x2w.n

        if stage == "attn":
            def _(g):
                g.wait_ge(s_x2w.h, X2_STORED)
                g.dma_start(out=dbg["x2"], in_=x2_hbm).then_inc(s_dbg.h, 16)
            on("gpsimd", _)
            s_dbg.n += 16

            def _(sy):
                sy.wait_ge(s_dbg.h, s_dbg.n)
            on("sync", _)
            return nc

        def pe_srcs(j):
            if j < 8:
                return (eg[j].rearrange("(c p) d -> p c d", p=128),
                        eu[j].rearrange("(c p) d -> p c d", p=128),
                        ed[j].rearrange("(c p) hh -> p c hh", p=128), j)
            jj = j - 8
            return (sg[:, jj * 512:(jj + 1) * 512].rearrange("(c p) d -> p c d", p=128),
                    su[:, jj * 512:(jj + 1) * 512].rearrange("(c p) d -> p c d", p=128),
                    sd[jj * 512:(jj + 1) * 512, :].rearrange("(c p) hh -> p c hh", p=128), None)

        groups = [[8, 9, 0, 1], [2, 3, 4, 5], [6, 7]]

        wg_free = [0, 0]
        wu_free = [0, 0]
        wd_free = [0] * 4
        crep_free = [0, 0]
        gu_evac = [(0, 0), (0, 0)]
        down_evac = [(0, 0), (0, 0)]

        first_ffn_load = [True]
        for hf in range(2):
            _shf = nc.enter_named_scope("ffn_pre%d" % hf, False)[0]
            x2_ready_ts = [0, 0]
            for ts in range(2):
                def _(sy, hf=hf, ts=ts, wy=s_y.n):
                    if hf == 0:
                        if ts == 0:
                            sy.wait_ge(s_x2w.h, 32)
                            sy.wait_ge(sPE.h, PT_FREE_QB3)
                    else:
                        if ts == 0:
                            sy.wait_ge(s_x2w.h, X2_STORED)
                            sy.wait_ge(s_y.h, wy)
                            sy.wait_ge(sDV.h, sDV.n)
                    sy.dma_start(out=x2sl[:, :, ts * 512:(ts + 1) * 512],
                                 in_=x2_hbm[:, :, hf * TH + ts * 512:hf * TH + (ts + 1) * 512]
                                 .rearrange("c p t -> p c t")).then_inc(s_x2r.h, 16)
                on("sync", _)
                s_x2r.n += 16
                x2_ready_ts[ts] = s_x2r.n
            x2_ready = s_x2r.n

            cast3_ts = [0, 0]
            for ts in range(2):
                def _(v, ts=ts, w=x2_ready_ts[ts], hf=hf):
                    v.wait_ge(s_x2r.h, w)
                    if hf == 0 and ts == 0:
                        v.memset(coef_fm[0:32, :], 0.0)
                    v.tensor_copy(x2ln[:, :, ts * 512:(ts + 1) * 512],
                                  x2sl[:, :, ts * 512:(ts + 1) * 512]).then_inc(sDV.h, 1)
                on("vector", _)
                sDV.n += 1
                cast3_ts[ts] = sDV.n
            cast3_done = sDV.n

            def _(g, hf=hf, w=(16 if hf == 0 else s_mr.n), wu=sDV.n):
                g.wait_ge(s_mr.h, w)
                g.wait_ge(sDV.h, wu)
                for ts in range(2):
                    g.dma_start(out=mrh[ts],
                                in_=mr_hbm[:, (hf * 2 + ts) * 1024:(hf * 2 + ts + 1) * 1024]
                                .rearrange("p (a b) -> p a b", a=2, b=512)).then_inc(s_mr.h, 16)
            on("gpsimd", _)
            s_mr.n += 32
            mr_ready = s_mr.n

            apply_ts = [0, 0]
            for ts in range(2):
                def _(v, ts=ts, cast3_done=cast3_ts[ts], mr_ready=mr_ready):
                    v.wait_ge(sDV.h, cast3_done)
                    v.wait_ge(s_mr.h, mr_ready)
                    for dc in range(DC):
                        sl = x2ln[:, dc, ts * 512:(ts + 1) * 512]
                        v.tensor_tensor(sl, sl, mrh[ts][:, 0, :], ALU.subtract)
                        mm = v.tensor_tensor(sl, sl, mrh[ts][:, 1, :], ALU.mult)
                    mm.then_inc(sDV.h, 1)
                on("vector", _)
                sDV.n += 1
                apply_ts[ts] = sDV.n
            x2ln_apply = sDV.n

            rst = {"coef_done_dv": 0, "router_ac": 0, "logits_pe": 0}
            top2_done = [0] * 8
            pexp_done_tt = [0] * 8

            def emit_logits(apply_ts=apply_ts):
                for half4 in range(2):
                    def _(t, half4=half4, w=apply_ts[half4]):
                        t.wait_ge(sDV.h, w)
                        n = 0
                        for tt in range(half4 * 4, half4 * 4 + 4):
                            for dc in range(DC):
                                n += 1
                                mm = t.matmul(PS[6][:, tt * 8:(tt + 1) * 8],
                                              x2ln[:, dc, tt * 128:(tt + 1) * 128],
                                              rout_b[:, dc, :], start=(half4 == 0 and n == 1),
                                              stop=(half4 == 1 and n == 32),
                                              skip_group_check=True)
                        mm.then_inc(sPE.h, 1)
                    on("tensor", _)
                    sPE.n += 1
                rst["logits_pe"] = sPE.n

            def make_chain(tt):
                def chain():
                    def _(v, tt=tt, lp=rst["logits_pe"], base=sDV.n,
                          pe_prev=(pexp_done_tt[tt - 1] if tt else 0)):
                        v.wait_ge(sPE.h, lp)
                        if pe_prev:
                            v.wait_ge(sAC.h, pe_prev)
                        v.reduce_max(lmax[:], PS[6][:, tt * 8:(tt + 1) * 8], axis=AX.X).then_inc(sDV.h, 1)
                        v.wait_ge(sDV.h, base + 1)
                        v.tensor_scalar(lmax[:], lmax[:], -1.0, None, ALU.mult).then_inc(sDV.h, 1)
                    on("vector", _)
                    sDV.n += 2
                    nmax_done = sDV.n

                    def _(sc, tt=tt, nmax_done=nmax_done,
                          t2p=(top2_done[tt - 1] if tt else 0)):
                        sc.wait_ge(sDV.h, nmax_done)
                        if t2p:
                            sc.wait_ge(sDV.h, t2p)
                        sc.activation(f_t[:, 0:8], PS[6][:, tt * 8:(tt + 1) * 8],
                                      AF.Exp, bias=lmax[:]).then_inc(sAC.h, 1)
                    on("scalar", _)
                    sAC.n += 1
                    pexp_done_tt[tt] = sAC.n

                    def _(v, tt=tt, pexp=pexp_done_tt[tt], base=sDV.n):
                        v.wait_ge(sAC.h, pexp)
                        n = base
                        ops = [
                            lambda: v.reduce_sum(lsum[:], f_t[:, 0:8], axis=AX.X),
                            lambda: v.reciprocal(lrec[:], lsum[:]),
                            lambda: v.tensor_scalar(f_t[:, 0:8], f_t[:, 0:8], lrec[:], None, ALU.mult),
                            lambda: v.reduce_max(lmax[:], f_t[:, 0:8], axis=AX.X),
                            lambda: v.tensor_scalar(f_t[:, 8:16], f_t[:, 0:8], lmax[:], None, ALU.is_ge),
                            lambda: v.tensor_scalar(f_t[:, 8:16], f_t[:, 8:16], 2.0, None, ALU.mult),
                            lambda: v.tensor_tensor(f_t[:, 8:16], f_t[:, 0:8], f_t[:, 8:16], ALU.subtract),
                            lambda: v.reduce_max(lmax[:], f_t[:, 8:16], axis=AX.X),
                            lambda: v.tensor_scalar(f_t[:, 8:16], f_t[:, 0:8], lmax[:], None, ALU.is_ge),
                            lambda: v.tensor_tensor(coef_tma[:, tt * 8:(tt + 1) * 8],
                                                    f_t[:, 0:8], f_t[:, 8:16], ALU.mult),
                        ]
                        for op in ops:
                            op().then_inc(sDV.h, 1)
                            n += 1
                            v.wait_ge(sDV.h, n)
                    on("vector", _)
                    sDV.n += 10
                    top2_done[tt] = sDV.n
                return chain

            chain_queue = [make_chain(tt) for tt in range(8)]

            def emit_router_tail():
                while chain_queue:
                    chain_queue.pop(0)()

                def _(t, w=top2_done, pe7=pexp_done_tt):
                    t.wait_ge(sDV.h, max(w))
                    t.wait_ge(sAC.h, max(pe7))
                    for tt in range(8):
                        bank = PS[6] if tt < 4 else PS[7]
                        t.matmul(bank[0:8, (tt % 4) * 128:(tt % 4 + 1) * 128],
                                 coef_tma[:, tt * 8:(tt + 1) * 8], eye_bf,
                                 start=(tt % 4 == 0), stop=(tt % 4 == 3),
                                 skip_group_check=True).then_inc(sPE.h, 1)
                on("tensor", _)
                sPE.n += 8
                ctr_done = sPE.n

                def _(v, ctr_done=ctr_done):
                    v.wait_ge(sPE.h, ctr_done)
                    v.tensor_copy(coef_fm[0:8, 0:512], PS[6][0:8, 0:512])
                    mm = v.tensor_copy(coef_fm[0:8, 512:1024], PS[7][0:8, 0:512])
                    mm.then_inc(sDV.h, 1)
                on("vector", _)
                sDV.n += 1
                rst["coef_done_dv"] = sDV.n
            coef_done_dv = None

            for gi, grp in enumerate(groups):
                _sgg = nc.enter_named_scope("ffn_h%dg%d_gu" % (hf, gi), False)[0]
                grp_crep = [0]

                def emit_group_creps(routed):
                    if not routed:
                        return
                    for wave in range(0, len(routed), 2):
                        pair = routed[wave:wave + 2]

                        def _(t, pair=pair, cd=rst["coef_done_dv"]):
                            t.wait_ge(sDV.h, max(cd, sDV.n))
                            for i, ce in enumerate(pair):
                                for ts2 in range(2):
                                    mm = t.matmul(PS[4 + 2 * i + ts2][:], sel_b[:, ce, :],
                                                  coef_fm[0:32, ts2 * 512:(ts2 + 1) * 512],
                                                  start=True, stop=True)
                            mm.then_inc(sPE.h, 1)
                        on("tensor", _)
                        sPE.n += 1
                        bc_done = sPE.n

                        def _(v, pair=pair, bc_done=bc_done):
                            v.wait_ge(sPE.h, bc_done)
                            for i, ce in enumerate(pair):
                                for ts2 in range(2):
                                    mm = v.tensor_copy(crep[ce % 4][:, ts2 * 512:(ts2 + 1) * 512],
                                                       PS[4 + 2 * i + ts2][:])
                            mm.then_inc(sDV.h, 1)
                        on("vector", _)
                        sDV.n += 1
                    grp_crep[0] = sDV.n

                if gi > 0:
                    emit_group_creps([p for p in grp if p < 8])
                for j, pj in enumerate(grp):
                    if gi == 0 and j == 0:
                        emit_logits()
                    if gi == 0 and j == 2:
                        emit_router_tail()
                        emit_group_creps([p for p in grp if p < 8])
                    pre_router = (gi == 0 and j < 2)
                    wsrc, usrc, dsrc, ce = pe_srcs(pj)
                    slot = j % 2

                    def _(g, slot=slot, j=j, wsrc=wsrc, usrc=usrc, dsrc=dsrc,
                          wgf=wg_free[slot], wuf=wu_free[slot], wdf=wd_free[j],
                          first=first_ffn_load[0]):
                        if first:
                            g.wait_ge(sPE.h, ATTN_PE)
                            g.wait_ge(sDV.h, ATTN_DV)
                        g.wait_ge(sPE.h, wgf)
                        g.dma_start(out=wgb[slot], in_=wsrc).then_inc(s_wg[slot].h, 16)
                        g.wait_ge(sPE.h, wuf)
                        g.dma_start(out=wub[slot], in_=usrc).then_inc(s_wu[slot].h, 16)
                        g.wait_ge(sPE.h, wdf)
                        g.dma_start(out=wdb[j], in_=dsrc).then_inc(s_wd[j].h, 16)
                    on("gpsimd", _)
                    first_ffn_load[0] = False
                    s_wg[slot].n += 16
                    s_wu[slot].n += 16
                    s_wd[j].n += 16
                    wg_ready = s_wg[slot].n
                    wu_ready = s_wu[slot].n

                    cslot = (ce % 4) if ce is not None else None
                    crep_ready = grp_crep[0] if ce is not None else 0

                    for dffc in range(4):
                        par = (dffc % 2) if not pre_router else 0
                        bg = 0 if par == 0 else 4
                        eac, edv = gu_evac[par]

                        def _(t, slot=slot, dffc=dffc, bg=bg, wg_ready=wg_ready,
                              wu_ready=wu_ready, eac=eac, edv=edv, x2ln_apply=x2ln_apply,
                              crep_ready=crep_ready):
                            if dffc == 0:
                                t.wait_ge(s_wg[slot].h, wg_ready)
                                t.wait_ge(s_wu[slot].h, wu_ready)
                                t.wait_ge(sDV.h, x2ln_apply)
                            if dffc == 1 and crep_ready:
                                t.wait_ge(sDV.h, crep_ready)
                            if eac:
                                t.wait_ge(sAC.h, eac)
                            if edv:
                                t.wait_ge(sDV.h, edv)
                            for dc in range(DC):
                                for ts2 in range(2):
                                    t.matmul(PS[bg + ts2][:], wgb[slot][:, dc, dffc * 128:(dffc + 1) * 128],
                                             x2ln[:, dc, ts2 * 512:(ts2 + 1) * 512],
                                             start=(dc == 0), stop=(dc == DC - 1))
                            for dc in range(DC):
                                for ts2 in range(2):
                                    mm = t.matmul(PS[bg + 2 + ts2][:], wub[slot][:, dc, dffc * 128:(dffc + 1) * 128],
                                                  x2ln[:, dc, ts2 * 512:(ts2 + 1) * 512],
                                                  start=(dc == 0), stop=(dc == DC - 1))
                            mm.then_inc(sPE.h, 1)
                        on("tensor", _)
                        sPE.n += 1
                        gu_done = sPE.n

                        def _(sc, j=j, dffc=dffc, bg=bg, gu_done=gu_done):
                            sc.wait_ge(sPE.h, gu_done)
                            for ts2 in range(2):
                                cp = sc.activation(hb[j][:, dffc, ts2 * 512:(ts2 + 1) * 512],
                                                   PS[bg + ts2][:], AF.Silu)
                            cp.then_inc(sAC.h, 1)
                        on("scalar", _)
                        sAC.n += 1
                        silu_done = sAC.n

                        def _(v, j=j, dffc=dffc, bg=bg, silu_done=silu_done,
                              cslot=cslot, crep_ready=crep_ready):
                            v.wait_ge(sAC.h, silu_done)
                            if crep_ready:
                                v.wait_ge(sDV.h, crep_ready)
                            for ts2 in range(2):
                                hsl = hb[j][:, dffc, ts2 * 512:(ts2 + 1) * 512]
                                if cslot is not None:
                                    v.tensor_tensor(uwtmp[:, ts2, :], PS[bg + 2 + ts2][:],
                                                    crep[cslot][:, ts2 * 512:(ts2 + 1) * 512], ALU.mult)
                                else:
                                    v.tensor_copy(uwtmp[:, ts2, :], PS[bg + 2 + ts2][:])
                                mm = v.tensor_tensor(hsl, hsl, uwtmp[:, ts2, :], ALU.mult)
                            mm.then_inc(sDV.h, 1)
                        on("vector", _)
                        sDV.n += 1
                        gu_evac[par] = (silu_done, sDV.n)
                        if pre_router and chain_queue:
                            chain_queue.pop(0)()
                    wg_free[slot] = sPE.n
                    wu_free[slot] = sPE.n

                nc.leave_named_scope("ffn_h%dg%d_gu" % (hf, gi), _sgg, False)
                _sgd = nc.enter_named_scope("ffn_h%dg%d_dn" % (hf, gi), False)[0]
                H_READY_DV = sDV.n
                WD_READY = [s_wd[j].n for j in range(len(grp))]

                for hc in range(DC):
                    par = hc % 2
                    bk = 0 if par == 0 else 2
                    eac, edv = down_evac[par]

                    def _(t, gi=gi, grp=grp, hc=hc, bk=bk, H_READY_DV=H_READY_DV,
                          WD_READY=WD_READY, eac=eac, edv=edv):
                        t.wait_ge(sDV.h, H_READY_DV)
                        if hc == 0:
                            for j in range(len(grp)):
                                t.wait_ge(s_wd[j].h, WD_READY[j])
                        if eac:
                            t.wait_ge(sAC.h, eac)
                        if edv:
                            t.wait_ge(sDV.h, edv)
                        nmm = 4 * len(grp)
                        for ts2 in range(2):
                            n = 0
                            for j in range(len(grp)):
                                for dffc in range(4):
                                    n += 1
                                    mm = t.matmul(PS[bk + ts2][:], wdb[j][:, dffc, hc * 128:(hc + 1) * 128],
                                                  hb[j][:, dffc, ts2 * 512:(ts2 + 1) * 512],
                                                  start=(n == 1), stop=(n == nmm))
                        mm.then_inc(sPE.h, 1)
                    on("tensor", _)
                    sPE.n += 1
                    down_done = sPE.n

                    if gi == 0:
                        def _(sc, hc=hc, bk=bk, down_done=down_done):
                            sc.wait_ge(sPE.h, down_done)
                            for ts2 in range(2):
                                cp = sc.copy(acc[:, hc, ts2 * 512:(ts2 + 1) * 512], PS[bk + ts2][:])
                            cp.then_inc(sAC.h, 1)
                        on("scalar", _)
                        sAC.n += 1
                        down_evac[par] = (sAC.n, down_evac[par][1])
                    else:
                        def _(v, hc=hc, bk=bk, down_done=down_done):
                            v.wait_ge(sPE.h, down_done)
                            for ts2 in range(2):
                                asl = acc[:, hc, ts2 * 512:(ts2 + 1) * 512]
                                mm = v.tensor_tensor(asl, asl, PS[bk + ts2][:], ALU.add)
                            mm.then_inc(sDV.h, 1)
                        on("vector", _)
                        sDV.n += 1
                        down_evac[par] = (down_evac[par][0], sDV.n)
                for j in range(len(grp)):
                    wd_free[j] = sPE.n
                gu_evac[0] = (max(gu_evac[0][0], down_evac[0][0], down_evac[1][0]),
                              max(gu_evac[0][1], down_evac[0][1], down_evac[1][1]))
                nc.leave_named_scope("ffn_h%dg%d_dn" % (hf, gi), _sgd, False)
            DOWN_PE = sPE.n
            DOWN_DV = sDV.n
            DOWN_AC = sAC.n

            if stage == "ffn0":
                def _(g, a=DOWN_PE, b=DOWN_DV, c=DOWN_AC):
                    g.wait_ge(sPE.h, a)
                    g.wait_ge(sDV.h, b)
                    g.wait_ge(sAC.h, c)
                    g.dma_start(out=dbg["x2ln"], in_=x2ln.rearrange("p a b -> p (a b)")).then_inc(s_dbg.h, 16)
                    g.dma_start(out=dbg["coef"], in_=coef_fm).then_inc(s_dbg.h, 16)
                    g.dma_start(out=dbg["acc"], in_=acc.rearrange("p a b -> p (a b)")).then_inc(s_dbg.h, 16)
                    g.dma_start(out=dbg["h0"], in_=hb[0].rearrange("p a b -> p (a b)")).then_inc(s_dbg.h, 16)
                    g.dma_start(out=dbg["crep0"], in_=crep[0]).then_inc(s_dbg.h, 16)
                on("gpsimd", _)
                s_dbg.n += 16 * 5

                def _(sy, w=s_dbg.n):
                    sy.wait_ge(s_dbg.h, w)
                on("sync", _)
                return nc

            _sl4 = nc.enter_named_scope("ffn_h%d_ln4" % hf, False)[0]
            ln4_stats_pe = 0
            for ts in range(2):
                def _(v, ts=ts, DOWN_DV=DOWN_DV, DOWN_PE=DOWN_PE, DOWN_AC=DOWN_AC,
                      ln4_stats_pe=ln4_stats_pe):
                    v.wait_ge(sDV.h, DOWN_DV)
                    v.wait_ge(sAC.h, DOWN_AC)
                    v.wait_ge(sPE.h, DOWN_PE if ts == 0 else ln4_stats_pe)
                    for dc in range(DC):
                        v.tensor_copy(castb[:, dc, :], acc[:, dc, ts * 512:(ts + 1) * 512])
                        mm = v.tensor_tensor(sqb[:, dc, :], castb[:, dc, :], castb[:, dc, :], ALU.mult)
                    mm.then_inc(sDV.h, 1)
                on("vector", _)
                sDV.n += 1
                sq4_done = sDV.n

                def _(t, sq4_done=sq4_done):
                    t.wait_ge(sDV.h, sq4_done)
                    for dc in range(DC):
                        t.matmul(PS[6][:], ones_bf, castb[:, dc, :], start=(dc == 0), stop=(dc == DC - 1))
                    for dc in range(DC):
                        mm = t.matmul(PS[7][:], ones_bf, sqb[:, dc, :], start=(dc == 0), stop=(dc == DC - 1))
                    mm.then_inc(sPE.h, 1)
                on("tensor", _)
                sPE.n += 1
                ln4_stats_pe = sPE.n

                def _(v, done=ln4_stats_pe):
                    v.wait_ge(sPE.h, done)
                    v.tensor_scalar(f_m[:], PS[6][:], 1.0 / D, None, ALU.mult)
                    v.tensor_scalar(f_ms[:], PS[7][:], 1.0 / D, None, ALU.mult)
                    v.tensor_tensor(f_r[:], f_m[:], f_m[:], ALU.mult)
                    v.tensor_tensor(f_ms[:], f_ms[:], f_r[:], ALU.subtract).then_inc(sDV.h, 1)
                on("vector", _)
                sDV.n += 1
                var4 = sDV.n

                def _(sc, var4=var4):
                    sc.wait_ge(sDV.h, var4)
                    sc.activation(f_ms[:], f_ms[:], AF.Ln, bias=eps5[:])
                    sc.activation(f_r[:], f_ms[:], AF.Exp, scale=-0.5).then_inc(sAC.h, 1)
                on("scalar", _)
                sAC.n += 1
                rstd4 = sAC.n

                def _(v, ts=ts, rstd4=rstd4):
                    v.wait_ge(sAC.h, rstd4)
                    for dc in range(DC):
                        asl = acc[:, dc, ts * 512:(ts + 1) * 512]
                        xsl = x2sl[:, dc, ts * 512:(ts + 1) * 512]
                        v.tensor_tensor(f_t2[:], asl, f_m[:], ALU.subtract)
                        v.tensor_tensor(f_t2[:], f_t2[:], f_r[:], ALU.mult)
                        mm = v.tensor_tensor(xsl, xsl, f_t2[:], ALU.add)
                    mm.then_inc(sDV.h, 1)
                on("vector", _)
                sDV.n += 1
            OUT_DONE = sDV.n

            def _(sy, hf=hf, OUT_DONE=OUT_DONE):
                sy.wait_ge(sDV.h, OUT_DONE)
                sy.dma_start(out=y_r[:, :, hf * TH:(hf + 1) * TH], in_=x2sl).then_inc(s_y.h, 16)
            on("sync", _)
            s_y.n += 16
            nc.leave_named_scope("ffn_h%d_ln4" % hf, _sl4, False)

        def _(sy, w=s_y.n):
            sy.wait_ge(s_y.h, w)
        on("sync", _)

    return nc


_NC_CACHE = {}


def _get_nc(stage):
    if stage not in _NC_CACHE:
        _NC_CACHE[stage] = build(stage)
    return _NC_CACHE[stage]


def make_in_maps(L0_attn_x, L0_attn_W_qkv, L0_ffn_router, L0_ffn_experts_gate,
                 L0_ffn_experts_up, L0_ffn_experts_down, L0_ffn_shared_gate,
                 L0_ffn_shared_up, L0_ffn_shared_down):
    x = np.asarray(L0_attn_x, np.float32)
    sel8 = np.zeros((32, 1024), dtype=np.float32)
    for e in range(8):
        sel8[e, e * 128:(e + 1) * 128] = 1.0
    shared = dict(
        wqkvT=np.ascontiguousarray(np.asarray(L0_attn_W_qkv, np.float32).T),
        router=np.ascontiguousarray(np.asarray(L0_ffn_router, np.float32)),
        eg=np.ascontiguousarray(np.asarray(L0_ffn_experts_gate, np.float32)),
        eu=np.ascontiguousarray(np.asarray(L0_ffn_experts_up, np.float32)),
        ed=np.ascontiguousarray(np.asarray(L0_ffn_experts_down, np.float32)),
        sg=np.ascontiguousarray(np.asarray(L0_ffn_shared_gate, np.float32)),
        su=np.ascontiguousarray(np.asarray(L0_ffn_shared_up, np.float32)),
        sd=np.ascontiguousarray(np.asarray(L0_ffn_shared_down, np.float32)),
        eye=np.eye(128, dtype=np.float32),
        sel8=sel8,
    )
    in_maps = []
    for c in range(8):
        b, h = c // 2, c % 2
        xb = x[b]
        if h:
            xb = np.concatenate([xb[2048:], xb[:2048]], axis=0)
        m = dict(shared)
        m["xb"] = np.ascontiguousarray(xb.T)
        in_maps.append(m)
    return in_maps


def kernel(**inputs):
    in_maps = make_in_maps(**inputs)
    nc = _get_nc("full")
    res = run_bass_kernel_spmd(nc, in_maps, core_ids=list(range(8)))
    out = np.empty((4, 4096, 1024), np.float32)
    for c in range(8):
        b, h = c // 2, c % 2
        out[b, h * 2048:(h + 1) * 2048, :] = np.asarray(res.results[c]["y"]).T
    return out
